# version 04
# baseline (speedup 1.0000x reference)
import math
import numpy as np
import ml_dtypes
import concourse.bass as bass
import concourse.mybir as mybir
from concourse.bass_utils import run_bass_kernel_spmd
from contextlib import ExitStack

F32 = mybir.dt.float32
BF16 = mybir.dt.bfloat16
I32 = mybir.dt.int32
U8 = mybir.dt.uint8
AF = mybir.ActivationFunctionType
ALU = mybir.AluOpType
AX = mybir.AxisListType

B, T, D = 2, 4096, 4096
NQH, NKV, HD = 32, 8, 128
DC = 32
NSLOT = 16
OWN = 2048
QSH = 2048
Q0T, Q1T = 611, 4095
NROUT = Q1T - Q0T
EPS = 1e-5
ISQ = 1.0 / math.sqrt(HD)
MAGIC = 1234567
LN_RB = -math.log(500000.0) / 64.0
TWO_PI = 2.0 * math.pi
PI = math.pi
KB = 1024

ENGS = ("sync", "tensor", "scalar", "vector", "gpsimd")

_DEBUG = []
_LAST_DEBUG = None
_PHASES = (1, 2, 3, 4)
_ATT1 = False
_P4N = 99


class Prog:
    def __init__(self, nc, ctx):
        self.nc, self.ctx = nc, ctx
        self.th = {e: [] for e in ENGS}
        self.sems, self.ctr = {}, {}

    def sem(self, name):
        if name not in self.sems:
            self.sems[name] = self.ctx.enter_context(self.nc.semaphore(name))
            self.ctr[name] = 0
        return self.sems[name]

    def inc(self, name, k=1):
        self.sem(name)
        self.ctr[name] += k
        return self.ctr[name]

    def cur(self, name):
        self.sem(name)
        return self.ctr[name]

    def barrier_gates(self):
        return [("te", self.cur("te")), ("ve", self.cur("ve")),
                ("se", self.cur("se"))]

    def step(self, eng, sem_name, fn, waits=()):
        ws = [(self.sem(n), v) for n, v in waits]
        tgt = self.inc(sem_name, 1)
        s = self.sem(sem_name)

        def f(e, fn=fn, ws=ws, s=s):
            for sem, v in ws:
                e.wait_ge(sem, v)
            fn(e).then_inc(s, 1)
        self.emit(eng, f)
        return tgt

    def emit(self, eng, fn):
        self.th[eng].append(fn)

    def dma(self, eng, sem_name, out, in_, waits=(), **kw):
        ws = [(self.sem(n), v) for n, v in waits]
        tgt = self.inc(sem_name, 16)
        s = self.sem(sem_name)

        def f(e, out=out, in_=in_, s=s, ws=ws, kw=kw):
            for sem, v in ws:
                e.wait_ge(sem, v)
            e.dma_start(out=out, in_=in_, **kw).then_inc(s, 16)
        self.emit(eng, f)
        return tgt

    def dma_ifsel(self, eng, sem_name, mk_oi, nvar, waits=(), idword=0):
        ws = [(self.sem(n), v) for n, v in waits]
        tgt = self.inc(sem_name, 16)
        s = self.sem(sem_name)
        A = self.A
        pairs = [mk_oi(v) for v in range(nvar)]

        def f(e, pairs=pairs, ws=ws, s=s):
            for sem, v in ws:
                e.wait_ge(sem, v)
            with e.register() as r:
                e.reg_load(r, A.ids_s[0:1, idword:idword + 1])
                for v, (out, in_) in enumerate(pairs):
                    with e.If_cmp(r, v, "IS_EQ"):
                        e.dma_start(out=out, in_=in_).then_inc(s, 16)
        self.emit(eng, f)
        return tgt

    def fence(self, waits, name="sy"):
        ws = [(self.sem(n), v) for n, v in waits]
        tgt = self.inc(name, 1)
        s = self.sem(name)

        def f(e, ws=ws, s=s):
            for sem, v in ws:
                e.wait_ge(sem, v)
            e.nop().then_inc(s, 1)
        self.emit("sync", f)
        return tgt


def _bk3(w):
    Dd, Oo = w.shape
    return np.ascontiguousarray(
        w.reshape(Dd // 128, 128, Oo // 128, 128).transpose(2, 1, 0, 3)
    ).reshape(Oo // 128, 128, Dd)


def build():
    nc = bass.Bass("TRN2", target_bir_lowering=False)
    ctx = ExitStack()
    p = Prog(nc, ctx)
    A = type("A", (), {})()
    A.nc, A.p = nc, p
    p.A = A

    din = {}
    for name, shape, dt in [
        ("xT", [D, OWN], F32),
        ("wq_blk", [DC, 128, DC * 128], F32),
        ("wo_blk", [DC, 128, DC * 128], F32),
        ("wt_blk", [DC, 128, DC * 128], F32),
        ("wk_blk", [NKV, 128, DC * 128], F32),
        ("wv_blk", [NKV, 128, DC * 128], F32),
        ("at_blk", [128, DC * 128], F32),
        ("bmat", [128, D], F32),
        ("rwT_blk", [128, DC * 8], F32),
        ("biasw", [128, DC], F32),
        ("rbias", [8, 1], F32),
        ("id128", [128, 128], F32),
        ("maskA", [128, 128], F32),
        ("maskB", [128, 128], F32),
        ("e16", [8, 128], F32),
        ("iot", [1, 64], F32),
        ("posr", [1, OWN], F32),
        ("posq", [1, QSH], F32),
        ("ids", [1, 8], I32),
        ("nonce", [1, 1], I32),
    ]:
        din[name] = nc.dram_tensor(name, shape, dt, kind="ExternalInput").ap()
    A.d = din

    A.out = nc.dram_tensor("out", [D, 1024], F32, kind="ExternalOutput").ap()
    Sh = dict(kind="Internal", addr_space="Shared")
    A.sh_x1 = nc.dram_tensor("sh_x1", [D, T], BF16, **Sh).ap()
    A.sh_k = nc.dram_tensor("sh_k", [1024, T], BF16, **Sh).ap()
    A.sh_v = nc.dram_tensor("sh_v", [T, 1024], BF16, **Sh).ap()
    A.sh_x2 = nc.dram_tensor("sh_x2", [D, T], BF16, **Sh).ap()
    A.sh_r1 = nc.dram_tensor("sh_r1", [2, 128, DC], F32, **Sh).ap()
    A.sh_fl = nc.dram_tensor("sh_fl", [1, 16], I32, **Sh).ap()
    A.pr_ctx = nc.dram_tensor("pr_ctx", [D, QSH], BF16, kind="Internal").ap()
    A.pr_rt = nc.dram_tensor("pr_rt", [1, 8], F32, kind="Internal").ap()

    A.dbg = {}
    A._dbg_pending = list(_DEBUG)

    sb = lambda *a: ctx.enter_context(nc.sbuf_tensor(*a))
    A.at_bf = sb("at_bf", [128, DC * 128], BF16)
    A.bt_bf = sb("bt_bf", [128, D], BF16)
    A.rw_s = sb("rw_s", [128, DC * 8], F32)
    A.bias_s = sb("bias_s", [128, DC], F32)
    A.hbias_s = sb("hbias_s", [128, DC], F32)
    A.rb8_s = sb("rb8_s", [8, 1], F32)
    A.rowb8 = sb("rowb8", [8, 1], F32)
    A.ones8f = sb("ones8f", [8, 1], F32)
    A.id_bf = sb("id_bf", [128, 128], BF16)
    A.mA_s = sb("mA_s", [128, 128], BF16)
    A.mB_s = sb("mB_s", [128, 128], BF16)
    A.ones_col = sb("ones_col", [128, 1], BF16)
    A.ones_row = sb("ones_row", [1, 128], F32)
    A.e16_s = sb("e16_s", [8, 128], F32)
    A.eps_s = sb("eps_s", [1, 1], F32)
    A.magic_s = sb("magic_s", [1, 1], I32)
    A.ids_s = sb("ids_s", [1, 8], I32)
    A.iot_s = sb("iot_s", [1, 64], F32)
    A.inv_s = sb("inv_s", [1, 64], F32)
    A.fl_s = sb("fl_s", [1, 16], I32)
    A.rt_row = sb("rt_row", [1, 8], F32)
    A.rt_col8 = sb("rt_col8", [8, 1], F32)
    A.rt_col = sb("rt_col", [128, 1], F32)
    A.s1_col = sb("s1_col", [128, 1], F32)
    A.r1part = sb("r1part", [128, DC], F32)
    A.r1mate = sb("r1mate", [128, DC], F32)
    A.q2col = sb("q2col", [128, DC], F32)
    A.rowa = sb("rowa", [1, 512], F32)
    A.rowb = sb("rowb", [1, 512], F32)
    A.rowc = sb("rowc", [1, 512], F32)
    A.rrow = sb("rrow", [1, 512], F32)
    A.sc1 = sb("sc1", [1, 1], F32)
    A.sc2 = sb("sc2", [1, 1], F32)
    A.cs_cos = sb("cs_cos", [64, OWN], BF16)
    A.cs_sin = sb("cs_sin", [64, OWN], BF16)
    A.csq_cos = sb("csq_cos", [64, QSH], BF16)
    A.csq_sin = sb("csq_sin", [64, QSH], BF16)

    ARENA = 160 * KB
    A.arena = sb("arena", [128, ARENA], U8)

    def carve(off, cols, dt, rows=128):
        esz = mybir.dt.size(dt)
        assert off % 32 == 0 and off + cols * esz <= ARENA, (off, cols, esz)
        return A.arena[0:rows, off:off + cols * esz].bitcast(dt)
    A.carve = carve

    A.ps0 = ctx.enter_context(nc.psum_tensor("ps0", [128, 512], F32))
    A.ps1 = ctx.enter_context(nc.psum_tensor("ps1", [128, 512], F32))
    A.ps2 = ctx.enter_context(nc.psum_tensor("ps2", [128, 512], F32))
    A.psW = ctx.enter_context(nc.psum_tensor("psW", [128, 1024], F32))
    A.psW2 = ctx.enter_context(nc.psum_tensor("psW2", [128, 1024], F32))
    A.ps_t = ctx.enter_context(nc.psum_tensor("ps_t", [128, 512], BF16))

    phase0(A)
    if 1 in _PHASES:
        phase1(A)
    if 2 in _PHASES:
        phase2(A)
    if 3 in _PHASES:
        phase3(A)
    if 4 in _PHASES:
        phase4(A)
    finish(A)

    with nc.Block() as block:
        block.sync(lambda e: [f(e) for f in p.th["sync"]])
        block.tensor(lambda e: [f(e) for f in p.th["tensor"]])
        block.scalar(lambda e: [f(e) for f in p.th["scalar"]])
        block.vector(lambda e: [f(e) for f in p.th["vector"]])
        block.gpsimd(lambda e: [f(e) for f in p.th["gpsimd"]])
    ctx.close()
    return nc


def phase0(A):
    p, d = A.p, A.d
    for name, dst in [("rwT_blk", A.rw_s), ("biasw", A.bias_s),
                      ("rbias", A.rb8_s), ("iot", A.iot_s), ("ids", A.ids_s),
                      ("e16", A.e16_s), ("nonce", A.magic_s)]:
        p.dma("sync", "si", dst[:], d[name])
    for name, dst in [("at_blk", A.at_bf), ("id128", A.id_bf),
                      ("maskA", A.mA_s), ("maskB", A.mB_s)]:
        p.dma("gpsimd", "gw", dst[:], d[name])
    ld_small = p.fence([("si", p.cur("si")), ("gw", p.cur("gw"))])

    def f_ms(e):
        e.memset(A.ones_col[:], 1.0)
        e.memset(A.ones8f[:], 1.0)
        e.memset(A.ones_row[:], 1.0)
        e.memset(A.eps_s[:], EPS)
        return e.tensor_scalar_mul(out=A.hbias_s[:], in0=A.bias_s[:], scalar1=0.5)
    ve_ms = p.step("vector", "ve", f_ms, waits=[("sy", ld_small)])

    se_inv = p.step("scalar", "se",
                    lambda e: e.activation(out=A.inv_s[:], in_=A.iot_s[:],
                                           func=AF.Exp, scale=LN_RB),
                    waits=[("sy", ld_small)])

    pos_s = A.carve(0, 2048, F32, rows=1)
    ang = A.carve(32 * KB, 2048, F32, rows=64)
    tmp = A.carve(48 * KB, 2048, F32, rows=64)
    tif = A.carve(64 * KB, 2048, F32, rows=64)
    for posname, cdst, sdst in [("posr", A.cs_cos, A.cs_sin),
                                ("posq", A.csq_cos, A.csq_sin)]:
        ldp = p.fence([("si", p.dma("sync", "si", pos_s[:], d[posname],
                                    waits=[("se", p.cur("se"))]))])
        for blk in range(4):
            sl = slice(blk * 512, (blk + 1) * 512)
            te_a = p.step("tensor", "te",
                          lambda e, sl=sl: e.matmul(
                              A.psW[0:64, 0:512], A.inv_s[:], pos_s[0:1, sl],
                              start=True, stop=True),
                          waits=[("sy", ldp), ("se", se_inv),
                                 ("ve", p.cur("ve"))])

            def f_rr(e, sl=sl):
                e.tensor_scalar_mul(out=ang[0:64, sl], in0=A.psW[0:64, 0:512],
                                    scalar1=1.0 / TWO_PI)
                e.tensor_copy(out=tif[0:64, sl].bitcast(I32), in_=ang[0:64, sl])
                e.tensor_copy(out=tmp[0:64, sl], in_=tif[0:64, sl].bitcast(I32))
                e.tensor_tensor(out=ang[0:64, sl], in0=ang[0:64, sl],
                                in1=tmp[0:64, sl], op=ALU.subtract)
                e.tensor_scalar_mul(out=ang[0:64, sl], in0=ang[0:64, sl],
                                    scalar1=TWO_PI)
                e.tensor_scalar_add(out=tmp[0:64, sl], in0=ang[0:64, sl],
                                    scalar1=PI / 2.0)
                e.tensor_scalar(out=tif[0:64, sl], in0=tmp[0:64, sl],
                                scalar1=PI, scalar2=None, op0=ALU.is_gt)
                e.tensor_scalar_mul(out=tif[0:64, sl], in0=tif[0:64, sl],
                                    scalar1=-TWO_PI)
                return e.tensor_tensor(out=tmp[0:64, sl], in0=tmp[0:64, sl],
                                       in1=tif[0:64, sl], op=ALU.add)
            ve_rr = p.step("vector", "ve", f_rr, waits=[("te", te_a)])

            def f_sc(e, sl=sl, cdst=cdst, sdst=sdst):
                e.activation(out=sdst[0:64, sl], in_=ang[0:64, sl], func=AF.Sin)
                return e.activation(out=cdst[0:64, sl], in_=tmp[0:64, sl],
                                    func=AF.Sin)
            p.step("scalar", "se", f_sc, waits=[("ve", ve_rr)])
    A.tables_done = p.fence([("se", p.cur("se")), ("ve", ve_ms)])


def ln_block(A, ncols, loader, sq_off, mub, rbb, rows_out=None):
    p = A.p
    nblk = ncols // 512
    sq = A.carve(sq_off, ncols, BF16)
    srow = [A.ps0[0:1, :], A.ps1[0:1, :]]
    qrow = [A.ps2[0:1, :], A.psW[0:1, 0:512]]
    te_hist = {}
    for dc in range(DC):
        gates = [("te", te_hist[dc - 2])] if dc >= 2 else p.barrier_gates()
        ch, csem = loader(dc, gates)
        se_sq = p.step("scalar", "se",
                       lambda e, ch=ch: e.activation(out=sq[:, 0:ncols], in_=ch,
                                                     func=AF.Square),
                       waits=[csem, ("te", p.cur("te"))])

        def f_mm(e, ch=ch, dc=dc):
            last = None
            for b in range(nblk):
                sl = slice(b * 512, (b + 1) * 512)
                e.matmul(srow[b], A.ones_col[:], ch[:, sl],
                         start=(dc == 0), stop=True)
                last = e.matmul(qrow[b], A.ones_col[:], sq[:, sl],
                                start=(dc == 0), stop=True)
            return last
        wts = [csem, ("se", se_sq)]
        if dc == 0:
            wts.append(("ve", p.cur("ve")))
        te_hist[dc] = p.step("tensor", "te", f_mm, waits=wts)
    te_mm = p.cur("te")
    se_last = None
    te_b_prev = None
    for b in range(nblk):
        sl = slice(b * 512, (b + 1) * 512)

        def f_pp(e, b=b):
            e.tensor_scalar_mul(out=A.rowa[:], in0=srow[b], scalar1=1.0 / D)
            e.tensor_scalar_mul(out=A.rowb[:], in0=qrow[b], scalar1=1.0 / D)
            e.tensor_tensor(out=A.rowc[:], in0=A.rowa[:], in1=A.rowa[:],
                            op=ALU.mult)
            return e.tensor_tensor(out=A.rowb[:], in0=A.rowb[:], in1=A.rowc[:],
                                   op=ALU.subtract)
        ppw = [("te", te_mm)]
        if te_b_prev is not None:
            ppw.append(("te", te_b_prev))
        ve1 = p.step("vector", "ve", f_pp, waits=ppw)
        se1 = p.step("scalar", "se",
                     lambda e: e.activation(out=A.rowb[:], in_=A.rowb[:],
                                            func=AF.Sqrt, bias=A.eps_s[:]),
                     waits=[("ve", ve1)])
        ve2 = p.step("vector", "ve",
                     lambda e: e.reciprocal(out=A.rowc[:], in_=A.rowb[:]),
                     waits=[("se", se1)])
        if rows_out is not None:
            mu_row, r_row = rows_out

            def f_cp(e, sl=sl, mu_row=mu_row, r_row=r_row):
                e.tensor_copy(out=mu_row[0:1, sl], in_=A.rowa[:])
                return e.tensor_copy(out=r_row[0:1, sl], in_=A.rowc[:])
            ve2 = p.step("vector", "ve", f_cp, waits=[("ve", ve2)])

        def f_bc(e):
            e.matmul(A.psW2[:, 0:512], A.ones_row[:], A.rowa[:],
                     start=True, stop=True)
            return e.matmul(A.psW2[:, 512:1024], A.ones_row[:], A.rowc[:],
                            start=True, stop=True)
        te_b = p.step("tensor", "te", f_bc,
                      waits=[("ve", ve2), ("se", p.cur("se"))])
        te_b_prev = te_b

        def f_ev(e, sl=sl):
            e.copy(out=mub[:, sl], in_=A.psW2[:, 0:512])
            return e.copy(out=rbb[:, sl], in_=A.psW2[:, 512:1024])
        se_last = p.step("scalar", "se", f_ev, waits=[("te", te_b)])
    return se_last


def ln_normalize(A, ncols, loader, mub, rbb, xout, tmpf_off, ready_se):
    p = A.p
    tmpf = A.carve(tmpf_off, ncols, F32)
    ve_hist = {}
    ve_last = None
    for dc in range(DC):
        gates = [("ve", ve_hist[dc - 2])] if dc >= 2 else p.barrier_gates()
        ch, csem = loader(dc, gates)
        xo = xout(dc)

        def f_n(e, ch=ch, xo=xo):
            e.tensor_tensor(out=tmpf[:, 0:ncols], in0=ch, in1=mub[:, 0:ncols],
                            op=ALU.subtract)
            return e.tensor_tensor(out=xo[:, 0:ncols], in0=tmpf[:, 0:ncols],
                                   in1=rbb[:, 0:ncols], op=ALU.mult)
        wts = [csem]
        if dc == 0:
            wts += [("se", ready_se), ("sy", A.tables_done)]
        ve_last = p.step("vector", "ve", f_n, waits=wts)
        ve_hist[dc] = ve_last
    return ve_last


def router_pipeline(A, qcol, mean_scale, extra_gates=()):
    p = A.p

    def f_lg(e):
        last = None
        for dc in range(DC):
            last = e.matmul(A.ps2[0:8, 0:1], A.rw_s[:, dc * 8:(dc + 1) * 8],
                            qcol[:, dc:dc + 1], start=(dc == 0), stop=True)
        return last
    te_lg = p.step("tensor", "te", f_lg,
                   waits=[("ve", p.cur("ve")), ("se", p.cur("se"))] +
                   list(extra_gates))
    se_e = p.step("scalar", "se",
                  lambda e: e.activation(out=A.rowb8[:], in_=A.ps2[0:8, 0:1],
                                         func=AF.Exp, bias=A.rb8_s[:],
                                         scale=mean_scale),
                  waits=[("te", te_lg)])

    def f_sx(e):
        e.matmul(A.ps2[0:1, 8:9], A.rowb8[:], A.ones8f[:],
                 start=True, stop=True)
        return e.matmul(A.ps2[:, 16:17], A.e16_s[:], A.rowb8[:],
                        start=True, stop=True)
    te_sx = p.step("tensor", "te", f_sx, waits=[("se", se_e)])

    def f_rcp(e):
        e.reciprocal(out=A.sc2[:], in_=A.ps2[0:1, 8:9])
        return e.tensor_copy(out=A.sc1[:], in_=A.sc2[:])
    ve_r = p.step("vector", "ve", f_rcp, waits=[("te", te_sx)])
    te_bc = p.step("tensor", "te",
                   lambda e: e.matmul(A.ps2[:, 24:25], A.ones_row[0:1, :],
                                      A.sc2[:], start=True, stop=True),
                   waits=[("ve", ve_r)])
    bsem = p.dma("gpsimd", "gw", A.bt_bf[:], A.d["bmat"],
                 waits=p.barrier_gates())

    def f_bt(e):
        e.tensor_scalar_mul(out=A.bt_bf[:], in0=A.bt_bf[:],
                            scalar1=A.ps2[:, 16:17])
        e.tensor_scalar_mul(out=A.bt_bf[:], in0=A.bt_bf[:],
                            scalar1=A.ps2[:, 24:25])
        return e.tensor_copy(out=A.rt_col[:], in_=A.bt_bf[0:128, 0:1])
    return p.step("vector", "ve", f_bt, waits=[("te", te_bc), ("gw", bsem)])


def emit_t2(A, xout, ncols, t2_off, extra_gates=()):
    p = A.p
    t2 = A.carve(t2_off, ncols, BF16)
    se_t = None
    for blk in range(ncols // 512):
        sl = slice(blk * 512, (blk + 1) * 512)

        def f_mm(e, sl=sl):
            last = None
            for dc in range(DC):
                last = e.matmul(A.ps2[:], A.at_bf[:, dc * 128:(dc + 1) * 128],
                                xout(dc)[:, sl], start=(dc == 0), stop=True)
            return last
        wts = [("se", p.cur("se"))] + (list(extra_gates) if blk == 0 else [])
        te_t = p.step("tensor", "te", f_mm, waits=wts)
        se_t = p.step("scalar", "se",
                      lambda e, sl=sl: e.copy(out=t2[:, sl], in_=A.ps2[:]),
                      waits=[("te", te_t)])
    return se_t, t2


def ffn_base(A, slab_src, xout, ncols, t2, resid_loader, out_writer,
             first_gates=()):
    p = A.p
    nblk = ncols // 512
    slabs = [A.carve(A.SLAB_OFF + i * 8 * KB, DC * 128, BF16) for i in range(2)]
    esz = mybir.dt.size(A.OST_DT)
    osts = [A.carve(A.OST_OFF + i * 512 * esz, 512, A.OST_DT) for i in range(2)]
    te_oc = {}
    ve_hist = {}
    st_hist = {}
    for oc in range(DC):
        sg = [("te", te_oc[oc - 2])] if oc >= 2 else p.barrier_gates()
        gsem = ("g" + str(oc % 2),
                p.dma("gpsimd", "g" + str(oc % 2), slabs[oc % 2][:],
                      slab_src(oc), waits=sg))
        rg = [("ve", ve_hist[oc - 2])] if oc >= 2 else p.barrier_gates()
        rap, rsem = resid_loader(oc, rg)
        for blk in range(nblk):
            sl = slice(blk * 512, (blk + 1) * 512)
            psu = A.ps0 if blk % 2 == 0 else A.ps1

            def f_mm(e, sl=sl, psu=psu, slab=slabs[oc % 2], oc=oc):
                for dc in range(DC):
                    e.matmul(psu[:], slab[:, dc * 128:(dc + 1) * 128],
                             xout(dc)[:, sl], start=(dc == 0), stop=True)
                return e.matmul(psu[:], A.bt_bf[:, oc * 128:(oc + 1) * 128],
                                t2[:, sl], start=False, stop=True)
            wts = [gsem, ("se", p.cur("se"))]
            if oc == 0 and blk == 0:
                wts += list(first_gates)
            te_b = p.step("tensor", "te", f_mm, waits=wts)
            if blk == nblk - 1:
                te_oc[oc] = te_b
            se_b = p.step("scalar", "se",
                          lambda e, psu=psu, oc=oc:
                          e.activation(out=A.ps2[:], in_=psu[:],
                                       func=AF.Identity,
                                       bias=A.hbias_s[:, oc:oc + 1], scale=0.5),
                          waits=[("te", te_b), ("ve", p.cur("ve"))])
            it = oc * nblk + blk
            wts_v = [("se", se_b), rsem]
            if it >= 2:
                wts_v.append(st_hist[it - 2])
            ve_b = p.step("vector", "ve",
                          lambda e, rap=rap, sl=sl, ost=osts[blk % 2]:
                          e.tensor_tensor(out=ost[:], in0=A.ps2[:],
                                          in1=rap[:, sl], op=ALU.add),
                          waits=wts_v)
            ve_hist[oc] = ve_b
            st_hist[it] = out_writer(oc, blk, osts[blk % 2], ("ve", ve_b))


def spin_barrier(A, words, pbname):
    p = A.p

    p.sem(pbname)

    def f(e):
        with e.register() as tot, e.register() as cnd, e.register() as tmpr, \
             e.register() as non:
            e.reg_load(non, A.magic_s[0:1, 0:1])

            def cond():
                e.dma_start(out=A.fl_s[:], in_=A.sh_fl).then_inc(
                    p.sems[pbname], 16)
                e.nop(cycle_cnt=30000)
                e.nop(cycle_cnt=30000)
                e.reg_mov(tot, 0)
                for w in words:
                    e.reg_load(tmpr, A.fl_s[0:1, w:w + 1])
                    e.reg_alu(tmpr, tmpr, non, ALU.is_equal)
                    e.reg_alu(tot, tot, tmpr, ALU.add)
                e.reg_alu(cnd, tot, len(words), ALU.is_lt)
                return cnd
            with e.While(cond):
                e.nop()
    p.emit("sync", f)


def write_flag(A, words_by_pp, waits):
    p = A.p
    ws = [(p.sem(n), v) for n, v in waits]
    p.inc("so", 16)

    def f(e, ws=ws):
        for sem, v in ws:
            e.wait_ge(sem, v)
        with e.register() as r:
            e.reg_load(r, A.ids_s[0:1, 0:1])
            for pp, w in words_by_pp.items():
                with e.If_cmp(r, pp, "IS_EQ"):
                    e.dma_start(out=A.sh_fl[0:1, w:w + 1],
                                in_=A.magic_s[:]).then_inc(p.sems["so"], 16)
    p.emit("sync", f)
    return p.cur("so")


def phase1(A):
    p, d = A.p, A.d
    xbf = [A.carve(128 * KB + i * 4 * KB, OWN, BF16) for i in range(2)]
    mub = A.carve(136 * KB, OWN, F32)
    rbb = A.carve(144 * KB, OWN, F32)

    def xln1(dc):
        return A.carve(dc * 4 * KB, OWN, BF16)

    kx = [0]

    def mk_loader(h):
        def loader(dc, gates):
            slot = kx[0] % 2
            kx[0] += 1
            cols = OWN if h is None else 1024
            c0 = 0 if h is None else h * 1024
            sem = p.dma("gpsimd", "x" + str(slot), xbf[slot][:, 0:cols],
                        d["xT"][dc * 128:(dc + 1) * 128, c0:c0 + cols],
                        waits=gates)
            return xbf[slot][:, 0:cols], ("x" + str(slot), sem)
        return loader

    se_bc = None
    for h in range(2):
        se_bc = ln_block(A, 1024, mk_loader(h), 152 * KB,
                         mub[:, h * 1024:(h + 1) * 1024],
                         rbb[:, h * 1024:(h + 1) * 1024])
    ve_n = ln_normalize(A, OWN, mk_loader(None), mub, rbb, xln1, 152 * KB,
                        se_bc)

    def f_r1(e):
        with e.register() as r:
            e.reg_load(r, A.ids_s[0:1, 0:1])
            with e.If_cmp(r, 0, "IS_EQ"):
                for dc in range(DC):
                    e.tensor_reduce(out=A.r1part[:, dc:dc + 1],
                                    in_=xln1(dc)[:, Q0T:2048], axis=AX.X,
                                    op=ALU.add)
            with e.If_cmp(r, 1, "IS_EQ"):
                for dc in range(DC):
                    e.tensor_reduce(out=A.r1part[:, dc:dc + 1],
                                    in_=xln1(dc)[:, 0:Q1T - 2048], axis=AX.X,
                                    op=ALU.add)
        return e.tensor_copy(out=A.rowc[0:1, 0:1], in_=A.r1part[0:1, 0:1])
    ve_r1 = p.step("vector", "ve", f_r1, waits=[("ve", ve_n)])

    tgt_r1 = p.inc("so", 16)

    def f_wr1(e):
        e.wait_ge(p.sems["ve"], ve_r1)
        with e.register() as r:
            e.reg_load(r, A.ids_s[0:1, 0:1])
            for pp in (0, 1):
                with e.If_cmp(r, pp, "IS_EQ"):
                    e.dma_start(out=A.sh_r1[pp],
                                in_=A.r1part[:]).then_inc(p.sems["so"], 16)
    p.emit("sync", f_wr1)
    write_flag(A, {0: 0, 1: 1}, [("so", tgt_r1)])

    se_t2, t2 = emit_t2(A, xln1, OWN, 136 * KB, extra_gates=[("ve", ve_n)])

    spin_barrier(A, [0, 1], "pb0")
    tgt_m = p.inc("si", 16)

    def f_rdm(e):
        with e.register() as r:
            e.reg_load(r, A.ids_s[0:1, 0:1])
            for pp in (0, 1):
                with e.If_cmp(r, pp, "IS_EQ"):
                    e.dma_start(out=A.r1mate[:],
                                in_=A.sh_r1[1 - pp]).then_inc(p.sems["si"], 16)
    p.emit("sync", f_rdm)
    p.step("vector", "ve",
           lambda e: e.tensor_tensor(out=A.q2col[:], in0=A.r1part[:],
                                     in1=A.r1mate[:], op=ALU.add),
           waits=[("si", tgt_m)])
    ve_bt = router_pipeline(A, A.q2col, 1.0 / NROUT,
                            extra_gates=[("se", se_t2)])

    A.SLAB_OFF = 140 * KB
    A.OST_OFF = 156 * KB
    A.OST_DT = BF16

    def resid_loader(oc, gates):
        slot = kx[0] % 2
        kx[0] += 1
        sem = p.dma("gpsimd", "x" + str(slot), xbf[slot][:],
                    d["xT"][oc * 128:(oc + 1) * 128, :], waits=gates)
        return xbf[slot], ("x" + str(slot), sem)

    def out_writer(oc, blk, src, vsem):
        semn = "oA" + str((oc * 4 + blk) % 2)
        tgt = p.inc(semn, 16)

        def f(e, oc=oc, blk=blk, src=src, vsem=vsem, semn=semn):
            e.wait_ge(p.sems[vsem[0]], vsem[1])
            with e.register() as r:
                e.reg_load(r, A.ids_s[0:1, 0:1])
                for pp in (0, 1):
                    with e.If_cmp(r, pp, "IS_EQ"):
                        c0 = pp * 2048 + blk * 512
                        e.dma_start(
                            out=A.sh_x1[oc * 128:(oc + 1) * 128, c0:c0 + 512],
                            in_=src[:]).then_inc(p.sems[semn], 16)
        p.emit("sync", f)
        return (semn, tgt)

    ffn_base(A, lambda oc: d["wt_blk"][oc], xln1, OWN, t2, resid_loader,
             out_writer, first_gates=[("ve", ve_bt)])
    A.x1_done = [("oA0", p.cur("oA0")), ("oA1", p.cur("oA1"))]


def phase2(A):
    p, d = A.p, A.d
    x1bf = [A.carve(128 * KB + i * 4 * KB, OWN, BF16) for i in range(2)]
    mub = A.carve(136 * KB, OWN, F32)
    rbb = A.carve(144 * KB, OWN, F32)

    def xln2o(dc):
        return A.carve(dc * 4 * KB, OWN, BF16)

    kx = [0]

    def mk_loader(h):
        def loader(dc, gates):
            slot = kx[0] % 2
            kx[0] += 1
            cols = OWN if h is None else 1024
            tgt = p.inc("h" + str(slot), 16)
            s = p.sems["h" + str(slot)]
            wsl = [(p.sems[n], v) for n, v in gates] + \
                  [(p.sems[n], v) for n, v in A.x1_done]

            def f(e, dc=dc, slot=slot, h=h, s=s, wsl=wsl, cols=cols):
                for sem, v in wsl:
                    e.wait_ge(sem, v)
                with e.register() as r:
                    e.reg_load(r, A.ids_s[0:1, 0:1])
                    for pp in (0, 1):
                        with e.If_cmp(r, pp, "IS_EQ"):
                            c0 = pp * 2048 + (0 if h is None else h * 1024)
                            e.dma_start(out=x1bf[slot][:, 0:cols],
                                        in_=A.sh_x1[dc * 128:(dc + 1) * 128,
                                                    c0:c0 + cols]).then_inc(s, 16)
            p.emit("sync", f)
            return x1bf[slot][:, 0:cols], ("h" + str(slot), tgt)
        return loader

    se_bc = None
    for h in range(2):
        se_bc = ln_block(A, 1024, mk_loader(h), 152 * KB,
                         mub[:, h * 1024:(h + 1) * 1024],
                         rbb[:, h * 1024:(h + 1) * 1024])
    ve_n = ln_normalize(A, OWN, mk_loader(None), mub, rbb, xln2o, 152 * KB,
                        se_bc)

    kslab = [A.carve(136 * KB + i * 8 * KB, DC * 128, BF16) for i in range(2)]
    rs1 = A.carve(152 * KB, 512, F32)
    rs2 = A.carve(154 * KB, 512, F32)
    kvst = [A.carve(156 * KB + i * KB, 512, BF16) for i in range(2)]
    vtst = [A.carve(158 * KB + i * 256, 128, BF16) for i in range(2)]
    te_oc = {}
    st_hist = {}

    for oc in range(NKV):
        sg = [("te", te_oc[oc - 2])] if oc >= 2 else \
            [("ve", ve_n), ("te", p.cur("te"))]
        gsem = p.dma("gpsimd", "g" + str(oc % 2), kslab[oc % 2][:],
                     d["wk_blk"][oc], waits=sg)
        for blk in range(4):
            sl = slice(blk * 512, (blk + 1) * 512)
            psu = A.ps0 if blk % 2 == 0 else A.ps1

            def f_mm(e, sl=sl, psu=psu, slab=kslab[oc % 2]):
                last = None
                for dc in range(DC):
                    last = e.matmul(psu[:], slab[:, dc * 128:(dc + 1) * 128],
                                    xln2o(dc)[:, sl], start=(dc == 0), stop=True)
                return last
            te_b = p.step("tensor", "te", f_mm,
                          waits=[("g" + str(oc % 2), gsem),
                                 ("ve", p.cur("ve"))])
            if blk == 3:
                te_oc[oc] = te_b
            it = oc * 4 + blk
            wts_r = [("te", te_b)]
            if it >= 2:
                wts_r.append(st_hist[it - 2])

            def f_rope(e, sl=sl, psu=psu, st=kvst[blk % 2]):
                e.tensor_tensor(out=rs1[0:64, :], in0=psu[0:64, :],
                                in1=A.cs_cos[:, sl], op=ALU.mult)
                e.tensor_tensor(out=rs2[0:64, :], in0=psu[64:128, :],
                                in1=A.cs_sin[:, sl], op=ALU.mult)
                e.tensor_tensor(out=st[0:64, :], in0=rs1[0:64, :], in1=rs2[0:64, :],
                                op=ALU.subtract)
                e.tensor_tensor(out=rs1[0:64, :], in0=psu[0:64, :],
                                in1=A.cs_sin[:, sl], op=ALU.mult)
                e.tensor_tensor(out=rs2[0:64, :], in0=psu[64:128, :],
                                in1=A.cs_cos[:, sl], op=ALU.mult)
                return e.tensor_tensor(out=st[64:128, :], in0=rs1[0:64, :],
                                       in1=rs2[0:64, :], op=ALU.add)
            ve_r = p.step("vector", "ve", f_rope, waits=wts_r)
            semn = "oK" + str(blk % 2)
            tgt = p.inc(semn, 16)

            def wf(e, oc=oc, blk=blk, st=kvst[blk % 2], ve_r=ve_r, semn=semn):
                e.wait_ge(p.sems["ve"], ve_r)
                with e.register() as r:
                    e.reg_load(r, A.ids_s[0:1, 0:1])
                    for pp in (0, 1):
                        with e.If_cmp(r, pp, "IS_EQ"):
                            c0 = pp * 2048 + blk * 512
                            e.dma_start(
                                out=A.sh_k[oc * 128:(oc + 1) * 128, c0:c0 + 512],
                                in_=st[:]).then_inc(p.sems[semn], 16)
            p.emit("sync", wf)
            st_hist[it] = (semn, tgt)

    stv_hist = {}
    for oc in range(NKV):
        koc = NKV + oc
        sg = [("te", te_oc[koc - 2])]
        gsem = p.dma("gpsimd", "g" + str(koc % 2), kslab[koc % 2][:],
                     d["wv_blk"][oc], waits=sg)
        for blk in range(4):
            sl = slice(blk * 512, (blk + 1) * 512)
            psu = A.ps0 if blk % 2 == 0 else A.ps1

            def f_mm(e, sl=sl, psu=psu, slab=kslab[koc % 2]):
                last = None
                for dc in range(DC):
                    last = e.matmul(psu[:], slab[:, dc * 128:(dc + 1) * 128],
                                    xln2o(dc)[:, sl], start=(dc == 0), stop=True)
                return last
            te_b = p.step("tensor", "te", f_mm,
                          waits=[("g" + str(koc % 2), gsem),
                                 ("se", p.cur("se"))])
            if blk == 3:
                te_oc[koc] = te_b
            it = oc * 4 + blk
            wts_s = [("te", te_b), ("ve", p.cur("ve"))]
            if it >= 2:
                wts_s.append(st_hist[it - 2])
            else:
                wts_s += [("oK0", p.cur("oK0")), ("oK1", p.cur("oK1"))]
            se_v = p.step("scalar", "se",
                          lambda e, psu=psu, st=kvst[blk % 2]:
                          e.copy(out=st[:], in_=psu[:]),
                          waits=wts_s)
            st_hist[it] = ("se", se_v)
            for tt in range(4):
                itv = it * 4 + tt
                wts_t = [("se", se_v), ("ve", p.cur("ve"))]
                te_t = p.step("tensor", "te",
                              lambda e, st=kvst[blk % 2], tt=tt:
                              e.transpose(A.ps_t[:, 0:128],
                                          st[:, tt * 128:(tt + 1) * 128],
                                          A.id_bf[:]),
                              waits=wts_t)
                wts_c = [("te", te_t)]
                if itv >= 2:
                    wts_c.append(stv_hist[itv - 2])
                ve_t = p.step("vector", "ve",
                              lambda e, tt=tt:
                              e.tensor_copy(out=vtst[tt % 2][:],
                                            in_=A.ps_t[:, 0:128]),
                              waits=wts_c)
                semn = "oV" + str(tt % 2)
                tgt = p.inc(semn, 16)

                def wf(e, oc=oc, blk=blk, tt=tt, ve_t=ve_t,
                       st=vtst[tt % 2], semn=semn):
                    e.wait_ge(p.sems["ve"], ve_t)
                    with e.register() as r:
                        e.reg_load(r, A.ids_s[0:1, 0:1])
                        for pp in (0, 1):
                            with e.If_cmp(r, pp, "IS_EQ"):
                                r0 = pp * 2048 + blk * 512 + tt * 128
                                e.dma_start(
                                    out=A.sh_v[r0:r0 + 128,
                                               oc * 128:(oc + 1) * 128],
                                    in_=st[:]).then_inc(p.sems[semn], 16)
                p.emit("gpsimd", wf)
                stv_hist[itv] = (semn, tgt)

    fl_waits = [("oK0", p.cur("oK0")), ("oK1", p.cur("oK1")),
                ("oV0", p.cur("oV0")), ("oV1", p.cur("oV1"))] + A.x1_done
    write_flag(A, {0: 2, 1: 3}, fl_waits)


def phase3(A):
    p, d = A.p, A.d
    spin_barrier(A, [2, 3], "pb1")
    b1 = p.fence([], name="sy")

    x1bf = [A.carve(64 * KB + i * 2 * KB, 1024, BF16) for i in range(2)]
    mub = A.carve(72 * KB, 1024, F32)
    rbb = A.carve(76 * KB, 1024, F32)
    wqs = [A.carve(80 * KB + i * 8 * KB, DC * 128, BF16) for i in range(2)]
    khs = [A.carve(96 * KB + i * 8 * KB, 4096, BF16) for i in range(2)]
    vhs = [A.carve(112 * KB + i * 8 * KB, 4096, BF16) for i in range(2)]
    qss = [A.carve(128 * KB + i * 2 * KB, 1024, BF16) for i in range(2)]
    pbf = [A.carve(132 * KB + i * 2 * KB, 1024, BF16) for i in range(2)]
    rs1 = A.carve(136 * KB, 512, F32)
    rs2 = A.carve(138 * KB, 512, F32)
    ctxf = A.carve(140 * KB, 512, F32)
    ctxb = [A.carve(142 * KB + i * KB, 512, BF16) for i in range(2)]

    def xln2q(dc):
        return A.carve(dc * 2 * KB, 1024, BF16)

    kx = [0]
    last_exp_psX = {0: None, 1: None}
    last_av_pbf = {0: None, 1: None}
    last_block_end = {"ve_rc": None, "se_cx": None, "ve_cx": None}
    ctxst = {}
    nst = [0]

    for qh_half in ([0] if _ATT1 else range(2)):
        kmax = 16 if qh_half == 0 else 32

        def loader(dc, gates):
            slot = kx[0] % 2
            kx[0] += 1
            tgt = p.inc("h" + str(slot), 16)
            s = p.sems["h" + str(slot)]
            wsl = [(p.sems[n], v) for n, v in gates]

            def f(e, dc=dc, slot=slot, s=s, wsl=wsl, qh_half=qh_half):
                for sem, v in wsl:
                    e.wait_ge(sem, v)
                e.wait_ge(p.sems["sy"], b1)
                src = A.sh_x1[dc * 128:(dc + 1) * 128,
                              2048 * qh_half:2048 * (qh_half + 1)]
                src = src.rearrange("p (s two i) -> p s two i", two=2, i=128)
                with e.register() as r:
                    e.reg_load(r, A.ids_s[0:1, 0:1])
                    for pp in (0, 1):
                        with e.If_cmp(r, pp, "IS_EQ"):
                            e.dma_start(
                                out=x1bf[slot][:].rearrange(
                                    "p (s i) -> p s i", i=128),
                                in_=src[:, :, pp, :]).then_inc(s, 16)
            p.emit("sync", f)
            return x1bf[slot], ("h" + str(slot), tgt)

        se_bc = ln_block(A, 1024, loader, 68 * KB, mub, rbb)
        ve_n = ln_normalize(A, 1024, loader, mub, rbb, xln2q, 68 * KB, se_bc)

        te_h_done = {}
        for h in ([0] if _ATT1 else range(NKV)):
            kg = [("te", te_h_done[h - 2])] if h >= 2 else \
                [("te", p.cur("te")), ("se", p.cur("se"))]
            ksem = ("K" + str(h % 2),
                    p.dma("sync", "K" + str(h % 2), khs[h % 2][:, 0:kmax * 128],
                          A.sh_k[h * 128:(h + 1) * 128, 0:kmax * 128],
                          waits=kg + [("sy", b1)]))
            vsem = ("V" + str(h % 2),
                    p.dma("sync", "V" + str(h % 2),
                          vhs[h % 2][:, 0:kmax * 128].rearrange(
                              "p (tc v) -> p tc v", v=128),
                          A.sh_v[0:kmax * 128,
                                 h * 128:(h + 1) * 128].rearrange(
                              "(tc p) v -> p tc v", p=128),
                          waits=kg + [("sy", b1)]))
            kh = khs[h % 2]
            vh = vhs[h % 2]
            for s4 in ([0] if _ATT1 else range(4)):
                qh = 4 * h + s4
                qsl = qss[qh % 2]
                wg = [("ve", p.cur("ve"))]
                wqsem = ("g" + str(qh % 2),
                         p.dma("gpsimd", "g" + str(qh % 2), wqs[qh % 2][:],
                               d["wq_blk"][qh],
                               waits=[("te", p.cur("te"))] + (
                                   [("ve", ve_n)] if qh < 2 and qh_half == 0
                                   else [])))
                ve_rope = None
                for qb in range(2):
                    sl = slice(qb * 512, (qb + 1) * 512)

                    def f_q(e, sl=sl, slab=wqs[qh % 2]):
                        last = None
                        for dc in range(DC):
                            last = e.matmul(A.ps2[:],
                                            slab[:, dc * 128:(dc + 1) * 128],
                                            xln2q(dc)[:, sl],
                                            start=(dc == 0), stop=True)
                        return last
                    te_q = p.step("tensor", "te", f_q,
                                  waits=[wqsem, ("ve", p.cur("ve"))])
                    csl = slice(qh_half * 1024 + qb * 512,
                                qh_half * 1024 + qb * 512 + 512)

                    def f_rope(e, sl=sl, csl=csl, qsl=qsl):
                        e.tensor_tensor(out=rs1[0:64, :], in0=A.ps2[0:64, :],
                                        in1=A.csq_cos[:, csl], op=ALU.mult)
                        e.tensor_tensor(out=rs2[0:64, :], in0=A.ps2[64:128, :],
                                        in1=A.csq_sin[:, csl], op=ALU.mult)
                        e.tensor_tensor(out=qsl[0:64, sl], in0=rs1[0:64, :],
                                        in1=rs2[0:64, :], op=ALU.subtract)
                        e.tensor_tensor(out=rs1[0:64, :], in0=A.ps2[0:64, :],
                                        in1=A.csq_sin[:, csl], op=ALU.mult)
                        e.tensor_tensor(out=rs2[0:64, :], in0=A.ps2[64:128, :],
                                        in1=A.csq_cos[:, csl], op=ALU.mult)
                        e.tensor_tensor(out=qsl[64:128, sl], in0=rs1[0:64, :],
                                        in1=rs2[0:64, :], op=ALU.add)
                        return e.tensor_copy(out=A.rowa[0:1, 0:1],
                                             in_=qsl[64:65, 0:1])
                    ve_rope = p.step("vector", "ve", f_rope,
                                     waits=[("te", te_q)])

                for mb in ([0] if _ATT1 else range(2)):
                    m0g = 8 * qh_half + 4 * mb
                    qcols = slice(mb * 512, (mb + 1) * 512)
                    first = [True]

                    def block_first_gates():
                        g = []
                        if last_block_end["ve_rc"] is not None:
                            g.append(("ve", last_block_end["ve_rc"]))
                        if last_block_end["se_cx"] is not None:
                            g.append(("se", last_block_end["se_cx"]))
                        return g

                    def do_tile_s(kt, par, N, qoff, masked, z):
                        psX = A.psW if par == 0 else A.psW2
                        pb_ = pbf[par]
                        wts = [("ve", ve_rope)]
                        if first[0]:
                            wts.append(ksem)
                        if last_exp_psX[par] is not None:
                            wts.append(("se", last_exp_psX[par]))
                        if N == 512 and qoff == 0 and kt + 1 <= kmax - 1 and \
                                not masked and kt % 2 == 0 and z is None:
                            def f_s(e, kt=kt, psX=psX, kh=kh, qsl=qsl,
                                    qcols=qcols):
                                e.matmul(psX[:, 0:512],
                                         kh[:, kt * 128:(kt + 1) * 128],
                                         qsl[:, qcols], start=True, stop=True)
                                return e.matmul(psX[:, 512:1024],
                                                kh[:, (kt + 1) * 128:(kt + 2) * 128],
                                                qsl[:, qcols], start=True,
                                                stop=True)
                            te_s = p.step("tensor", "te", f_s, waits=wts)
                            wid = 1024
                        else:
                            def f_s(e, kt=kt, psX=psX, N=N, qoff=qoff,
                                    kh=kh, qsl=qsl, mb=mb):
                                return e.matmul(
                                    psX[:, 0:N],
                                    kh[:, kt * 128:(kt + 1) * 128],
                                    qsl[:, mb * 512 + qoff:(mb + 1) * 512],
                                    start=True, stop=True)
                            te_s = p.step("tensor", "te", f_s, waits=wts)
                            wid = N
                        wts2 = [("te", te_s)]
                        if last_av_pbf[par] is not None:
                            wts2.append(("te", last_av_pbf[par]))
                        se_e = p.step("scalar", "se",
                                      lambda e, psX=psX, pb_=pb_, wid=wid:
                                      e.activation(out=pb_[:, 0:wid],
                                                   in_=psX[:, 0:wid],
                                                   func=AF.Exp, scale=ISQ),
                                      waits=wts2)
                        last_exp_psX[par] = se_e
                        dep = ("se", se_e)
                        if masked:
                            mask = A.mA_s if z % 2 == 0 else A.mB_s

                            def f_m(e, pb_=pb_, mask=mask):
                                e.tensor_tensor(out=pb_[:, 0:128],
                                                in0=pb_[:, 0:128],
                                                in1=mask[:], op=ALU.mult)
                                return e.tensor_copy(out=A.rowa[0:1, 1:2],
                                                     in_=pb_[0:1, 0:1])
                            ve_m = p.step("vector", "ve", f_m,
                                          waits=[("se", se_e)])
                            dep = ("ve", ve_m)
                        first[0] = False
                        return dict(kt=kt, par=par, wid=wid, qoff=qoff,
                                    dep=dep)

                    def do_tile_av(u):
                        def f_av(e, kt=u["kt"], pb_=pbf[u["par"]],
                                 wid=u["wid"], qoff=u["qoff"],
                                 vh=vh, st=firstav[0]):
                            if wid == 1024:
                                e.matmul(A.ps1[0:1, :], A.ones_col[:],
                                         pb_[:, 0:512], start=st, stop=True)
                                e.matmul(A.ps1[0:1, :], A.ones_col[:],
                                         pb_[:, 512:1024], start=False, stop=True)
                                e.matmul(A.ps0[:],
                                         vh[:, kt * 128:(kt + 1) * 128],
                                         pb_[:, 0:512], start=st, stop=True)
                                return e.matmul(
                                    A.ps0[:], vh[:, (kt + 1) * 128:(kt + 2) * 128],
                                    pb_[:, 512:1024], start=False, stop=True)
                            e.matmul(A.ps1[0:1, qoff:512], A.ones_col[:],
                                     pb_[:, 0:wid], start=st, stop=True)
                            return e.matmul(A.ps0[:, qoff:512],
                                            vh[:, kt * 128:(kt + 1) * 128],
                                            pb_[:, 0:wid], start=st, stop=True)
                        wts3 = [u["dep"]]
                        if firstav[0]:
                            wts3.append(vsem)
                            wts3 += block_first_gates()
                        te_av = p.step("tensor", "te", f_av, waits=wts3)
                        last_av_pbf[u["par"]] = te_av
                        firstav[0] = False
                        return te_av

                    par = 0
                    te_av = None
                    pend = None
                    firstav = [True]
                    for g in range(m0g):
                        u = do_tile_s(2 * g, par, 512, 0, False, None)
                        if pend is not None:
                            te_av = do_tile_av(pend)
                        pend = u
                        par ^= 1
                    for z in range(8):
                        kt = 2 * m0g + z
                        qoff = (z // 2) * 128
                        u = do_tile_s(kt, par, 512 - qoff, qoff, True, z)
                        if pend is not None:
                            te_av = do_tile_av(pend)
                        pend = u
                        par ^= 1
                    te_av = do_tile_av(pend)
                    te_h_done[h] = te_av

                    def f_rc(e):
                        e.reciprocal(out=A.rrow[:], in_=A.ps1[0:1, :])
                        return e.tensor_copy(out=A.rowa[0:1, 2:3],
                                             in_=A.rrow[0:1, 511:512])
                    ve_rc = p.step("vector", "ve", f_rc,
                                   waits=[("te", te_av)])
                    te_bc = p.step("tensor", "te",
                                   lambda e: e.matmul(A.ps2[:], A.ones_row[:],
                                                      A.rrow[:], start=True,
                                                      stop=True),
                                   waits=[("ve", ve_rc)])
                    se_cx = p.step("scalar", "se",
                                   lambda e: e.copy(out=ctxf[:], in_=A.ps0[:]),
                                   waits=[("te", te_bc)])
                    cslot = nst[0] % 2
                    wts_cx = [("se", se_cx)]
                    if nst[0] >= 2:
                        wts_cx.append(ctxst[nst[0] - 2])
                    def f_cx(e, cb=ctxb[cslot]):
                        e.tensor_tensor(out=cb[:], in0=ctxf[:],
                                        in1=A.ps2[:], op=ALU.mult)
                        return e.tensor_copy(out=A.rowa[0:1, 3:4],
                                             in_=cb[0:1, 511:512])
                    ve_cx = p.step("vector", "ve", f_cx, waits=wts_cx)
                    semn = "oC" + str(cslot)
                    tgt = p.dma("sync", semn,
                                A.pr_ctx[qh * 128:(qh + 1) * 128,
                                         qh_half * 1024 + mb * 512:
                                         qh_half * 1024 + mb * 512 + 512],
                                ctxb[cslot][:], waits=[("ve", ve_cx)])
                    ctxst[nst[0]] = (semn, tgt)
                    nst[0] += 1
                    last_block_end["ve_rc"] = ve_rc
                    last_block_end["se_cx"] = se_cx

    if _ATT1:
        A.qss0 = qss[0]
        A.khs0 = khs[0]
        A.vhs0 = vhs[0]
        A.ctxf0 = ctxf
        A.rrow0 = A.rrow
        A.ctxb0 = ctxb[0]
        A.pbf0 = pbf[0]
        A.pbf1 = pbf[1]
        A.xq0 = xln2q(0)
        return
    phase3_wo(A)


def phase3_wo(A):
    p, d = A.p, A.d
    slabs = [[A.carve((g * 32 + i * 8) * KB, DC * 128, BF16) for i in range(4)]
             for g in range(2)]
    ctxblk = [A.carve(64 * KB + i * 32 * KB, 32 * 512, BF16) for i in range(2)]
    x1res = [A.carve(128 * KB + i * 4 * KB, 2048, BF16) for i in range(2)]
    x2st = [A.carve(136 * KB + i * 4 * KB, 2048, BF16) for i in range(2)]
    banks = [A.ps0[:, 0:512], A.ps1[:, 0:512], A.psW[:, 0:512],
             A.psW[:, 512:1024]]

    te_g = {}
    ve_it = {}
    st_it = {}
    for ocg in range(8):
        par = ocg % 2
        sg = [("te", te_g[ocg - 2])] if ocg >= 2 else p.barrier_gates()
        gsems = []
        for i in range(4):
            gsems.append(("g" + str(par),
                          p.dma("gpsimd", "g" + str(par), slabs[par][i][:],
                                d["wo_blk"][ocg * 4 + i],
                                waits=(sg if i == 0 else []))))
        for blk in range(4):
            it = ocg * 4 + blk
            cslot = it % 2
            cg = [("te", te_g[("c", it - 2)])] if it >= 2 else p.barrier_gates()
            csem = ("c" + str(cslot),
                    p.dma("sync", "c" + str(cslot),
                          ctxblk[cslot][:].rearrange("p (dcc i) -> p dcc i",
                                                     i=512),
                          A.pr_ctx[:, blk * 512:(blk + 1) * 512].rearrange(
                              "(dcc p) i -> p dcc i", p=128),
                          waits=cg))
            rg = [("ve", ve_it[it - 2])] if it >= 2 else p.barrier_gates()
            rtgt = p.inc("r" + str(cslot), 64)
            wsl = [(p.sems[n], v) for n, v in rg]

            def f_rr(e, ocg=ocg, blk=blk, cslot=cslot, wsl=wsl):
                for sem, v in wsl:
                    e.wait_ge(sem, v)
                with e.register() as r:
                    e.reg_load(r, A.ids_s[0:1, 0:1])
                    for pp in (0, 1):
                        with e.If_cmp(r, pp, "IS_EQ"):
                            for oci in range(4):
                                oc = ocg * 4 + oci
                                s3 = A.sh_x1[oc * 128:(oc + 1) * 128,
                                             blk * 1024:(blk + 1) * 1024]
                                s3 = s3.rearrange("p (s two i) -> p s two i",
                                                  two=2, i=128)
                                e.dma_start(
                                    out=x1res[cslot][:, oci * 512:(oci + 1) * 512]
                                    .rearrange("p (s i) -> p s i", i=128),
                                    in_=s3[:, :, pp, :]).then_inc(
                                    p.sems["r" + str(cslot)], 16)
            p.emit("gpsimd", f_rr)
            rsem = ("r" + str(cslot), rtgt)

            te_last = None
            for i in range(4):
                def f_mm(e, i=i, cslot=cslot, par=par):
                    last = None
                    for dcc in range(DC):
                        last = e.matmul(
                            banks[i], slabs[par][i][:, dcc * 128:(dcc + 1) * 128],
                            ctxblk[cslot][:, dcc * 512:(dcc + 1) * 512],
                            start=(dcc == 0), stop=True)
                    return last
                wts = [gsems[i], csem, ("ve", p.cur("ve"))]
                te_last = p.step("tensor", "te", f_mm, waits=wts)
            te_g[("c", it)] = te_last
            if blk == 3:
                te_g[ocg] = te_last

            ve_last = None
            for i in range(4):
                wts_v = [("te", te_last), rsem]
                if it >= 2:
                    wts_v.append(st_it[it - 2])
                ve_last = p.step("vector", "ve",
                                 lambda e, i=i, cslot=cslot:
                                 e.tensor_tensor(
                                     out=x2st[cslot][:, i * 512:(i + 1) * 512],
                                     in0=banks[i],
                                     in1=x1res[cslot][:, i * 512:(i + 1) * 512],
                                     op=ALU.add),
                                 waits=wts_v)
            ve_it[it] = ve_last

            semn = "oX" + str(cslot)
            tgt0 = p.cur(semn)
            for i in range(4):
                oc = ocg * 4 + i
                t = p.inc(semn, 16)

                def wf(e, oc=oc, blk=blk, i=i, cslot=cslot, ve_last=ve_last,
                       semn=semn):
                    e.wait_ge(p.sems["ve"], ve_last)
                    with e.register() as r:
                        e.reg_load(r, A.ids_s[0:1, 0:1])
                        for pp in (0, 1):
                            with e.If_cmp(r, pp, "IS_EQ"):
                                c0 = pp * 2048 + blk * 512
                                e.dma_start(
                                    out=A.sh_x2[oc * 128:(oc + 1) * 128,
                                                c0:c0 + 512],
                                    in_=x2st[cslot][:, i * 512:(i + 1) * 512]
                                ).then_inc(p.sems[semn], 16)
                p.emit("gpsimd", wf)
            st_it[it] = (semn, p.cur(semn))

    write_flag(A, {0: 4, 1: 5},
               [("oX0", p.cur("oX0")), ("oX1", p.cur("oX1"))])


def phase4(A):
    p, d = A.p, A.d
    spin_barrier(A, [4, 5], "pb2")
    b2 = p.fence([], name="sy")

    x2bf = [A.carve(i * 8 * KB, T, BF16) for i in range(2)]
    sq = A.carve(16 * KB, T, BF16)
    mu3 = A.carve(24 * KB, T, F32, rows=1)
    r3 = A.carve(40 * KB, T, F32, rows=1)
    wrow = A.carve(56 * KB, T, F32, rows=1)
    mubq = A.carve(72 * KB, 1024, F32)
    rbbq = A.carve(76 * KB, 1024, F32)
    wrowb = A.carve(96 * KB, T, F32)
    tmpf2 = A.carve(112 * KB, T, F32)
    trow = A.carve(88 * KB, T, F32, rows=1)

    srow = [A.ps0[0:1, :], A.ps0[32:33, :], A.ps0[64:65, :],
            A.ps1[0:1, :], A.ps1[32:33, :], A.ps1[64:65, :],
            A.ps2[0:1, :], A.ps2[32:33, :]]
    qrow = [A.psW[0:1, 0:512], A.psW[32:33, 0:512], A.psW[64:65, 0:512],
            A.psW[0:1, 512:1024], A.psW[32:33, 512:1024],
            A.psW[64:65, 512:1024],
            A.psW2[0:1, 0:512], A.psW2[32:33, 0:512]]
    kx = [0]
    te_hist = {}
    for dc in range(DC):
        slot = kx[0] % 2
        kx[0] += 1
        gates = [("te", te_hist[dc - 2])] if dc >= 2 else \
            p.barrier_gates() + [("sy", b2)]
        csem = p.dma("sync" if slot == 0 else "gpsimd", "h" + str(slot),
                     x2bf[slot][:],
                     A.sh_x2[dc * 128:(dc + 1) * 128, :], waits=gates)
        se_sq = p.step("scalar", "se",
                       lambda e, slot=slot:
                       e.activation(out=sq[:], in_=x2bf[slot][:],
                                    func=AF.Square),
                       waits=[("h" + str(slot), csem), ("te", p.cur("te"))])

        def f_mm(e, slot=slot, dc=dc):
            last = None
            for b in range(8):
                sl = slice(b * 512, (b + 1) * 512)
                e.matmul(srow[b], A.ones_col[:], x2bf[slot][:, sl],
                         start=(dc == 0), stop=True)
                last = e.matmul(qrow[b], A.ones_col[:], sq[:, sl],
                                start=(dc == 0), stop=True)
            return last
        wts = [("h" + str(slot), csem), ("se", se_sq)]
        if dc == 0:
            wts.append(("ve", p.cur("ve")))
        te_hist[dc] = p.step("tensor", "te", f_mm, waits=wts)
    te_mm = p.cur("te")
    ve_rows = None
    se1_prev = None
    for b in range(8):
        sl = slice(b * 512, (b + 1) * 512)

        def f_pp(e, b=b, sl=sl):
            e.tensor_scalar_mul(out=mu3[0:1, sl], in0=srow[b], scalar1=1.0 / D)
            e.tensor_scalar_mul(out=A.rowb[:], in0=qrow[b], scalar1=1.0 / D)
            e.tensor_tensor(out=A.rowc[:], in0=mu3[0:1, sl],
                            in1=mu3[0:1, sl], op=ALU.mult)
            return e.tensor_tensor(out=A.rowb[:], in0=A.rowb[:],
                                   in1=A.rowc[:], op=ALU.subtract)
        ppw = [("te", te_mm)]
        if se1_prev is not None:
            ppw.append(("se", se1_prev))
        ve1 = p.step("vector", "ve", f_pp, waits=ppw)
        se1 = p.step("scalar", "se",
                     lambda e: e.activation(out=A.rowb[:], in_=A.rowb[:],
                                            func=AF.Sqrt, bias=A.eps_s[:]),
                     waits=[("ve", ve1)])
        se1_prev = se1
        ve_rows = p.step("vector", "ve",
                         lambda e, sl=sl: e.reciprocal(out=r3[0:1, sl],
                                                       in_=A.rowb[:]),
                         waits=[("se", se1)])

    A.mu3_d = mu3
    A.r3_d = r3
    A.wrow_d = wrow
    if _P4N <= 1:
        return

    def f_w(e):
        e.tensor_copy(out=wrow[:], in_=r3[:])
        e.memset(wrow[0:1, 0:355], 0.0)
        e.memset(wrow[0:1, 2048:2304], 0.0)
        return e.memset(wrow[0:1, 4095:4096], 0.0)
    ve_w = p.step("vector", "ve", f_w, waits=[("ve", ve_rows)])

    def f_s1(e):
        e.tensor_tensor(out=trow[:], in0=mu3[:], in1=wrow[:], op=ALU.mult)
        return e.tensor_reduce(out=A.sc1[:], in_=trow[:], axis=AX.X,
                               op=ALU.add)
    ve_s1 = p.step("vector", "ve", f_s1, waits=[("ve", ve_w)])

    if _P4N <= 2:
        return
    se_wb = None
    for b in range(8):
        sl = slice(b * 512, (b + 1) * 512)
        te_b = p.step("tensor", "te",
                      lambda e, sl=sl:
                      e.matmul(A.psW2[:, 0:512], A.ones_row[:], wrow[0:1, sl],
                               start=True, stop=True),
                      waits=[("ve", ve_w), ("se", p.cur("se"))])
        se_wb = p.step("scalar", "se",
                       lambda e, sl=sl: e.copy(out=wrowb[:, sl],
                                               in_=A.psW2[:, 0:512]),
                       waits=[("te", te_b)])
    ve_q2 = None
    ve_q2h = {}
    for dc in range(DC):
        slot = kx[0] % 2
        kx[0] += 1
        gates = [("ve", ve_q2h[dc - 2])] if dc >= 2 else [("se", se_wb)]
        csem = p.dma("sync" if slot == 0 else "gpsimd", "h" + str(slot),
                     x2bf[slot][:],
                     A.sh_x2[dc * 128:(dc + 1) * 128, :], waits=gates)

        def f_q2(e, slot=slot, dc=dc):
            e.tensor_tensor(out=tmpf2[:], in0=x2bf[slot][:],
                            in1=wrowb[:], op=ALU.mult)
            return e.tensor_reduce(out=A.q2col[:, dc:dc + 1],
                                   in_=tmpf2[:], axis=AX.X, op=ALU.add)
        ve_q2 = p.step("vector", "ve", f_q2,
                       waits=[("h" + str(slot), csem), ("se", se_wb)])
        ve_q2h[dc] = ve_q2

    if _P4N <= 3:
        return
    te_s1 = p.step("tensor", "te",
                   lambda e: e.matmul(A.ps2[:, 0:1], A.ones_row[:], A.sc1[:],
                                      start=True, stop=True),
                   waits=[("ve", ve_s1), ("ve", ve_q2), ("se", p.cur("se"))])

    def f_sub(e):
        e.tensor_copy(out=A.s1_col[:], in_=A.ps2[:, 0:1])
        return e.tensor_scalar(out=A.q2col[:], in0=A.q2col[:],
                               scalar1=A.s1_col[:], scalar2=None,
                               op0=ALU.subtract)
    p.step("vector", "ve", f_sub, waits=[("te", te_s1)])
    ve_bt = router_pipeline(A, A.q2col, 1.0 / NROUT)

    if _P4N <= 4:
        return
    QOFF = [0, 1024, 2048, 3072]

    def xln3(dc):
        return A.carve(96 * KB + dc * 2 * KB, 1024, BF16)

    murq = A.carve(96 * KB, 2048, F32, rows=1)
    tgt_mq = p.inc("si", 32)
    wsl_mq = [(p.sems["ve"], p.cur("ve")), (p.sems["se"], p.cur("se"))]

    def f_mq(e, wsl=wsl_mq):
        for sem, v in wsl:
            e.wait_ge(sem, v)
        with e.register() as r:
            e.reg_load(r, A.ids_s[0:1, 4:5])
            for quad in range(4):
                with e.If_cmp(r, quad, "IS_EQ"):
                    q0 = QOFF[quad]
                    e.dma_start(out=murq[0:1, 0:1024],
                                in_=mu3[0:1, q0:q0 + 1024]).then_inc(
                        p.sems["si"], 16)
                    e.dma_start(out=murq[0:1, 1024:2048],
                                in_=r3[0:1, q0:q0 + 1024]).then_inc(
                        p.sems["si"], 16)
    p.emit("sync", f_mq)

    se_bcq = None
    for b in range(2):
        sl = slice(b * 512, (b + 1) * 512)

        def f_tb(e, b=b):
            e.matmul(A.psW2[:, 0:512], A.ones_row[:],
                     murq[0:1, b * 512:(b + 1) * 512], start=True, stop=True)
            return e.matmul(A.psW2[:, 512:1024], A.ones_row[:],
                            murq[0:1, 1024 + b * 512:1024 + (b + 1) * 512],
                            start=True, stop=True)
        tgt_t = p.step("tensor", "te", f_tb,
                       waits=[("si", tgt_mq), ("ve", ve_bt),
                              ("se", p.cur("se"))])

        def f_ev(e, sl=sl):
            e.copy(out=mubq[:, sl], in_=A.psW2[:, 0:512])
            return e.copy(out=rbbq[:, sl], in_=A.psW2[:, 512:1024])
        se_bcq = p.step("scalar", "se", f_ev, waits=[("te", tgt_t)])

    def loader_q(dc, gates):
        slot = kx[0] % 2
        kx[0] += 1
        tgt = p.inc("h" + str(slot), 16)
        s = p.sems["h" + str(slot)]
        wsl = [(p.sems[n], v) for n, v in gates]

        def f(e, dc=dc, slot=slot, s=s, wsl=wsl):
            for sem, v in wsl:
                e.wait_ge(sem, v)
            with e.register() as r:
                e.reg_load(r, A.ids_s[0:1, 4:5])
                for quad in range(4):
                    with e.If_cmp(r, quad, "IS_EQ"):
                        e.dma_start(out=x2bf[slot][:, 0:1024],
                                    in_=A.sh_x2[dc * 128:(dc + 1) * 128,
                                                QOFF[quad]:QOFF[quad] + 1024]
                                    ).then_inc(s, 16)
        p.emit("sync", f)
        return x2bf[slot][:, 0:1024], ("h" + str(slot), tgt)

    A.mubq_d = mubq
    A.rbbq_d = rbbq
    if _P4N <= 5:
        return
    ve_n3 = ln_normalize(A, 1024, loader_q, mubq, rbbq, xln3, 80 * KB, se_bcq)
    A.xln3_0 = xln3(0)
    if _P4N <= 6:
        return

    se_t2, t2q = emit_t2(A, xln3, 1024, 88 * KB, extra_gates=[("ve", ve_n3)])
    A.t2q_d = t2q
    if _P4N <= 7:
        return

    A.SLAB_OFF = 24 * KB
    A.OST_OFF = 40 * KB
    A.OST_DT = F32

    def resid_loader(oc, gates):
        slot = kx[0] % 2
        kx[0] += 1
        tgt = p.inc("h" + str(slot), 16)
        s = p.sems["h" + str(slot)]
        wsl = [(p.sems[n], v) for n, v in gates]

        def f(e, oc=oc, slot=slot, s=s, wsl=wsl):
            for sem, v in wsl:
                e.wait_ge(sem, v)
            with e.register() as r:
                e.reg_load(r, A.ids_s[0:1, 4:5])
                for quad in range(4):
                    with e.If_cmp(r, quad, "IS_EQ"):
                        e.dma_start(out=x2bf[slot][:, 0:1024],
                                    in_=A.sh_x2[oc * 128:(oc + 1) * 128,
                                                QOFF[quad]:QOFF[quad] + 1024]
                                    ).then_inc(s, 16)
        p.emit("sync", f)
        return x2bf[slot][:, 0:1024], ("h" + str(slot), tgt)

    def out_writer(oc, blk, src, vsem):
        semn = "oF" + str((oc * 2 + blk) % 2)
        tgt = p.dma("sync", semn,
                    A.out[oc * 128:(oc + 1) * 128, blk * 512:(blk + 1) * 512],
                    src[:], waits=[vsem])
        return (semn, tgt)

    ffn_base(A, lambda oc: d["wt_blk"][oc], xln3, 1024, t2q, resid_loader,
             out_writer, first_gates=[("se", se_t2)])
    A.out_done = [("oF0", p.cur("oF0")), ("oF1", p.cur("oF1"))]


def finish(A):
    p = A.p
    nc = A.nc
    for name in A._dbg_pending:
        s = getattr(A, name)
        sap = s if isinstance(s, bass.AP) else s[:]
        A.dbg[name] = (nc.dram_tensor("dbg_" + name, list(sap.shape),
                                      sap.dtype, kind="ExternalOutput").ap(),
                       sap)
    waitall = [(n, p.cur(n)) for n in list(p.ctr) if not n.startswith("pb")]
    ws0 = [(p.sem(n), v) for n, v in waitall]

    def f0(e, ws=ws0):
        for sem, v in ws:
            e.wait_ge(sem, v)
    p.emit("sync", f0)
    for name, (dst, src) in A.dbg.items():
        p.dma("sync", "so", dst, src)
    tot = p.cur("so")

    def f1(e, tot=tot):
        e.wait_ge(p.sems["so"], tot)
    p.emit("sync", f1)


def _prep_host(inputs):
    x = np.asarray(inputs["L0_attn_x"], np.float32)
    Wq = np.asarray(inputs["L0_attn_Wq"], np.float32)
    Wk = np.asarray(inputs["L0_attn_Wk"], np.float32)
    Wv = np.asarray(inputs["L0_attn_Wv"], np.float32)
    Wo = np.asarray(inputs["L0_attn_Wo"], np.float32)
    W = np.asarray(inputs["L0_ffn_W"], np.float32)
    bb = np.asarray(inputs["L0_ffn_b"], np.float32)
    Am = np.asarray(inputs["L0_ffn_A"], np.float32)
    Bm = np.asarray(inputs["L0_ffn_B"], np.float32)
    rW = np.asarray(inputs["L0_ffn_router_W"], np.float32)
    rb = np.asarray(inputs["L0_ffn_router_b"], np.float32)

    tri = np.ascontiguousarray(np.tril(np.ones((128, 128), np.float32)).T)
    shared = dict(
        wq_blk=_bk3(Wq), wo_blk=_bk3(Wo),
        wt_blk=_bk3(np.ascontiguousarray(W.T)),
        wk_blk=_bk3(Wk), wv_blk=_bk3(Wv),
        at_blk=np.ascontiguousarray(
            Am.reshape(128, D).T.reshape(DC, 128, 128).transpose(1, 0, 2)
        ).reshape(128, DC * 128),
        bmat=np.ascontiguousarray(
            np.concatenate([Bm[e].T for e in range(8)], axis=0)),
        rwT_blk=np.ascontiguousarray(
            rW.T.reshape(DC, 128, 8).transpose(1, 0, 2)).reshape(128, DC * 8),
        biasw=np.ascontiguousarray(bb.reshape(DC, 128).T),
        rbias=rb.reshape(8, 1).astype(np.float32),
        id128=np.eye(128, dtype=np.float32),
        e16=np.repeat(np.eye(8, dtype=np.float32) * 2.0, 16, axis=1),
        iot=np.arange(64, dtype=np.float32).reshape(1, 64),
    )
    ones = np.ones((128, 128), np.float32)
    zeros = np.zeros((128, 128), np.float32)
    nonce = np.array([[np.random.randint(1 << 20, 1 << 30)]], np.int32)

    in_maps = []
    for c in range(8):
        pp, ph, b = c % 2, (c // 2) % 2, c // 4
        own = slice(2048 * pp, 2048 * (pp + 1))
        posq = np.concatenate(
            [np.arange(128 * (2 * m + pp), 128 * (2 * m + pp) + 128)
             for m in range(NSLOT)])
        m = dict(shared)
        m["xT"] = np.ascontiguousarray(x[b, own, :].T)
        m["posr"] = np.arange(2048 * pp, 2048 * (pp + 1),
                              dtype=np.float32).reshape(1, OWN)
        m["posq"] = posq.astype(np.float32).reshape(1, QSH)
        m["maskA"] = tri if pp == 0 else ones
        m["maskB"] = zeros if pp == 0 else tri
        m["ids"] = np.array([[pp, ph, b, c, 2 * pp + ph, 0, 0, 0]], np.int32)
        m["nonce"] = nonce
        in_maps.append(m)
    return in_maps


def kernel(**inputs):
    global _LAST_DEBUG
    in_maps = _prep_host(inputs)
    nc = build()
    res = run_bass_kernel_spmd(nc, in_maps, core_ids=list(range(8)))
    _LAST_DEBUG = res.results

    outf = np.zeros((B, T, D), np.float32)
    for c in range(8):
        pp, ph, b = c % 2, (c // 2) % 2, c // 4
        o = np.asarray(res.results[c]["out"])
        for i, mslot in enumerate(range(8 * ph, 8 * ph + 8)):
            j = 2 * mslot + pp
            outf[b, 128 * j:128 * (j + 1), :] = o[:, 128 * i:128 * (i + 1)].T
    return outf


# version 05 (final)
# speedup vs baseline: 2.4175x; 2.4175x over previous
import math
import numpy as np
import ml_dtypes
import concourse.bass as bass
import concourse.mybir as mybir
from concourse.bass_utils import run_bass_kernel_spmd
from contextlib import ExitStack

F32 = mybir.dt.float32
BF16 = mybir.dt.bfloat16
I32 = mybir.dt.int32
U8 = mybir.dt.uint8
AF = mybir.ActivationFunctionType
ALU = mybir.AluOpType
AX = mybir.AxisListType

B, T, D = 2, 4096, 4096
NQH, NKV, HD = 32, 8, 128
DC = 32
NSLOT = 16
OWN = 2048
QSH = 2048
Q0T, Q1T = 611, 4095
NROUT = Q1T - Q0T
EPS = 1e-5
ISQ = 1.0 / math.sqrt(HD)
MAGIC = 1234567
LN_RB = -math.log(500000.0) / 64.0
TWO_PI = 2.0 * math.pi
PI = math.pi
KB = 1024

ENGS = ("sync", "tensor", "scalar", "vector", "gpsimd")

_DEBUG = []
_LAST_DEBUG = None
_PHASES = (1, 2, 3, 4)
_ATT1 = False
_P4N = 99


class Prog:
    def __init__(self, nc, ctx):
        self.nc, self.ctx = nc, ctx
        self.th = {e: [] for e in ENGS}
        self.sems, self.ctr = {}, {}

    def sem(self, name):
        if name not in self.sems:
            self.sems[name] = self.ctx.enter_context(self.nc.semaphore(name))
            self.ctr[name] = 0
        return self.sems[name]

    def inc(self, name, k=1):
        self.sem(name)
        self.ctr[name] += k
        return self.ctr[name]

    def cur(self, name):
        self.sem(name)
        return self.ctr[name]

    def barrier_gates(self):
        return [("te", self.cur("te")), ("ve", self.cur("ve")),
                ("se", self.cur("se"))]

    def step(self, eng, sem_name, fn, waits=()):
        ws = [(self.sem(n), v) for n, v in waits]
        tgt = self.inc(sem_name, 1)
        s = self.sem(sem_name)

        def f(e, fn=fn, ws=ws, s=s):
            for sem, v in ws:
                e.wait_ge(sem, v)
            fn(e).then_inc(s, 1)
        self.emit(eng, f)
        return tgt

    def emit(self, eng, fn):
        self.th[eng].append(fn)

    def dma(self, eng, sem_name, out, in_, waits=(), **kw):
        ws = [(self.sem(n), v) for n, v in waits]
        tgt = self.inc(sem_name, 16)
        s = self.sem(sem_name)

        def f(e, out=out, in_=in_, s=s, ws=ws, kw=kw):
            for sem, v in ws:
                e.wait_ge(sem, v)
            e.dma_start(out=out, in_=in_, **kw).then_inc(s, 16)
        self.emit(eng, f)
        return tgt

    def dma_ifsel(self, eng, sem_name, mk_oi, nvar, waits=(), idword=0):
        ws = [(self.sem(n), v) for n, v in waits]
        tgt = self.inc(sem_name, 16)
        s = self.sem(sem_name)
        A = self.A
        pairs = [mk_oi(v) for v in range(nvar)]

        def f(e, pairs=pairs, ws=ws, s=s):
            for sem, v in ws:
                e.wait_ge(sem, v)
            with e.register() as r:
                e.reg_load(r, A.ids_s[0:1, idword:idword + 1])
                for v, (out, in_) in enumerate(pairs):
                    with e.If_cmp(r, v, "IS_EQ"):
                        e.dma_start(out=out, in_=in_).then_inc(s, 16)
        self.emit(eng, f)
        return tgt

    def fence(self, waits, name="sy"):
        ws = [(self.sem(n), v) for n, v in waits]
        tgt = self.inc(name, 1)
        s = self.sem(name)

        def f(e, ws=ws, s=s):
            for sem, v in ws:
                e.wait_ge(sem, v)
            e.nop().then_inc(s, 1)
        self.emit("sync", f)
        return tgt


def _bk3(w):
    Dd, Oo = w.shape
    return np.ascontiguousarray(
        w.reshape(Dd // 128, 128, Oo // 128, 128).transpose(2, 1, 0, 3)
    ).reshape(Oo // 128, 128, Dd)


def build():
    nc = bass.Bass("TRN2", target_bir_lowering=False)
    ctx = ExitStack()
    p = Prog(nc, ctx)
    A = type("A", (), {})()
    A.nc, A.p = nc, p
    p.A = A

    din = {}
    for name, shape, dt in [
        ("xT", [D, OWN], F32),
        ("wq_blk", [DC, 128, DC * 128], F32),
        ("wo_blk", [DC, 128, DC * 128], F32),
        ("wt_blk", [DC, 128, DC * 128], F32),
        ("wk_blk", [NKV, 128, DC * 128], F32),
        ("wv_blk", [NKV, 128, DC * 128], F32),
        ("at_blk", [128, DC * 128], F32),
        ("bmat", [128, D], F32),
        ("rwT_blk", [128, DC * 8], F32),
        ("biasw", [128, DC], F32),
        ("rbias", [8, 1], F32),
        ("id128", [128, 128], F32),
        ("maskA", [128, 128], F32),
        ("maskB", [128, 128], F32),
        ("e16", [8, 128], F32),
        ("iot", [1, 64], F32),
        ("posr", [1, OWN], F32),
        ("posq", [1, QSH], F32),
        ("ids", [1, 8], I32),
        ("nonce", [1, 1], I32),
    ]:
        din[name] = nc.dram_tensor(name, shape, dt, kind="ExternalInput").ap()
    A.d = din

    A.out = nc.dram_tensor("out", [D, 1024], F32, kind="ExternalOutput").ap()
    Sh = dict(kind="Internal", addr_space="Shared")
    A.sh_x1 = nc.dram_tensor("sh_x1", [D, T], BF16, **Sh).ap()
    A.sh_k = nc.dram_tensor("sh_k", [1024, T], BF16, **Sh).ap()
    A.sh_v = nc.dram_tensor("sh_v", [T, 1024], BF16, **Sh).ap()
    A.sh_x2 = nc.dram_tensor("sh_x2", [D, T], BF16, **Sh).ap()
    A.sh_r1 = nc.dram_tensor("sh_r1", [2, 128, DC], F32, **Sh).ap()
    A.sh_fl = nc.dram_tensor("sh_fl", [1, 16], I32, **Sh).ap()
    A.pr_ctx = nc.dram_tensor("pr_ctx", [D, QSH], BF16, kind="Internal").ap()
    A.pr_rt = nc.dram_tensor("pr_rt", [1, 8], F32, kind="Internal").ap()

    A.dbg = {}
    A._dbg_pending = list(_DEBUG)

    sb = lambda *a: ctx.enter_context(nc.sbuf_tensor(*a))
    A.at_bf = sb("at_bf", [128, DC * 128], BF16)
    A.bt_bf = sb("bt_bf", [128, D], BF16)
    A.rw_s = sb("rw_s", [128, DC * 8], F32)
    A.bias_s = sb("bias_s", [128, DC], F32)
    A.hbias_s = sb("hbias_s", [128, DC], F32)
    A.rb8_s = sb("rb8_s", [8, 1], F32)
    A.rowb8 = sb("rowb8", [8, 1], F32)
    A.ones8f = sb("ones8f", [8, 1], F32)
    A.id_bf = sb("id_bf", [128, 128], BF16)
    A.mA_s = sb("mA_s", [128, 128], BF16)
    A.mB_s = sb("mB_s", [128, 128], BF16)
    A.ones_col = sb("ones_col", [128, 1], BF16)
    A.ones_row = sb("ones_row", [1, 128], F32)
    A.e16_s = sb("e16_s", [8, 128], F32)
    A.eps_s = sb("eps_s", [1, 1], F32)
    A.magic_s = sb("magic_s", [1, 1], I32)
    A.ids_s = sb("ids_s", [1, 8], I32)
    A.iot_s = sb("iot_s", [1, 64], F32)
    A.inv_s = sb("inv_s", [1, 64], F32)
    A.fl_s = sb("fl_s", [1, 16], I32)
    A.rt_row = sb("rt_row", [1, 8], F32)
    A.rt_col8 = sb("rt_col8", [8, 1], F32)
    A.rt_col = sb("rt_col", [128, 1], F32)
    A.s1_col = sb("s1_col", [128, 1], F32)
    A.r1part = sb("r1part", [128, DC], F32)
    A.r1mate = sb("r1mate", [128, DC], F32)
    A.q2col = sb("q2col", [128, DC], F32)
    A.rowa = sb("rowa", [1, 512], F32)
    A.rowb = sb("rowb", [1, 512], F32)
    A.rowc = sb("rowc", [1, 512], F32)
    A.rrow = sb("rrow", [1, 512], F32)
    A.sc1 = sb("sc1", [1, 1], F32)
    A.sc2 = sb("sc2", [1, 1], F32)
    A.cs_cos = sb("cs_cos", [64, OWN], BF16)
    A.cs_sin = sb("cs_sin", [64, OWN], BF16)
    A.csq_cos = sb("csq_cos", [64, QSH], BF16)
    A.csq_sin = sb("csq_sin", [64, QSH], BF16)

    ARENA = 160 * KB
    A.arena = sb("arena", [128, ARENA], U8)

    def carve(off, cols, dt, rows=128):
        esz = mybir.dt.size(dt)
        assert off % 32 == 0 and off + cols * esz <= ARENA, (off, cols, esz)
        return A.arena[0:rows, off:off + cols * esz].bitcast(dt)
    A.carve = carve

    A.ps0 = ctx.enter_context(nc.psum_tensor("ps0", [128, 512], F32))
    A.ps1 = ctx.enter_context(nc.psum_tensor("ps1", [128, 512], F32))
    A.ps2 = ctx.enter_context(nc.psum_tensor("ps2", [128, 512], F32))
    A.psW = ctx.enter_context(nc.psum_tensor("psW", [128, 1024], F32))
    A.psW2 = ctx.enter_context(nc.psum_tensor("psW2", [128, 1024], F32))
    A.ps_t = ctx.enter_context(nc.psum_tensor("ps_t", [128, 512], BF16))

    phase0(A)
    if 1 in _PHASES:
        phase1(A)
    if 2 in _PHASES:
        phase2(A)
    if 3 in _PHASES:
        phase3(A)
    if 4 in _PHASES:
        phase4(A)
    finish(A)

    with nc.Block() as block:
        block.sync(lambda e: [f(e) for f in p.th["sync"]])
        block.tensor(lambda e: [f(e) for f in p.th["tensor"]])
        block.scalar(lambda e: [f(e) for f in p.th["scalar"]])
        block.vector(lambda e: [f(e) for f in p.th["vector"]])
        block.gpsimd(lambda e: [f(e) for f in p.th["gpsimd"]])
    ctx.close()
    return nc


def phase0(A):
    p, d = A.p, A.d
    for name, dst in [("rwT_blk", A.rw_s), ("biasw", A.bias_s),
                      ("rbias", A.rb8_s), ("iot", A.iot_s), ("ids", A.ids_s),
                      ("e16", A.e16_s), ("nonce", A.magic_s)]:
        p.dma("sync", "si", dst[:], d[name])
    for name, dst in [("at_blk", A.at_bf), ("id128", A.id_bf),
                      ("maskA", A.mA_s), ("maskB", A.mB_s)]:
        p.dma("gpsimd", "gw", dst[:], d[name])
    ld_small = p.fence([("si", p.cur("si")), ("gw", p.cur("gw"))])

    def f_ms(e):
        e.memset(A.ones_col[:], 1.0)
        e.memset(A.ones8f[:], 1.0)
        e.memset(A.ones_row[:], 1.0)
        e.memset(A.eps_s[:], EPS)
        return e.tensor_scalar_mul(out=A.hbias_s[:], in0=A.bias_s[:], scalar1=0.5)
    ve_ms = p.step("vector", "ve", f_ms, waits=[("sy", ld_small)])

    se_inv = p.step("scalar", "se",
                    lambda e: e.activation(out=A.inv_s[:], in_=A.iot_s[:],
                                           func=AF.Exp, scale=LN_RB),
                    waits=[("sy", ld_small)])

    pos_s = A.carve(0, 2048, F32, rows=1)
    ang = A.carve(32 * KB, 2048, F32, rows=64)
    tmp = A.carve(48 * KB, 2048, F32, rows=64)
    tif = A.carve(64 * KB, 2048, F32, rows=64)
    for posname, cdst, sdst in [("posr", A.cs_cos, A.cs_sin),
                                ("posq", A.csq_cos, A.csq_sin)]:
        ldp = p.fence([("si", p.dma("sync", "si", pos_s[:], d[posname],
                                    waits=[("se", p.cur("se"))]))])
        for blk in range(4):
            sl = slice(blk * 512, (blk + 1) * 512)
            te_a = p.step("tensor", "te",
                          lambda e, sl=sl: e.matmul(
                              A.psW[0:64, 0:512], A.inv_s[:], pos_s[0:1, sl],
                              start=True, stop=True),
                          waits=[("sy", ldp), ("se", se_inv),
                                 ("ve", p.cur("ve"))])

            def f_rr(e, sl=sl):
                e.tensor_scalar_mul(out=ang[0:64, sl], in0=A.psW[0:64, 0:512],
                                    scalar1=1.0 / TWO_PI)
                e.tensor_copy(out=tif[0:64, sl].bitcast(I32), in_=ang[0:64, sl])
                e.tensor_copy(out=tmp[0:64, sl], in_=tif[0:64, sl].bitcast(I32))
                e.tensor_tensor(out=ang[0:64, sl], in0=ang[0:64, sl],
                                in1=tmp[0:64, sl], op=ALU.subtract)
                e.tensor_scalar_mul(out=ang[0:64, sl], in0=ang[0:64, sl],
                                    scalar1=TWO_PI)
                e.tensor_scalar_add(out=tmp[0:64, sl], in0=ang[0:64, sl],
                                    scalar1=PI / 2.0)
                e.tensor_scalar(out=tif[0:64, sl], in0=tmp[0:64, sl],
                                scalar1=PI, scalar2=None, op0=ALU.is_gt)
                e.tensor_scalar_mul(out=tif[0:64, sl], in0=tif[0:64, sl],
                                    scalar1=-TWO_PI)
                return e.tensor_tensor(out=tmp[0:64, sl], in0=tmp[0:64, sl],
                                       in1=tif[0:64, sl], op=ALU.add)
            ve_rr = p.step("vector", "ve", f_rr, waits=[("te", te_a)])

            def f_sc(e, sl=sl, cdst=cdst, sdst=sdst):
                e.activation(out=sdst[0:64, sl], in_=ang[0:64, sl], func=AF.Sin)
                return e.activation(out=cdst[0:64, sl], in_=tmp[0:64, sl],
                                    func=AF.Sin)
            p.step("scalar", "se", f_sc, waits=[("ve", ve_rr)])
    A.tables_done = p.fence([("se", p.cur("se")), ("ve", ve_ms)])


def ln_block(A, ncols, loader, sq_off, mub, rbb, rows_out=None):
    p = A.p
    nblk = ncols // 512
    sq = A.carve(sq_off, ncols, BF16)
    srow = [A.ps0[0:1, :], A.ps1[0:1, :]]
    qrow = [A.ps2[0:1, :], A.psW[0:1, 0:512]]
    te_hist = {}
    for dc in range(DC):
        gates = [("te", te_hist[dc - 2])] if dc >= 2 else p.barrier_gates()
        ch, csem = loader(dc, gates)
        se_sq = p.step("scalar", "se",
                       lambda e, ch=ch: e.activation(out=sq[:, 0:ncols], in_=ch,
                                                     func=AF.Square),
                       waits=[csem, ("te", p.cur("te"))])

        def f_mm(e, ch=ch, dc=dc):
            last = None
            for b in range(nblk):
                sl = slice(b * 512, (b + 1) * 512)
                e.matmul(srow[b], A.ones_col[:], ch[:, sl],
                         start=(dc == 0), stop=True)
                last = e.matmul(qrow[b], A.ones_col[:], sq[:, sl],
                                start=(dc == 0), stop=True)
            return last
        wts = [csem, ("se", se_sq)]
        if dc == 0:
            wts.append(("ve", p.cur("ve")))
        te_hist[dc] = p.step("tensor", "te", f_mm, waits=wts)
    te_mm = p.cur("te")
    se_last = None
    te_b_prev = None
    for b in range(nblk):
        sl = slice(b * 512, (b + 1) * 512)

        def f_pp(e, b=b):
            e.tensor_scalar_mul(out=A.rowa[:], in0=srow[b], scalar1=1.0 / D)
            e.tensor_scalar_mul(out=A.rowb[:], in0=qrow[b], scalar1=1.0 / D)
            e.tensor_tensor(out=A.rowc[:], in0=A.rowa[:], in1=A.rowa[:],
                            op=ALU.mult)
            return e.tensor_tensor(out=A.rowb[:], in0=A.rowb[:], in1=A.rowc[:],
                                   op=ALU.subtract)
        ppw = [("te", te_mm)]
        if te_b_prev is not None:
            ppw.append(("te", te_b_prev))
        ve1 = p.step("vector", "ve", f_pp, waits=ppw)
        se1 = p.step("scalar", "se",
                     lambda e: e.activation(out=A.rowb[:], in_=A.rowb[:],
                                            func=AF.Sqrt, bias=A.eps_s[:]),
                     waits=[("ve", ve1)])
        ve2 = p.step("vector", "ve",
                     lambda e: e.reciprocal(out=A.rowc[:], in_=A.rowb[:]),
                     waits=[("se", se1)])
        if rows_out is not None:
            mu_row, r_row = rows_out

            def f_cp(e, sl=sl, mu_row=mu_row, r_row=r_row):
                e.tensor_copy(out=mu_row[0:1, sl], in_=A.rowa[:])
                return e.tensor_copy(out=r_row[0:1, sl], in_=A.rowc[:])
            ve2 = p.step("vector", "ve", f_cp, waits=[("ve", ve2)])

        def f_bc(e):
            e.matmul(A.psW2[:, 0:512], A.ones_row[:], A.rowa[:],
                     start=True, stop=True)
            return e.matmul(A.psW2[:, 512:1024], A.ones_row[:], A.rowc[:],
                            start=True, stop=True)
        te_b = p.step("tensor", "te", f_bc,
                      waits=[("ve", ve2), ("se", p.cur("se"))])
        te_b_prev = te_b

        def f_ev(e, sl=sl):
            e.copy(out=mub[:, sl], in_=A.psW2[:, 0:512])
            return e.copy(out=rbb[:, sl], in_=A.psW2[:, 512:1024])
        se_last = p.step("scalar", "se", f_ev, waits=[("te", te_b)])
    return se_last


def ln_normalize(A, ncols, loader, mub, rbb, xout, tmpf_off, ready_se):
    p = A.p
    tmpf = A.carve(tmpf_off, ncols, F32)
    ve_hist = {}
    ve_last = None
    for dc in range(DC):
        gates = [("ve", ve_hist[dc - 2])] if dc >= 2 else p.barrier_gates()
        ch, csem = loader(dc, gates)
        xo = xout(dc)

        def f_n(e, ch=ch, xo=xo):
            e.tensor_tensor(out=tmpf[:, 0:ncols], in0=ch, in1=mub[:, 0:ncols],
                            op=ALU.subtract)
            return e.tensor_tensor(out=xo[:, 0:ncols], in0=tmpf[:, 0:ncols],
                                   in1=rbb[:, 0:ncols], op=ALU.mult)
        wts = [csem]
        if dc == 0:
            wts += [("se", ready_se), ("sy", A.tables_done)]
        ve_last = p.step("vector", "ve", f_n, waits=wts)
        ve_hist[dc] = ve_last
    return ve_last


def router_pipeline(A, qcol, mean_scale, extra_gates=()):
    p = A.p

    def f_lg(e):
        last = None
        for dc in range(DC):
            last = e.matmul(A.ps2[0:8, 0:1], A.rw_s[:, dc * 8:(dc + 1) * 8],
                            qcol[:, dc:dc + 1], start=(dc == 0), stop=True)
        return last
    te_lg = p.step("tensor", "te", f_lg,
                   waits=[("ve", p.cur("ve")), ("se", p.cur("se"))] +
                   list(extra_gates))
    se_e = p.step("scalar", "se",
                  lambda e: e.activation(out=A.rowb8[:], in_=A.ps2[0:8, 0:1],
                                         func=AF.Exp, bias=A.rb8_s[:],
                                         scale=mean_scale),
                  waits=[("te", te_lg)])

    def f_sx(e):
        e.matmul(A.ps2[0:1, 8:9], A.rowb8[:], A.ones8f[:],
                 start=True, stop=True)
        return e.matmul(A.ps2[:, 16:17], A.e16_s[:], A.rowb8[:],
                        start=True, stop=True)
    te_sx = p.step("tensor", "te", f_sx, waits=[("se", se_e)])

    def f_rcp(e):
        e.reciprocal(out=A.sc2[:], in_=A.ps2[0:1, 8:9])
        return e.tensor_copy(out=A.sc1[:], in_=A.sc2[:])
    ve_r = p.step("vector", "ve", f_rcp, waits=[("te", te_sx)])
    te_bc = p.step("tensor", "te",
                   lambda e: e.matmul(A.ps2[:, 24:25], A.ones_row[0:1, :],
                                      A.sc2[:], start=True, stop=True),
                   waits=[("ve", ve_r)])
    bsem = p.dma("gpsimd", "gw", A.bt_bf[:], A.d["bmat"],
                 waits=p.barrier_gates())

    def f_bt(e):
        e.tensor_scalar_mul(out=A.bt_bf[:], in0=A.bt_bf[:],
                            scalar1=A.ps2[:, 16:17])
        e.tensor_scalar_mul(out=A.bt_bf[:], in0=A.bt_bf[:],
                            scalar1=A.ps2[:, 24:25])
        return e.tensor_copy(out=A.rt_col[:], in_=A.bt_bf[0:128, 0:1])
    return p.step("vector", "ve", f_bt, waits=[("te", te_bc), ("gw", bsem)])


def emit_t2(A, xout, ncols, t2_off, extra_gates=()):
    p = A.p
    t2 = A.carve(t2_off, ncols, BF16)
    se_t = None
    for blk in range(ncols // 512):
        sl = slice(blk * 512, (blk + 1) * 512)

        def f_mm(e, sl=sl):
            last = None
            for dc in range(DC):
                last = e.matmul(A.ps2[:], A.at_bf[:, dc * 128:(dc + 1) * 128],
                                xout(dc)[:, sl], start=(dc == 0), stop=True)
            return last
        wts = [("se", p.cur("se"))] + (list(extra_gates) if blk == 0 else [])
        te_t = p.step("tensor", "te", f_mm, waits=wts)
        se_t = p.step("scalar", "se",
                      lambda e, sl=sl: e.copy(out=t2[:, sl], in_=A.ps2[:]),
                      waits=[("te", te_t)])
    return se_t, t2


def ffn_base(A, slab_src, xout, ncols, t2, resid_loader, out_writer,
             first_gates=()):
    p = A.p
    nblk = ncols // 512
    slabs = [A.carve(A.SLAB_OFF + i * 8 * KB, DC * 128, BF16) for i in range(2)]
    esz = mybir.dt.size(A.OST_DT)
    osts = [A.carve(A.OST_OFF + i * 512 * esz, 512, A.OST_DT) for i in range(2)]
    te_oc = {}
    ve_hist = {}
    st_hist = {}
    for oc in range(DC):
        sg = [("te", te_oc[oc - 2])] if oc >= 2 else p.barrier_gates()
        gsem = ("g" + str(oc % 2),
                p.dma("gpsimd", "g" + str(oc % 2), slabs[oc % 2][:],
                      slab_src(oc), waits=sg))
        rg = [("ve", ve_hist[oc - 2])] if oc >= 2 else p.barrier_gates()
        rap, rsem = resid_loader(oc, rg)
        for blk in range(nblk):
            sl = slice(blk * 512, (blk + 1) * 512)
            psu = A.ps0 if blk % 2 == 0 else A.ps1

            def f_mm(e, sl=sl, psu=psu, slab=slabs[oc % 2], oc=oc):
                for dc in range(DC):
                    e.matmul(psu[:], slab[:, dc * 128:(dc + 1) * 128],
                             xout(dc)[:, sl], start=(dc == 0), stop=True)
                return e.matmul(psu[:], A.bt_bf[:, oc * 128:(oc + 1) * 128],
                                t2[:, sl], start=False, stop=True)
            wts = [gsem, ("se", p.cur("se"))]
            if oc == 0 and blk == 0:
                wts += list(first_gates)
            te_b = p.step("tensor", "te", f_mm, waits=wts)
            if blk == nblk - 1:
                te_oc[oc] = te_b
            se_b = p.step("scalar", "se",
                          lambda e, psu=psu, oc=oc:
                          e.activation(out=A.ps2[:], in_=psu[:],
                                       func=AF.Identity,
                                       bias=A.hbias_s[:, oc:oc + 1], scale=0.5),
                          waits=[("te", te_b), ("ve", p.cur("ve"))])
            it = oc * nblk + blk
            wts_v = [("se", se_b), rsem]
            if it >= 2:
                wts_v.append(st_hist[it - 2])
            ve_b = p.step("vector", "ve",
                          lambda e, rap=rap, sl=sl, ost=osts[blk % 2]:
                          e.tensor_tensor(out=ost[:], in0=A.ps2[:],
                                          in1=rap[:, sl], op=ALU.add),
                          waits=wts_v)
            ve_hist[oc] = ve_b
            st_hist[it] = out_writer(oc, blk, osts[blk % 2], ("ve", ve_b))


def spin_barrier(A, words, pbname):
    p = A.p

    p.sem(pbname)

    def f(e):
        with e.register() as tot, e.register() as cnd, e.register() as tmpr, \
             e.register() as non:
            e.reg_load(non, A.magic_s[0:1, 0:1])

            def cond():
                e.dma_start(out=A.fl_s[:], in_=A.sh_fl).then_inc(
                    p.sems[pbname], 16)
                e.nop(cycle_cnt=30000)
                e.nop(cycle_cnt=30000)
                e.reg_mov(tot, 0)
                for w in words:
                    e.reg_load(tmpr, A.fl_s[0:1, w:w + 1])
                    e.reg_alu(tmpr, tmpr, non, ALU.is_equal)
                    e.reg_alu(tot, tot, tmpr, ALU.add)
                e.reg_alu(cnd, tot, len(words), ALU.is_lt)
                return cnd
            with e.While(cond):
                e.nop()
    p.emit("sync", f)


def write_flag(A, words_by_pp, waits):
    p = A.p
    ws = [(p.sem(n), v) for n, v in waits]
    p.inc("so", 16)

    def f(e, ws=ws):
        for sem, v in ws:
            e.wait_ge(sem, v)
        with e.register() as r:
            e.reg_load(r, A.ids_s[0:1, 0:1])
            for pp, w in words_by_pp.items():
                with e.If_cmp(r, pp, "IS_EQ"):
                    e.dma_start(out=A.sh_fl[0:1, w:w + 1],
                                in_=A.magic_s[:]).then_inc(p.sems["so"], 16)
    p.emit("sync", f)
    return p.cur("so")


def phase1(A):
    p, d = A.p, A.d
    xbf = [A.carve(128 * KB + i * 4 * KB, OWN, BF16) for i in range(2)]
    mub = A.carve(136 * KB, OWN, F32)
    rbb = A.carve(144 * KB, OWN, F32)

    def xln1(dc):
        return A.carve(dc * 4 * KB, OWN, BF16)

    kx = [0]

    def mk_loader(h):
        def loader(dc, gates):
            slot = kx[0] % 2
            kx[0] += 1
            cols = OWN if h is None else 1024
            c0 = 0 if h is None else h * 1024
            sem = p.dma("gpsimd", "x" + str(slot), xbf[slot][:, 0:cols],
                        d["xT"][dc * 128:(dc + 1) * 128, c0:c0 + cols],
                        waits=gates)
            return xbf[slot][:, 0:cols], ("x" + str(slot), sem)
        return loader

    se_bc = None
    for h in range(2):
        se_bc = ln_block(A, 1024, mk_loader(h), 152 * KB,
                         mub[:, h * 1024:(h + 1) * 1024],
                         rbb[:, h * 1024:(h + 1) * 1024])
    ve_n = ln_normalize(A, OWN, mk_loader(None), mub, rbb, xln1, 152 * KB,
                        se_bc)

    def f_r1(e):
        with e.register() as r:
            e.reg_load(r, A.ids_s[0:1, 0:1])
            with e.If_cmp(r, 0, "IS_EQ"):
                for dc in range(DC):
                    e.tensor_reduce(out=A.r1part[:, dc:dc + 1],
                                    in_=xln1(dc)[:, Q0T:2048], axis=AX.X,
                                    op=ALU.add)
            with e.If_cmp(r, 1, "IS_EQ"):
                for dc in range(DC):
                    e.tensor_reduce(out=A.r1part[:, dc:dc + 1],
                                    in_=xln1(dc)[:, 0:Q1T - 2048], axis=AX.X,
                                    op=ALU.add)
        return e.tensor_copy(out=A.rowc[0:1, 0:1], in_=A.r1part[0:1, 0:1])
    ve_r1 = p.step("vector", "ve", f_r1, waits=[("ve", ve_n)])

    tgt_r1 = p.inc("so", 16)

    def f_wr1(e):
        e.wait_ge(p.sems["ve"], ve_r1)
        with e.register() as r:
            e.reg_load(r, A.ids_s[0:1, 0:1])
            for pp in (0, 1):
                with e.If_cmp(r, pp, "IS_EQ"):
                    e.dma_start(out=A.sh_r1[pp],
                                in_=A.r1part[:]).then_inc(p.sems["so"], 16)
    p.emit("sync", f_wr1)
    write_flag(A, {0: 0, 1: 1}, [("so", tgt_r1)])

    se_t2, t2 = emit_t2(A, xln1, OWN, 136 * KB, extra_gates=[("ve", ve_n)])

    spin_barrier(A, [0, 1], "pb0")
    tgt_m = p.inc("si", 16)

    def f_rdm(e):
        with e.register() as r:
            e.reg_load(r, A.ids_s[0:1, 0:1])
            for pp in (0, 1):
                with e.If_cmp(r, pp, "IS_EQ"):
                    e.dma_start(out=A.r1mate[:],
                                in_=A.sh_r1[1 - pp]).then_inc(p.sems["si"], 16)
    p.emit("sync", f_rdm)
    p.step("vector", "ve",
           lambda e: e.tensor_tensor(out=A.q2col[:], in0=A.r1part[:],
                                     in1=A.r1mate[:], op=ALU.add),
           waits=[("si", tgt_m)])
    ve_bt = router_pipeline(A, A.q2col, 1.0 / NROUT,
                            extra_gates=[("se", se_t2)])

    A.SLAB_OFF = 140 * KB
    A.OST_OFF = 156 * KB
    A.OST_DT = BF16

    def resid_loader(oc, gates):
        slot = kx[0] % 2
        kx[0] += 1
        sem = p.dma("gpsimd", "x" + str(slot), xbf[slot][:],
                    d["xT"][oc * 128:(oc + 1) * 128, :], waits=gates)
        return xbf[slot], ("x" + str(slot), sem)

    def out_writer(oc, blk, src, vsem):
        semn = "oA" + str((oc * 4 + blk) % 2)
        tgt = p.inc(semn, 16)

        def f(e, oc=oc, blk=blk, src=src, vsem=vsem, semn=semn):
            e.wait_ge(p.sems[vsem[0]], vsem[1])
            with e.register() as r:
                e.reg_load(r, A.ids_s[0:1, 0:1])
                for pp in (0, 1):
                    with e.If_cmp(r, pp, "IS_EQ"):
                        c0 = pp * 2048 + blk * 512
                        e.dma_start(
                            out=A.sh_x1[oc * 128:(oc + 1) * 128, c0:c0 + 512],
                            in_=src[:]).then_inc(p.sems[semn], 16)
        p.emit("sync", f)
        return (semn, tgt)

    ffn_base(A, lambda oc: d["wt_blk"][oc], xln1, OWN, t2, resid_loader,
             out_writer, first_gates=[("ve", ve_bt)])
    A.x1_done = [("oA0", p.cur("oA0")), ("oA1", p.cur("oA1"))]


def phase2(A):
    p, d = A.p, A.d
    x1bf = [A.carve(128 * KB + i * 4 * KB, OWN, BF16) for i in range(2)]
    mub = A.carve(136 * KB, OWN, F32)
    rbb = A.carve(144 * KB, OWN, F32)

    def xln2o(dc):
        return A.carve(dc * 4 * KB, OWN, BF16)

    kx = [0]

    def mk_loader(h):
        def loader(dc, gates):
            slot = kx[0] % 2
            kx[0] += 1
            cols = OWN if h is None else 1024
            tgt = p.inc("h" + str(slot), 16)
            s = p.sems["h" + str(slot)]
            wsl = [(p.sems[n], v) for n, v in gates] + \
                  [(p.sems[n], v) for n, v in A.x1_done]

            def f(e, dc=dc, slot=slot, h=h, s=s, wsl=wsl, cols=cols):
                for sem, v in wsl:
                    e.wait_ge(sem, v)
                with e.register() as r:
                    e.reg_load(r, A.ids_s[0:1, 0:1])
                    for pp in (0, 1):
                        with e.If_cmp(r, pp, "IS_EQ"):
                            c0 = pp * 2048 + (0 if h is None else h * 1024)
                            e.dma_start(out=x1bf[slot][:, 0:cols],
                                        in_=A.sh_x1[dc * 128:(dc + 1) * 128,
                                                    c0:c0 + cols]).then_inc(s, 16)
            p.emit("sync", f)
            return x1bf[slot][:, 0:cols], ("h" + str(slot), tgt)
        return loader

    se_bc = None
    for h in range(2):
        se_bc = ln_block(A, 1024, mk_loader(h), 152 * KB,
                         mub[:, h * 1024:(h + 1) * 1024],
                         rbb[:, h * 1024:(h + 1) * 1024])
    ve_n = ln_normalize(A, OWN, mk_loader(None), mub, rbb, xln2o, 152 * KB,
                        se_bc)

    kslab = [A.carve(136 * KB + i * 8 * KB, DC * 128, BF16) for i in range(2)]
    rs1 = A.carve(152 * KB, 512, F32)
    rs2 = A.carve(154 * KB, 512, F32)
    kvst = [A.carve(156 * KB + i * KB, 512, BF16) for i in range(2)]
    vtst = [A.carve(158 * KB + i * 256, 128, BF16) for i in range(2)]
    te_oc = {}
    st_hist = {}

    for oc in range(NKV):
        sg = [("te", te_oc[oc - 2])] if oc >= 2 else \
            [("ve", ve_n), ("te", p.cur("te"))]
        gsem = p.dma("gpsimd", "g" + str(oc % 2), kslab[oc % 2][:],
                     d["wk_blk"][oc], waits=sg)
        for blk in range(4):
            sl = slice(blk * 512, (blk + 1) * 512)
            psu = A.ps0 if blk % 2 == 0 else A.ps1

            def f_mm(e, sl=sl, psu=psu, slab=kslab[oc % 2]):
                last = None
                for dc in range(DC):
                    last = e.matmul(psu[:], slab[:, dc * 128:(dc + 1) * 128],
                                    xln2o(dc)[:, sl], start=(dc == 0), stop=True)
                return last
            te_b = p.step("tensor", "te", f_mm,
                          waits=[("g" + str(oc % 2), gsem),
                                 ("ve", p.cur("ve"))])
            if blk == 3:
                te_oc[oc] = te_b
            it = oc * 4 + blk
            wts_r = [("te", te_b)]
            if it >= 2:
                wts_r.append(st_hist[it - 2])

            def f_rope(e, sl=sl, psu=psu, st=kvst[blk % 2]):
                e.tensor_tensor(out=rs1[0:64, :], in0=psu[0:64, :],
                                in1=A.cs_cos[:, sl], op=ALU.mult)
                e.tensor_tensor(out=rs2[0:64, :], in0=psu[64:128, :],
                                in1=A.cs_sin[:, sl], op=ALU.mult)
                e.tensor_tensor(out=st[0:64, :], in0=rs1[0:64, :], in1=rs2[0:64, :],
                                op=ALU.subtract)
                e.tensor_tensor(out=rs1[0:64, :], in0=psu[0:64, :],
                                in1=A.cs_sin[:, sl], op=ALU.mult)
                e.tensor_tensor(out=rs2[0:64, :], in0=psu[64:128, :],
                                in1=A.cs_cos[:, sl], op=ALU.mult)
                return e.tensor_tensor(out=st[64:128, :], in0=rs1[0:64, :],
                                       in1=rs2[0:64, :], op=ALU.add)
            ve_r = p.step("vector", "ve", f_rope, waits=wts_r)
            semn = "oK" + str(blk % 2)
            tgt = p.inc(semn, 16)

            def wf(e, oc=oc, blk=blk, st=kvst[blk % 2], ve_r=ve_r, semn=semn):
                e.wait_ge(p.sems["ve"], ve_r)
                with e.register() as r:
                    e.reg_load(r, A.ids_s[0:1, 0:1])
                    for pp in (0, 1):
                        with e.If_cmp(r, pp, "IS_EQ"):
                            c0 = pp * 2048 + blk * 512
                            e.dma_start(
                                out=A.sh_k[oc * 128:(oc + 1) * 128, c0:c0 + 512],
                                in_=st[:]).then_inc(p.sems[semn], 16)
            p.emit("sync", wf)
            st_hist[it] = (semn, tgt)

    stv_hist = {}
    for oc in range(NKV):
        koc = NKV + oc
        sg = [("te", te_oc[koc - 2])]
        gsem = p.dma("gpsimd", "g" + str(koc % 2), kslab[koc % 2][:],
                     d["wv_blk"][oc], waits=sg)
        for blk in range(4):
            sl = slice(blk * 512, (blk + 1) * 512)
            psu = A.ps0 if blk % 2 == 0 else A.ps1

            def f_mm(e, sl=sl, psu=psu, slab=kslab[koc % 2]):
                last = None
                for dc in range(DC):
                    last = e.matmul(psu[:], slab[:, dc * 128:(dc + 1) * 128],
                                    xln2o(dc)[:, sl], start=(dc == 0), stop=True)
                return last
            te_b = p.step("tensor", "te", f_mm,
                          waits=[("g" + str(koc % 2), gsem),
                                 ("se", p.cur("se"))])
            if blk == 3:
                te_oc[koc] = te_b
            it = oc * 4 + blk
            wts_s = [("te", te_b), ("ve", p.cur("ve"))]
            if it >= 2:
                wts_s.append(st_hist[it - 2])
            else:
                wts_s += [("oK0", p.cur("oK0")), ("oK1", p.cur("oK1"))]
            se_v = p.step("scalar", "se",
                          lambda e, psu=psu, st=kvst[blk % 2]:
                          e.copy(out=st[:], in_=psu[:]),
                          waits=wts_s)
            st_hist[it] = ("se", se_v)
            for tt in range(4):
                itv = it * 4 + tt
                wts_t = [("se", se_v), ("ve", p.cur("ve"))]
                te_t = p.step("tensor", "te",
                              lambda e, st=kvst[blk % 2], tt=tt:
                              e.transpose(A.ps_t[:, 0:128],
                                          st[:, tt * 128:(tt + 1) * 128],
                                          A.id_bf[:]),
                              waits=wts_t)
                wts_c = [("te", te_t)]
                if itv >= 2:
                    wts_c.append(stv_hist[itv - 2])
                ve_t = p.step("vector", "ve",
                              lambda e, tt=tt:
                              e.tensor_copy(out=vtst[tt % 2][:],
                                            in_=A.ps_t[:, 0:128]),
                              waits=wts_c)
                semn = "oV" + str(tt % 2)
                tgt = p.inc(semn, 16)

                def wf(e, oc=oc, blk=blk, tt=tt, ve_t=ve_t,
                       st=vtst[tt % 2], semn=semn):
                    e.wait_ge(p.sems["ve"], ve_t)
                    with e.register() as r:
                        e.reg_load(r, A.ids_s[0:1, 0:1])
                        for pp in (0, 1):
                            with e.If_cmp(r, pp, "IS_EQ"):
                                r0 = pp * 2048 + blk * 512 + tt * 128
                                e.dma_start(
                                    out=A.sh_v[r0:r0 + 128,
                                               oc * 128:(oc + 1) * 128],
                                    in_=st[:]).then_inc(p.sems[semn], 16)
                p.emit("sync", wf)
                stv_hist[itv] = (semn, tgt)

    fl_waits = [("oK0", p.cur("oK0")), ("oK1", p.cur("oK1")),
                ("oV0", p.cur("oV0")), ("oV1", p.cur("oV1"))] + A.x1_done
    write_flag(A, {0: 2, 1: 3}, fl_waits)


def phase3(A):
    p, d = A.p, A.d
    spin_barrier(A, [2, 3], "pb1")
    b1 = p.fence([], name="sy")

    x1bf = [A.carve(64 * KB + i * 2 * KB, 1024, BF16) for i in range(2)]
    mub = A.carve(72 * KB, 1024, F32)
    rbb = A.carve(76 * KB, 1024, F32)
    wqs = [A.carve(80 * KB + i * 8 * KB, DC * 128, BF16) for i in range(2)]
    khs = [A.carve(96 * KB + i * 8 * KB, 4096, BF16) for i in range(2)]
    vhs = [A.carve(112 * KB + i * 8 * KB, 4096, BF16) for i in range(2)]
    qss = [A.carve(128 * KB + i * 2 * KB, 1024, BF16) for i in range(2)]
    pbf = [A.carve(132 * KB + i * 2 * KB, 1024, BF16) for i in range(2)]
    rs1 = A.carve(136 * KB, 512, F32)
    rs2 = A.carve(138 * KB, 512, F32)
    ctxf = A.carve(140 * KB, 512, F32)
    ctxb = [A.carve(142 * KB + i * KB, 512, BF16) for i in range(2)]

    def xln2q(dc):
        return A.carve(dc * 2 * KB, 1024, BF16)

    kx = [0]
    last_exp_psX = {0: None, 1: None}
    last_av_pbf = {0: None, 1: None}
    last_block_end = {"ve_rc": None, "se_cx": None, "ve_cx": None}
    ctxst = {}
    nst = [0]

    for qh_half in ([0] if _ATT1 else range(2)):
        kmax = 16 if qh_half == 0 else 32

        def loader(dc, gates):
            slot = kx[0] % 2
            kx[0] += 1
            tgt = p.inc("h" + str(slot), 16)
            s = p.sems["h" + str(slot)]
            wsl = [(p.sems[n], v) for n, v in gates]

            def f(e, dc=dc, slot=slot, s=s, wsl=wsl, qh_half=qh_half):
                for sem, v in wsl:
                    e.wait_ge(sem, v)
                e.wait_ge(p.sems["sy"], b1)
                src = A.sh_x1[dc * 128:(dc + 1) * 128,
                              2048 * qh_half:2048 * (qh_half + 1)]
                src = src.rearrange("p (s two i) -> p s two i", two=2, i=128)
                with e.register() as r:
                    e.reg_load(r, A.ids_s[0:1, 0:1])
                    for pp in (0, 1):
                        with e.If_cmp(r, pp, "IS_EQ"):
                            e.dma_start(
                                out=x1bf[slot][:].rearrange(
                                    "p (s i) -> p s i", i=128),
                                in_=src[:, :, pp, :]).then_inc(s, 16)
            p.emit("sync", f)
            return x1bf[slot], ("h" + str(slot), tgt)

        se_bc = ln_block(A, 1024, loader, 68 * KB, mub, rbb)
        ve_n = ln_normalize(A, 1024, loader, mub, rbb, xln2q, 68 * KB, se_bc)

        te_h_done = {}
        for h in ([0] if _ATT1 else range(NKV)):
            kg = [("te", te_h_done[h - 2])] if h >= 2 else \
                [("te", p.cur("te")), ("se", p.cur("se"))]
            ksem = ("K" + str(h % 2),
                    p.dma("sync", "K" + str(h % 2), khs[h % 2][:, 0:kmax * 128],
                          A.sh_k[h * 128:(h + 1) * 128, 0:kmax * 128],
                          waits=kg + [("sy", b1)]))
            vsem = ("V" + str(h % 2),
                    p.dma("sync", "V" + str(h % 2),
                          vhs[h % 2][:, 0:kmax * 128].rearrange(
                              "p (tc v) -> p tc v", v=128),
                          A.sh_v[0:kmax * 128,
                                 h * 128:(h + 1) * 128].rearrange(
                              "(tc p) v -> p tc v", p=128),
                          waits=kg + [("sy", b1)]))
            kh = khs[h % 2]
            vh = vhs[h % 2]
            for s4 in ([0] if _ATT1 else range(4)):
                qh = 4 * h + s4
                qsl = qss[qh % 2]
                wg = [("ve", p.cur("ve"))]
                wqsem = ("g" + str(qh % 2),
                         p.dma("gpsimd", "g" + str(qh % 2), wqs[qh % 2][:],
                               d["wq_blk"][qh],
                               waits=[("te", p.cur("te"))] + (
                                   [("ve", ve_n)] if qh < 2 and qh_half == 0
                                   else [])))
                ve_rope = None
                for qb in range(2):
                    sl = slice(qb * 512, (qb + 1) * 512)

                    def f_q(e, sl=sl, slab=wqs[qh % 2]):
                        last = None
                        for dc in range(DC):
                            last = e.matmul(A.ps2[:],
                                            slab[:, dc * 128:(dc + 1) * 128],
                                            xln2q(dc)[:, sl],
                                            start=(dc == 0), stop=True)
                        return last
                    te_q = p.step("tensor", "te", f_q,
                                  waits=[wqsem, ("ve", p.cur("ve"))])
                    csl = slice(qh_half * 1024 + qb * 512,
                                qh_half * 1024 + qb * 512 + 512)

                    def f_rope(e, sl=sl, csl=csl, qsl=qsl):
                        e.tensor_tensor(out=rs1[0:64, :], in0=A.ps2[0:64, :],
                                        in1=A.csq_cos[:, csl], op=ALU.mult)
                        e.tensor_tensor(out=rs2[0:64, :], in0=A.ps2[64:128, :],
                                        in1=A.csq_sin[:, csl], op=ALU.mult)
                        e.tensor_tensor(out=qsl[0:64, sl], in0=rs1[0:64, :],
                                        in1=rs2[0:64, :], op=ALU.subtract)
                        e.tensor_tensor(out=rs1[0:64, :], in0=A.ps2[0:64, :],
                                        in1=A.csq_sin[:, csl], op=ALU.mult)
                        e.tensor_tensor(out=rs2[0:64, :], in0=A.ps2[64:128, :],
                                        in1=A.csq_cos[:, csl], op=ALU.mult)
                        e.tensor_tensor(out=qsl[64:128, sl], in0=rs1[0:64, :],
                                        in1=rs2[0:64, :], op=ALU.add)
                        return e.tensor_copy(out=A.rowa[0:1, 0:1],
                                             in_=qsl[64:65, 0:1])
                    ve_rope = p.step("vector", "ve", f_rope,
                                     waits=[("te", te_q)])

                for mb in ([0] if _ATT1 else range(2)):
                    m0g = 8 * qh_half + 4 * mb
                    qcols = slice(mb * 512, (mb + 1) * 512)
                    first = [True]

                    def block_first_gates():
                        g = []
                        if last_block_end["ve_rc"] is not None:
                            g.append(("ve", last_block_end["ve_rc"]))
                        if last_block_end["se_cx"] is not None:
                            g.append(("se", last_block_end["se_cx"]))
                        return g

                    def do_tile_s(kt, par, N, qoff, masked, z):
                        psX = A.psW if par == 0 else A.psW2
                        pb_ = pbf[par]
                        wts = [("ve", ve_rope)]
                        if first[0]:
                            wts.append(ksem)
                        if last_exp_psX[par] is not None:
                            wts.append(("se", last_exp_psX[par]))
                        if N == 512 and qoff == 0 and kt + 1 <= kmax - 1 and \
                                not masked and kt % 2 == 0 and z is None:
                            def f_sa(e, kt=kt, psX=psX, kh=kh, qsl=qsl,
                                     qcols=qcols):
                                return e.matmul(psX[:, 0:512],
                                                kh[:, kt * 128:(kt + 1) * 128],
                                                qsl[:, qcols], start=True,
                                                stop=True)
                            te_sa = p.step("tensor", "te", f_sa, waits=wts)

                            def f_sb(e, kt=kt, psX=psX, kh=kh, qsl=qsl,
                                     qcols=qcols):
                                return e.matmul(
                                    psX[:, 512:1024],
                                    kh[:, (kt + 1) * 128:(kt + 2) * 128],
                                    qsl[:, qcols], start=True, stop=True)
                            te_sb = p.step("tensor", "te", f_sb, waits=[])
                            w2a = [("te", te_sa)]
                            if last_av_pbf[par] is not None:
                                w2a.append(("te", last_av_pbf[par]))
                            se_ea = p.step(
                                "scalar", "se",
                                lambda e, psX=psX, pb_=pb_:
                                e.activation(out=pb_[:, 0:512],
                                             in_=psX[:, 0:512],
                                             func=AF.Exp, scale=ISQ),
                                waits=w2a)
                            se_eb = p.step(
                                "scalar", "se",
                                lambda e, psX=psX, pb_=pb_:
                                e.activation(out=pb_[:, 512:1024],
                                             in_=psX[:, 512:1024],
                                             func=AF.Exp, scale=ISQ),
                                waits=[("te", te_sb)])
                            last_exp_psX[par] = se_eb
                            first[0] = False
                            return dict(kt=kt, par=par, wid=1024, qoff=0,
                                        dep=("se", se_ea),
                                        dep_b=("se", se_eb))
                        else:
                            def f_s(e, kt=kt, psX=psX, N=N, qoff=qoff,
                                    kh=kh, qsl=qsl, mb=mb):
                                return e.matmul(
                                    psX[:, 0:N],
                                    kh[:, kt * 128:(kt + 1) * 128],
                                    qsl[:, mb * 512 + qoff:(mb + 1) * 512],
                                    start=True, stop=True)
                            te_s = p.step("tensor", "te", f_s, waits=wts)
                            wid = N
                        wts2 = [("te", te_s)]
                        if last_av_pbf[par] is not None:
                            wts2.append(("te", last_av_pbf[par]))
                        se_e = p.step("scalar", "se",
                                      lambda e, psX=psX, pb_=pb_, wid=wid:
                                      e.activation(out=pb_[:, 0:wid],
                                                   in_=psX[:, 0:wid],
                                                   func=AF.Exp, scale=ISQ),
                                      waits=wts2)
                        last_exp_psX[par] = se_e
                        dep = ("se", se_e)
                        if masked:
                            mask = A.mA_s if z % 2 == 0 else A.mB_s

                            def f_m(e, pb_=pb_, mask=mask):
                                e.tensor_tensor(out=pb_[:, 0:128],
                                                in0=pb_[:, 0:128],
                                                in1=mask[:], op=ALU.mult)
                                return e.tensor_copy(out=A.rowa[0:1, 1:2],
                                                     in_=pb_[0:1, 0:1])
                            ve_m = p.step("vector", "ve", f_m,
                                          waits=[("se", se_e)])
                            dep = ("ve", ve_m)
                        first[0] = False
                        return dict(kt=kt, par=par, wid=wid, qoff=qoff,
                                    dep=dep)

                    def do_tile_av(u):
                        wts3 = [u["dep"]]
                        if firstav[0]:
                            wts3.append(vsem)
                            wts3 += block_first_gates()
                        if "dep_b" in u:
                            def f_ava(e, kt=u["kt"], pb_=pbf[u["par"]],
                                      vh=vh, st=firstav[0]):
                                e.matmul(A.ps1[0:1, :], A.ones_col[:],
                                         pb_[:, 0:512], start=st, stop=True)
                                return e.matmul(
                                    A.ps0[:], vh[:, kt * 128:(kt + 1) * 128],
                                    pb_[:, 0:512], start=st, stop=True)
                            p.step("tensor", "te", f_ava, waits=wts3)

                            def f_avb(e, kt=u["kt"], pb_=pbf[u["par"]],
                                      vh=vh):
                                e.matmul(A.ps1[0:1, :], A.ones_col[:],
                                         pb_[:, 512:1024], start=False,
                                         stop=True)
                                return e.matmul(
                                    A.ps0[:],
                                    vh[:, (u["kt"] + 1) * 128:(u["kt"] + 2) * 128],
                                    pb_[:, 512:1024], start=False, stop=True)
                            te_av = p.step("tensor", "te", f_avb,
                                           waits=[u["dep_b"]])
                        else:
                            def f_av(e, kt=u["kt"], pb_=pbf[u["par"]],
                                     wid=u["wid"], qoff=u["qoff"],
                                     vh=vh, st=firstav[0]):
                                e.matmul(A.ps1[0:1, qoff:512], A.ones_col[:],
                                         pb_[:, 0:wid], start=st, stop=True)
                                return e.matmul(A.ps0[:, qoff:512],
                                                vh[:, kt * 128:(kt + 1) * 128],
                                                pb_[:, 0:wid], start=st,
                                                stop=True)
                            te_av = p.step("tensor", "te", f_av, waits=wts3)
                        last_av_pbf[u["par"]] = te_av
                        firstav[0] = False
                        return te_av

                    par = 0
                    te_av = None
                    pend = None
                    firstav = [True]
                    for g in range(m0g):
                        u = do_tile_s(2 * g, par, 512, 0, False, None)
                        if pend is not None:
                            te_av = do_tile_av(pend)
                        pend = u
                        par ^= 1
                    for z in range(8):
                        kt = 2 * m0g + z
                        qoff = (z // 2) * 128
                        u = do_tile_s(kt, par, 512 - qoff, qoff, True, z)
                        if pend is not None:
                            te_av = do_tile_av(pend)
                        pend = u
                        par ^= 1
                    te_av = do_tile_av(pend)
                    te_h_done[h] = te_av

                    def f_rc(e):
                        e.reciprocal(out=A.rrow[:], in_=A.ps1[0:1, :])
                        return e.tensor_copy(out=A.rowa[0:1, 2:3],
                                             in_=A.rrow[0:1, 511:512])
                    ve_rc = p.step("vector", "ve", f_rc,
                                   waits=[("te", te_av)])
                    te_bc = p.step("tensor", "te",
                                   lambda e: e.matmul(A.ps2[:], A.ones_row[:],
                                                      A.rrow[:], start=True,
                                                      stop=True),
                                   waits=[("ve", ve_rc)])
                    se_cx = p.step("scalar", "se",
                                   lambda e: e.copy(out=ctxf[:], in_=A.ps0[:]),
                                   waits=[("te", te_bc)])
                    cslot = nst[0] % 2
                    wts_cx = [("se", se_cx)]
                    if nst[0] >= 2:
                        wts_cx.append(ctxst[nst[0] - 2])
                    def f_cx(e, cb=ctxb[cslot]):
                        e.tensor_tensor(out=cb[:], in0=ctxf[:],
                                        in1=A.ps2[:], op=ALU.mult)
                        return e.tensor_copy(out=A.rowa[0:1, 3:4],
                                             in_=cb[0:1, 511:512])
                    ve_cx = p.step("vector", "ve", f_cx, waits=wts_cx)
                    semn = "oC" + str(cslot)
                    tgt = p.dma("sync", semn,
                                A.pr_ctx[qh * 128:(qh + 1) * 128,
                                         qh_half * 1024 + mb * 512:
                                         qh_half * 1024 + mb * 512 + 512],
                                ctxb[cslot][:], waits=[("ve", ve_cx)])
                    ctxst[nst[0]] = (semn, tgt)
                    nst[0] += 1
                    last_block_end["ve_rc"] = ve_rc
                    last_block_end["se_cx"] = se_cx

    if _ATT1:
        A.qss0 = qss[0]
        A.khs0 = khs[0]
        A.vhs0 = vhs[0]
        A.ctxf0 = ctxf
        A.rrow0 = A.rrow
        A.ctxb0 = ctxb[0]
        A.pbf0 = pbf[0]
        A.pbf1 = pbf[1]
        A.xq0 = xln2q(0)
        return
    phase3_wo(A)


def phase3_wo(A):
    p, d = A.p, A.d
    slabs = [[A.carve((g * 32 + i * 8) * KB, DC * 128, BF16) for i in range(4)]
             for g in range(2)]
    ctxblk = [A.carve(64 * KB + i * 32 * KB, 32 * 512, BF16) for i in range(2)]
    x1res = [A.carve(128 * KB + i * 4 * KB, 2048, BF16) for i in range(2)]
    x2st = [A.carve(136 * KB + i * 4 * KB, 2048, BF16) for i in range(2)]
    banks = [A.ps0[:, 0:512], A.ps1[:, 0:512], A.psW[:, 0:512],
             A.psW[:, 512:1024]]

    te_g = {}
    ve_it = {}
    st_it = {}
    for ocg in range(8):
        par = ocg % 2
        sg = [("te", te_g[ocg - 2])] if ocg >= 2 else p.barrier_gates()
        gsems = []
        for i in range(4):
            gsems.append(("g" + str(par),
                          p.dma("gpsimd", "g" + str(par), slabs[par][i][:],
                                d["wo_blk"][ocg * 4 + i],
                                waits=(sg if i == 0 else []))))
        for blk in range(4):
            it = ocg * 4 + blk
            cslot = it % 2
            cg = [("te", te_g[("c", it - 2)])] if it >= 2 else p.barrier_gates()
            csem = ("c" + str(cslot),
                    p.dma("sync", "c" + str(cslot),
                          ctxblk[cslot][:].rearrange("p (dcc i) -> p dcc i",
                                                     i=512),
                          A.pr_ctx[:, blk * 512:(blk + 1) * 512].rearrange(
                              "(dcc p) i -> p dcc i", p=128),
                          waits=cg))
            rg = [("ve", ve_it[it - 2])] if it >= 2 else p.barrier_gates()
            rtgt = p.inc("r" + str(cslot), 64)
            wsl = [(p.sems[n], v) for n, v in rg]

            def f_rr(e, ocg=ocg, blk=blk, cslot=cslot, wsl=wsl):
                for sem, v in wsl:
                    e.wait_ge(sem, v)
                with e.register() as r:
                    e.reg_load(r, A.ids_s[0:1, 0:1])
                    for pp in (0, 1):
                        with e.If_cmp(r, pp, "IS_EQ"):
                            for oci in range(4):
                                oc = ocg * 4 + oci
                                s3 = A.sh_x1[oc * 128:(oc + 1) * 128,
                                             blk * 1024:(blk + 1) * 1024]
                                s3 = s3.rearrange("p (s two i) -> p s two i",
                                                  two=2, i=128)
                                e.dma_start(
                                    out=x1res[cslot][:, oci * 512:(oci + 1) * 512]
                                    .rearrange("p (s i) -> p s i", i=128),
                                    in_=s3[:, :, pp, :]).then_inc(
                                    p.sems["r" + str(cslot)], 16)
            p.emit("sync", f_rr)
            rsem = ("r" + str(cslot), rtgt)

            te_last = None
            for i in range(4):
                def f_mm(e, i=i, cslot=cslot, par=par):
                    last = None
                    for dcc in range(DC):
                        last = e.matmul(
                            banks[i], slabs[par][i][:, dcc * 128:(dcc + 1) * 128],
                            ctxblk[cslot][:, dcc * 512:(dcc + 1) * 512],
                            start=(dcc == 0), stop=True)
                    return last
                wts = [gsems[i], csem, ("ve", p.cur("ve"))]
                te_last = p.step("tensor", "te", f_mm, waits=wts)
            te_g[("c", it)] = te_last
            if blk == 3:
                te_g[ocg] = te_last

            ve_last = None
            for i in range(4):
                wts_v = [("te", te_last), rsem]
                if it >= 2:
                    wts_v.append(st_it[it - 2])
                ve_last = p.step("vector", "ve",
                                 lambda e, i=i, cslot=cslot:
                                 e.tensor_tensor(
                                     out=x2st[cslot][:, i * 512:(i + 1) * 512],
                                     in0=banks[i],
                                     in1=x1res[cslot][:, i * 512:(i + 1) * 512],
                                     op=ALU.add),
                                 waits=wts_v)
            ve_it[it] = ve_last

            semn = "oX" + str(cslot)
            tgt0 = p.cur(semn)
            for i in range(4):
                oc = ocg * 4 + i
                t = p.inc(semn, 16)

                def wf(e, oc=oc, blk=blk, i=i, cslot=cslot, ve_last=ve_last,
                       semn=semn):
                    e.wait_ge(p.sems["ve"], ve_last)
                    with e.register() as r:
                        e.reg_load(r, A.ids_s[0:1, 0:1])
                        for pp in (0, 1):
                            with e.If_cmp(r, pp, "IS_EQ"):
                                c0 = pp * 2048 + blk * 512
                                e.dma_start(
                                    out=A.sh_x2[oc * 128:(oc + 1) * 128,
                                                c0:c0 + 512],
                                    in_=x2st[cslot][:, i * 512:(i + 1) * 512]
                                ).then_inc(p.sems[semn], 16)
                p.emit("sync", wf)
            st_it[it] = (semn, p.cur(semn))

    write_flag(A, {0: 4, 1: 5},
               [("oX0", p.cur("oX0")), ("oX1", p.cur("oX1"))])


def phase4(A):
    p, d = A.p, A.d
    spin_barrier(A, [4, 5], "pb2")
    b2 = p.fence([], name="sy")

    x2bf = [A.carve(i * 8 * KB, T, BF16) for i in range(2)]
    sq = A.carve(16 * KB, T, BF16)
    mu3 = A.carve(24 * KB, T, F32, rows=1)
    r3 = A.carve(40 * KB, T, F32, rows=1)
    wrow = A.carve(56 * KB, T, F32, rows=1)
    mubq = A.carve(72 * KB, 1024, F32)
    rbbq = A.carve(76 * KB, 1024, F32)
    wrowb = A.carve(96 * KB, T, F32)
    tmpf2 = A.carve(112 * KB, T, F32)
    trow = A.carve(88 * KB, T, F32, rows=1)

    srow = [A.ps0[0:1, :], A.ps0[32:33, :], A.ps0[64:65, :],
            A.ps1[0:1, :], A.ps1[32:33, :], A.ps1[64:65, :],
            A.ps2[0:1, :], A.ps2[32:33, :]]
    qrow = [A.psW[0:1, 0:512], A.psW[32:33, 0:512], A.psW[64:65, 0:512],
            A.psW[0:1, 512:1024], A.psW[32:33, 512:1024],
            A.psW[64:65, 512:1024],
            A.psW2[0:1, 0:512], A.psW2[32:33, 0:512]]
    kx = [0]
    te_hist = {}
    for dc in range(DC):
        slot = kx[0] % 2
        kx[0] += 1
        gates = [("te", te_hist[dc - 2])] if dc >= 2 else \
            p.barrier_gates() + [("sy", b2)]
        csem = p.dma("sync", "h" + str(slot), x2bf[slot][:],
                     A.sh_x2[dc * 128:(dc + 1) * 128, :], waits=gates)
        se_sq = p.step("scalar", "se",
                       lambda e, slot=slot:
                       e.activation(out=sq[:], in_=x2bf[slot][:],
                                    func=AF.Square),
                       waits=[("h" + str(slot), csem), ("te", p.cur("te"))])

        def f_mm(e, slot=slot, dc=dc):
            last = None
            for b in range(8):
                sl = slice(b * 512, (b + 1) * 512)
                e.matmul(srow[b], A.ones_col[:], x2bf[slot][:, sl],
                         start=(dc == 0), stop=True)
                last = e.matmul(qrow[b], A.ones_col[:], sq[:, sl],
                                start=(dc == 0), stop=True)
            return last
        wts = [("h" + str(slot), csem), ("se", se_sq)]
        if dc == 0:
            wts.append(("ve", p.cur("ve")))
        te_hist[dc] = p.step("tensor", "te", f_mm, waits=wts)
    te_mm = p.cur("te")
    ve_rows = None
    se1_prev = None
    for b in range(8):
        sl = slice(b * 512, (b + 1) * 512)

        def f_pp(e, b=b, sl=sl):
            e.tensor_scalar_mul(out=mu3[0:1, sl], in0=srow[b], scalar1=1.0 / D)
            e.tensor_scalar_mul(out=A.rowb[:], in0=qrow[b], scalar1=1.0 / D)
            e.tensor_tensor(out=A.rowc[:], in0=mu3[0:1, sl],
                            in1=mu3[0:1, sl], op=ALU.mult)
            return e.tensor_tensor(out=A.rowb[:], in0=A.rowb[:],
                                   in1=A.rowc[:], op=ALU.subtract)
        ppw = [("te", te_mm)]
        if se1_prev is not None:
            ppw.append(("se", se1_prev))
        ve1 = p.step("vector", "ve", f_pp, waits=ppw)
        se1 = p.step("scalar", "se",
                     lambda e: e.activation(out=A.rowb[:], in_=A.rowb[:],
                                            func=AF.Sqrt, bias=A.eps_s[:]),
                     waits=[("ve", ve1)])
        se1_prev = se1
        ve_rows = p.step("vector", "ve",
                         lambda e, sl=sl: e.reciprocal(out=r3[0:1, sl],
                                                       in_=A.rowb[:]),
                         waits=[("se", se1)])

    A.mu3_d = mu3
    A.r3_d = r3
    A.wrow_d = wrow
    if _P4N <= 1:
        return

    def f_w(e):
        e.tensor_copy(out=wrow[:], in_=r3[:])
        e.memset(wrow[0:1, 0:355], 0.0)
        e.memset(wrow[0:1, 2048:2304], 0.0)
        return e.memset(wrow[0:1, 4095:4096], 0.0)
    ve_w = p.step("vector", "ve", f_w, waits=[("ve", ve_rows)])

    def f_s1(e):
        e.tensor_tensor(out=trow[:], in0=mu3[:], in1=wrow[:], op=ALU.mult)
        return e.tensor_reduce(out=A.sc1[:], in_=trow[:], axis=AX.X,
                               op=ALU.add)
    ve_s1 = p.step("vector", "ve", f_s1, waits=[("ve", ve_w)])

    if _P4N <= 2:
        return
    se_wb = None
    for b in range(8):
        sl = slice(b * 512, (b + 1) * 512)
        te_b = p.step("tensor", "te",
                      lambda e, sl=sl:
                      e.matmul(A.psW2[:, 0:512], A.ones_row[:], wrow[0:1, sl],
                               start=True, stop=True),
                      waits=[("ve", ve_w), ("se", p.cur("se"))])
        se_wb = p.step("scalar", "se",
                       lambda e, sl=sl: e.copy(out=wrowb[:, sl],
                                               in_=A.psW2[:, 0:512]),
                       waits=[("te", te_b)])
    ve_q2 = None
    ve_q2h = {}
    for dc in range(DC):
        slot = kx[0] % 2
        kx[0] += 1
        gates = [("ve", ve_q2h[dc - 2])] if dc >= 2 else [("se", se_wb)]
        csem = p.dma("sync", "h" + str(slot), x2bf[slot][:],
                     A.sh_x2[dc * 128:(dc + 1) * 128, :], waits=gates)

        def f_q2(e, slot=slot, dc=dc):
            e.tensor_tensor(out=tmpf2[:], in0=x2bf[slot][:],
                            in1=wrowb[:], op=ALU.mult)
            return e.tensor_reduce(out=A.q2col[:, dc:dc + 1],
                                   in_=tmpf2[:], axis=AX.X, op=ALU.add)
        ve_q2 = p.step("vector", "ve", f_q2,
                       waits=[("h" + str(slot), csem), ("se", se_wb)])
        ve_q2h[dc] = ve_q2

    if _P4N <= 3:
        return
    te_s1 = p.step("tensor", "te",
                   lambda e: e.matmul(A.ps2[:, 0:1], A.ones_row[:], A.sc1[:],
                                      start=True, stop=True),
                   waits=[("ve", ve_s1), ("ve", ve_q2), ("se", p.cur("se"))])

    def f_sub(e):
        e.tensor_copy(out=A.s1_col[:], in_=A.ps2[:, 0:1])
        return e.tensor_scalar(out=A.q2col[:], in0=A.q2col[:],
                               scalar1=A.s1_col[:], scalar2=None,
                               op0=ALU.subtract)
    p.step("vector", "ve", f_sub, waits=[("te", te_s1)])
    ve_bt = router_pipeline(A, A.q2col, 1.0 / NROUT)

    if _P4N <= 4:
        return
    QOFF = [0, 1024, 2048, 3072]

    def xln3(dc):
        return A.carve(96 * KB + dc * 2 * KB, 1024, BF16)

    murq = A.carve(96 * KB, 2048, F32, rows=1)
    tgt_mq = p.inc("si", 32)
    wsl_mq = [(p.sems["ve"], p.cur("ve")), (p.sems["se"], p.cur("se"))]

    def f_mq(e, wsl=wsl_mq):
        for sem, v in wsl:
            e.wait_ge(sem, v)
        with e.register() as r:
            e.reg_load(r, A.ids_s[0:1, 4:5])
            for quad in range(4):
                with e.If_cmp(r, quad, "IS_EQ"):
                    q0 = QOFF[quad]
                    e.dma_start(out=murq[0:1, 0:1024],
                                in_=mu3[0:1, q0:q0 + 1024]).then_inc(
                        p.sems["si"], 16)
                    e.dma_start(out=murq[0:1, 1024:2048],
                                in_=r3[0:1, q0:q0 + 1024]).then_inc(
                        p.sems["si"], 16)
    p.emit("sync", f_mq)

    se_bcq = None
    for b in range(2):
        sl = slice(b * 512, (b + 1) * 512)

        def f_tb(e, b=b):
            e.matmul(A.psW2[:, 0:512], A.ones_row[:],
                     murq[0:1, b * 512:(b + 1) * 512], start=True, stop=True)
            return e.matmul(A.psW2[:, 512:1024], A.ones_row[:],
                            murq[0:1, 1024 + b * 512:1024 + (b + 1) * 512],
                            start=True, stop=True)
        tgt_t = p.step("tensor", "te", f_tb,
                       waits=[("si", tgt_mq), ("ve", ve_bt),
                              ("se", p.cur("se"))])

        def f_ev(e, sl=sl):
            e.copy(out=mubq[:, sl], in_=A.psW2[:, 0:512])
            return e.copy(out=rbbq[:, sl], in_=A.psW2[:, 512:1024])
        se_bcq = p.step("scalar", "se", f_ev, waits=[("te", tgt_t)])

    def loader_q(dc, gates):
        slot = kx[0] % 2
        kx[0] += 1
        tgt = p.inc("h" + str(slot), 16)
        s = p.sems["h" + str(slot)]
        wsl = [(p.sems[n], v) for n, v in gates]

        def f(e, dc=dc, slot=slot, s=s, wsl=wsl):
            for sem, v in wsl:
                e.wait_ge(sem, v)
            with e.register() as r:
                e.reg_load(r, A.ids_s[0:1, 4:5])
                for quad in range(4):
                    with e.If_cmp(r, quad, "IS_EQ"):
                        e.dma_start(out=x2bf[slot][:, 0:1024],
                                    in_=A.sh_x2[dc * 128:(dc + 1) * 128,
                                                QOFF[quad]:QOFF[quad] + 1024]
                                    ).then_inc(s, 16)
        p.emit("sync", f)
        return x2bf[slot][:, 0:1024], ("h" + str(slot), tgt)

    A.mubq_d = mubq
    A.rbbq_d = rbbq
    if _P4N <= 5:
        return
    ve_n3 = ln_normalize(A, 1024, loader_q, mubq, rbbq, xln3, 80 * KB, se_bcq)
    A.xln3_0 = xln3(0)
    if _P4N <= 6:
        return

    se_t2, t2q = emit_t2(A, xln3, 1024, 88 * KB, extra_gates=[("ve", ve_n3)])
    A.t2q_d = t2q
    if _P4N <= 7:
        return

    A.SLAB_OFF = 24 * KB
    A.OST_OFF = 40 * KB
    A.OST_DT = F32

    def resid_loader(oc, gates):
        slot = kx[0] % 2
        kx[0] += 1
        tgt = p.inc("h" + str(slot), 16)
        s = p.sems["h" + str(slot)]
        wsl = [(p.sems[n], v) for n, v in gates]

        def f(e, oc=oc, slot=slot, s=s, wsl=wsl):
            for sem, v in wsl:
                e.wait_ge(sem, v)
            with e.register() as r:
                e.reg_load(r, A.ids_s[0:1, 4:5])
                for quad in range(4):
                    with e.If_cmp(r, quad, "IS_EQ"):
                        e.dma_start(out=x2bf[slot][:, 0:1024],
                                    in_=A.sh_x2[oc * 128:(oc + 1) * 128,
                                                QOFF[quad]:QOFF[quad] + 1024]
                                    ).then_inc(s, 16)
        p.emit("sync", f)
        return x2bf[slot][:, 0:1024], ("h" + str(slot), tgt)

    def out_writer(oc, blk, src, vsem):
        semn = "oF" + str((oc * 2 + blk) % 2)
        tgt = p.dma("sync", semn,
                    A.out[oc * 128:(oc + 1) * 128, blk * 512:(blk + 1) * 512],
                    src[:], waits=[vsem])
        return (semn, tgt)

    ffn_base(A, lambda oc: d["wt_blk"][oc], xln3, 1024, t2q, resid_loader,
             out_writer, first_gates=[("se", se_t2)])
    A.out_done = [("oF0", p.cur("oF0")), ("oF1", p.cur("oF1"))]


def finish(A):
    p = A.p
    nc = A.nc
    for name in A._dbg_pending:
        s = getattr(A, name)
        sap = s if isinstance(s, bass.AP) else s[:]
        A.dbg[name] = (nc.dram_tensor("dbg_" + name, list(sap.shape),
                                      sap.dtype, kind="ExternalOutput").ap(),
                       sap)
    waitall = [(n, p.cur(n)) for n in list(p.ctr) if not n.startswith("pb")]
    ws0 = [(p.sem(n), v) for n, v in waitall]

    def f0(e, ws=ws0):
        for sem, v in ws:
            e.wait_ge(sem, v)
    p.emit("sync", f0)
    for name, (dst, src) in A.dbg.items():
        p.dma("sync", "so", dst, src)
    tot = p.cur("so")

    def f1(e, tot=tot):
        e.wait_ge(p.sems["so"], tot)
    p.emit("sync", f1)


def _prep_host(inputs):
    x = np.asarray(inputs["L0_attn_x"], np.float32)
    Wq = np.asarray(inputs["L0_attn_Wq"], np.float32)
    Wk = np.asarray(inputs["L0_attn_Wk"], np.float32)
    Wv = np.asarray(inputs["L0_attn_Wv"], np.float32)
    Wo = np.asarray(inputs["L0_attn_Wo"], np.float32)
    W = np.asarray(inputs["L0_ffn_W"], np.float32)
    bb = np.asarray(inputs["L0_ffn_b"], np.float32)
    Am = np.asarray(inputs["L0_ffn_A"], np.float32)
    Bm = np.asarray(inputs["L0_ffn_B"], np.float32)
    rW = np.asarray(inputs["L0_ffn_router_W"], np.float32)
    rb = np.asarray(inputs["L0_ffn_router_b"], np.float32)

    tri = np.ascontiguousarray(np.tril(np.ones((128, 128), np.float32)).T)
    shared = dict(
        wq_blk=_bk3(Wq), wo_blk=_bk3(Wo),
        wt_blk=_bk3(np.ascontiguousarray(W.T)),
        wk_blk=_bk3(Wk), wv_blk=_bk3(Wv),
        at_blk=np.ascontiguousarray(
            Am.reshape(128, D).T.reshape(DC, 128, 128).transpose(1, 0, 2)
        ).reshape(128, DC * 128),
        bmat=np.ascontiguousarray(
            np.concatenate([Bm[e].T for e in range(8)], axis=0)),
        rwT_blk=np.ascontiguousarray(
            rW.T.reshape(DC, 128, 8).transpose(1, 0, 2)).reshape(128, DC * 8),
        biasw=np.ascontiguousarray(bb.reshape(DC, 128).T),
        rbias=rb.reshape(8, 1).astype(np.float32),
        id128=np.eye(128, dtype=np.float32),
        e16=np.repeat(np.eye(8, dtype=np.float32) * 2.0, 16, axis=1),
        iot=np.arange(64, dtype=np.float32).reshape(1, 64),
    )
    ones = np.ones((128, 128), np.float32)
    zeros = np.zeros((128, 128), np.float32)
    nonce = np.array([[np.random.randint(1 << 20, 1 << 30)]], np.int32)

    in_maps = []
    for c in range(8):
        pp, ph, b = c % 2, (c // 2) % 2, c // 4
        own = slice(2048 * pp, 2048 * (pp + 1))
        posq = np.concatenate(
            [np.arange(128 * (2 * m + pp), 128 * (2 * m + pp) + 128)
             for m in range(NSLOT)])
        m = dict(shared)
        m["xT"] = np.ascontiguousarray(x[b, own, :].T)
        m["posr"] = np.arange(2048 * pp, 2048 * (pp + 1),
                              dtype=np.float32).reshape(1, OWN)
        m["posq"] = posq.astype(np.float32).reshape(1, QSH)
        m["maskA"] = tri if pp == 0 else ones
        m["maskB"] = zeros if pp == 0 else tri
        m["ids"] = np.array([[pp, ph, b, c, 2 * pp + ph, 0, 0, 0]], np.int32)
        m["nonce"] = nonce
        in_maps.append(m)
    return in_maps


def kernel(**inputs):
    global _LAST_DEBUG
    in_maps = _prep_host(inputs)
    nc = build()
    res = run_bass_kernel_spmd(nc, in_maps, core_ids=list(range(8)))
    _LAST_DEBUG = res.results

    outf = np.zeros((B, T, D), np.float32)
    for c in range(8):
        pp, ph, b = c % 2, (c // 2) % 2, c // 4
        o = np.asarray(res.results[c]["out"])
        for i, mslot in enumerate(range(8 * ph, 8 * ph + 8)):
            j = 2 * mslot + pp
            outf[b, 128 * j:128 * (j + 1), :] = o[:, 128 * i:128 * (i + 1)].T
    return outf


# version 06
# speedup vs baseline: 2.4480x; 1.0126x over previous
import math
import numpy as np
import ml_dtypes
import concourse.bass as bass
import concourse.mybir as mybir
from concourse.bass_utils import run_bass_kernel_spmd
from contextlib import ExitStack

F32 = mybir.dt.float32
BF16 = mybir.dt.bfloat16
I32 = mybir.dt.int32
U8 = mybir.dt.uint8
AF = mybir.ActivationFunctionType
ALU = mybir.AluOpType
AX = mybir.AxisListType

B, T, D = 2, 4096, 4096
NQH, NKV, HD = 32, 8, 128
DC = 32
NSLOT = 16
OWN = 2048
QSH = 2048
Q0T, Q1T = 611, 4095
NROUT = Q1T - Q0T
EPS = 1e-5
ISQ = 1.0 / math.sqrt(HD)
MAGIC = 1234567
LN_RB = -math.log(500000.0) / 64.0
TWO_PI = 2.0 * math.pi
PI = math.pi
KB = 1024

ENGS = ("sync", "tensor", "scalar", "vector", "gpsimd")

_DEBUG = []
_LAST_DEBUG = None
_PHASES = (1, 2, 3, 4)
_ATT1 = False
_P4N = 99


class Prog:
    def __init__(self, nc, ctx):
        self.nc, self.ctx = nc, ctx
        self.th = {e: [] for e in ENGS}
        self.sems, self.ctr = {}, {}

    def sem(self, name):
        if name not in self.sems:
            self.sems[name] = self.ctx.enter_context(self.nc.semaphore(name))
            self.ctr[name] = 0
        return self.sems[name]

    def inc(self, name, k=1):
        self.sem(name)
        self.ctr[name] += k
        return self.ctr[name]

    def cur(self, name):
        self.sem(name)
        return self.ctr[name]

    def barrier_gates(self):
        return [("te", self.cur("te")), ("ve", self.cur("ve")),
                ("se", self.cur("se"))]

    def step(self, eng, sem_name, fn, waits=()):
        ws = [(self.sem(n), v) for n, v in waits]
        tgt = self.inc(sem_name, 1)
        s = self.sem(sem_name)

        def f(e, fn=fn, ws=ws, s=s):
            for sem, v in ws:
                e.wait_ge(sem, v)
            fn(e).then_inc(s, 1)
        self.emit(eng, f)
        return tgt

    def emit(self, eng, fn):
        self.th[eng].append(fn)

    def dma(self, eng, sem_name, out, in_, waits=(), **kw):
        ws = [(self.sem(n), v) for n, v in waits]
        tgt = self.inc(sem_name, 16)
        s = self.sem(sem_name)

        def f(e, out=out, in_=in_, s=s, ws=ws, kw=kw):
            for sem, v in ws:
                e.wait_ge(sem, v)
            e.dma_start(out=out, in_=in_, **kw).then_inc(s, 16)
        self.emit(eng, f)
        return tgt

    def dma_ifsel(self, eng, sem_name, mk_oi, nvar, waits=(), idword=0):
        ws = [(self.sem(n), v) for n, v in waits]
        tgt = self.inc(sem_name, 16)
        s = self.sem(sem_name)
        A = self.A
        pairs = [mk_oi(v) for v in range(nvar)]

        def f(e, pairs=pairs, ws=ws, s=s):
            for sem, v in ws:
                e.wait_ge(sem, v)
            with e.register() as r:
                e.reg_load(r, A.ids_s[0:1, idword:idword + 1])
                for v, (out, in_) in enumerate(pairs):
                    with e.If_cmp(r, v, "IS_EQ"):
                        e.dma_start(out=out, in_=in_).then_inc(s, 16)
        self.emit(eng, f)
        return tgt

    def fence(self, waits, name="sy"):
        ws = [(self.sem(n), v) for n, v in waits]
        tgt = self.inc(name, 1)
        s = self.sem(name)

        def f(e, ws=ws, s=s):
            for sem, v in ws:
                e.wait_ge(sem, v)
            e.nop().then_inc(s, 1)
        self.emit("sync", f)
        return tgt


def _bk3(w):
    Dd, Oo = w.shape
    return np.ascontiguousarray(
        w.reshape(Dd // 128, 128, Oo // 128, 128).transpose(2, 1, 0, 3)
    ).reshape(Oo // 128, 128, Dd)


def build():
    nc = bass.Bass("TRN2", target_bir_lowering=False)
    ctx = ExitStack()
    p = Prog(nc, ctx)
    A = type("A", (), {})()
    A.nc, A.p = nc, p
    p.A = A

    din = {}
    for name, shape, dt in [
        ("xT", [D, OWN], F32),
        ("wq_blk", [DC, 128, DC * 128], F32),
        ("wo_blk", [DC, 128, DC * 128], F32),
        ("wt_blk", [DC, 128, DC * 128], F32),
        ("wk_blk", [NKV, 128, DC * 128], F32),
        ("wv_blk", [NKV, 128, DC * 128], F32),
        ("at_blk", [128, DC * 128], F32),
        ("bmat", [128, D], F32),
        ("rwT_blk", [128, DC * 8], F32),
        ("biasw", [128, DC], F32),
        ("rbias", [8, 1], F32),
        ("id128", [128, 128], F32),
        ("maskA", [128, 128], F32),
        ("maskB", [128, 128], F32),
        ("e16", [8, 128], F32),
        ("iot", [1, 64], F32),
        ("posr", [1, OWN], F32),
        ("posq", [1, QSH], F32),
        ("ids", [1, 8], I32),
        ("nonce", [1, 1], I32),
    ]:
        din[name] = nc.dram_tensor(name, shape, dt, kind="ExternalInput").ap()
    A.d = din

    A.out = nc.dram_tensor("out", [D, 1024], F32, kind="ExternalOutput").ap()
    Sh = dict(kind="Internal", addr_space="Shared")
    A.sh_x1 = nc.dram_tensor("sh_x1", [D, T], BF16, **Sh).ap()
    A.sh_k = nc.dram_tensor("sh_k", [1024, T], BF16, **Sh).ap()
    A.sh_v = nc.dram_tensor("sh_v", [T, 1024], BF16, **Sh).ap()
    A.sh_x2 = nc.dram_tensor("sh_x2", [D, T], BF16, **Sh).ap()
    A.sh_r1 = nc.dram_tensor("sh_r1", [2, 128, DC], F32, **Sh).ap()
    A.sh_fl = nc.dram_tensor("sh_fl", [1, 16], I32, **Sh).ap()
    A.pr_ctx = nc.dram_tensor("pr_ctx", [D, QSH], BF16, kind="Internal").ap()
    A.pr_rt = nc.dram_tensor("pr_rt", [1, 8], F32, kind="Internal").ap()

    A.dbg = {}
    A._dbg_pending = list(_DEBUG)

    sb = lambda *a: ctx.enter_context(nc.sbuf_tensor(*a))
    A.at_bf = sb("at_bf", [128, DC * 128], BF16)
    A.bt_bf = sb("bt_bf", [128, D], BF16)
    A.rw_s = sb("rw_s", [128, DC * 8], F32)
    A.bias_s = sb("bias_s", [128, DC], F32)
    A.hbias_s = sb("hbias_s", [128, DC], F32)
    A.rb8_s = sb("rb8_s", [8, 1], F32)
    A.rowb8 = sb("rowb8", [8, 1], F32)
    A.ones8f = sb("ones8f", [8, 1], F32)
    A.id_bf = sb("id_bf", [128, 128], BF16)
    A.mA_s = sb("mA_s", [128, 128], BF16)
    A.mB_s = sb("mB_s", [128, 128], BF16)
    A.ones_col = sb("ones_col", [128, 1], BF16)
    A.ones_row = sb("ones_row", [1, 128], F32)
    A.e16_s = sb("e16_s", [8, 128], F32)
    A.eps_s = sb("eps_s", [1, 1], F32)
    A.magic_s = sb("magic_s", [1, 1], I32)
    A.ids_s = sb("ids_s", [1, 8], I32)
    A.iot_s = sb("iot_s", [1, 64], F32)
    A.inv_s = sb("inv_s", [1, 64], F32)
    A.fl_s = sb("fl_s", [1, 16], I32)
    A.rt_row = sb("rt_row", [1, 8], F32)
    A.rt_col8 = sb("rt_col8", [8, 1], F32)
    A.rt_col = sb("rt_col", [128, 1], F32)
    A.s1_col = sb("s1_col", [128, 1], F32)
    A.r1part = sb("r1part", [128, DC], F32)
    A.r1mate = sb("r1mate", [128, DC], F32)
    A.q2col = sb("q2col", [128, DC], F32)
    A.rowa = sb("rowa", [1, 512], F32)
    A.rowb = sb("rowb", [1, 512], F32)
    A.rowc = sb("rowc", [1, 512], F32)
    A.rrow = sb("rrow", [1, 512], F32)
    A.sc1 = sb("sc1", [1, 1], F32)
    A.sc2 = sb("sc2", [1, 1], F32)
    A.cs_cos = sb("cs_cos", [64, OWN], BF16)
    A.cs_sin = sb("cs_sin", [64, OWN], BF16)
    A.csq_cos = sb("csq_cos", [64, QSH], BF16)
    A.csq_sin = sb("csq_sin", [64, QSH], BF16)

    ARENA = 160 * KB
    A.arena = sb("arena", [128, ARENA], U8)

    def carve(off, cols, dt, rows=128):
        esz = mybir.dt.size(dt)
        assert off % 32 == 0 and off + cols * esz <= ARENA, (off, cols, esz)
        return A.arena[0:rows, off:off + cols * esz].bitcast(dt)
    A.carve = carve

    A.ps0 = ctx.enter_context(nc.psum_tensor("ps0", [128, 512], F32))
    A.ps1 = ctx.enter_context(nc.psum_tensor("ps1", [128, 512], F32))
    A.ps2 = ctx.enter_context(nc.psum_tensor("ps2", [128, 512], F32))
    A.psW = ctx.enter_context(nc.psum_tensor("psW", [128, 1024], F32))
    A.psW2 = ctx.enter_context(nc.psum_tensor("psW2", [128, 1024], F32))
    A.ps_t = ctx.enter_context(nc.psum_tensor("ps_t", [128, 512], BF16))

    phase0(A)
    if 1 in _PHASES:
        phase1(A)
    if 2 in _PHASES:
        phase2(A)
    if 3 in _PHASES:
        phase3(A)
    if 4 in _PHASES:
        phase4(A)
    finish(A)

    with nc.Block() as block:
        block.sync(lambda e: [f(e) for f in p.th["sync"]])
        block.tensor(lambda e: [f(e) for f in p.th["tensor"]])
        block.scalar(lambda e: [f(e) for f in p.th["scalar"]])
        block.vector(lambda e: [f(e) for f in p.th["vector"]])
        block.gpsimd(lambda e: [f(e) for f in p.th["gpsimd"]])
    ctx.close()
    return nc


def phase0(A):
    p, d = A.p, A.d
    for name, dst in [("rwT_blk", A.rw_s), ("biasw", A.bias_s),
                      ("rbias", A.rb8_s), ("iot", A.iot_s), ("ids", A.ids_s),
                      ("e16", A.e16_s), ("nonce", A.magic_s)]:
        p.dma("sync", "si", dst[:], d[name])
    for name, dst in [("at_blk", A.at_bf), ("id128", A.id_bf),
                      ("maskA", A.mA_s), ("maskB", A.mB_s)]:
        p.dma("gpsimd", "gw", dst[:], d[name])
    ld_small = p.fence([("si", p.cur("si")), ("gw", p.cur("gw"))])

    def f_ms(e):
        e.memset(A.ones_col[:], 1.0)
        e.memset(A.ones8f[:], 1.0)
        e.memset(A.ones_row[:], 1.0)
        e.memset(A.eps_s[:], EPS)
        return e.tensor_scalar_mul(out=A.hbias_s[:], in0=A.bias_s[:], scalar1=0.5)
    ve_ms = p.step("vector", "ve", f_ms, waits=[("sy", ld_small)])

    se_inv = p.step("scalar", "se",
                    lambda e: e.activation(out=A.inv_s[:], in_=A.iot_s[:],
                                           func=AF.Exp, scale=LN_RB),
                    waits=[("sy", ld_small)])

    pos_s = A.carve(0, 2048, F32, rows=1)
    ang = A.carve(32 * KB, 2048, F32, rows=64)
    tmp = A.carve(48 * KB, 2048, F32, rows=64)
    tif = A.carve(64 * KB, 2048, F32, rows=64)
    for posname, cdst, sdst in [("posr", A.cs_cos, A.cs_sin),
                                ("posq", A.csq_cos, A.csq_sin)]:
        ldp = p.fence([("si", p.dma("sync", "si", pos_s[:], d[posname],
                                    waits=[("se", p.cur("se"))]))])
        for blk in range(4):
            sl = slice(blk * 512, (blk + 1) * 512)
            te_a = p.step("tensor", "te",
                          lambda e, sl=sl: e.matmul(
                              A.psW[0:64, 0:512], A.inv_s[:], pos_s[0:1, sl],
                              start=True, stop=True),
                          waits=[("sy", ldp), ("se", se_inv),
                                 ("ve", p.cur("ve"))])

            def f_rr(e, sl=sl):
                e.tensor_scalar_mul(out=ang[0:64, sl], in0=A.psW[0:64, 0:512],
                                    scalar1=1.0 / TWO_PI)
                e.tensor_copy(out=tif[0:64, sl].bitcast(I32), in_=ang[0:64, sl])
                e.tensor_copy(out=tmp[0:64, sl], in_=tif[0:64, sl].bitcast(I32))
                e.tensor_tensor(out=ang[0:64, sl], in0=ang[0:64, sl],
                                in1=tmp[0:64, sl], op=ALU.subtract)
                e.tensor_scalar_mul(out=ang[0:64, sl], in0=ang[0:64, sl],
                                    scalar1=TWO_PI)
                e.tensor_scalar_add(out=tmp[0:64, sl], in0=ang[0:64, sl],
                                    scalar1=PI / 2.0)
                e.tensor_scalar(out=tif[0:64, sl], in0=tmp[0:64, sl],
                                scalar1=PI, scalar2=None, op0=ALU.is_gt)
                e.tensor_scalar_mul(out=tif[0:64, sl], in0=tif[0:64, sl],
                                    scalar1=-TWO_PI)
                return e.tensor_tensor(out=tmp[0:64, sl], in0=tmp[0:64, sl],
                                       in1=tif[0:64, sl], op=ALU.add)
            ve_rr = p.step("vector", "ve", f_rr, waits=[("te", te_a)])

            def f_sc(e, sl=sl, cdst=cdst, sdst=sdst):
                e.activation(out=sdst[0:64, sl], in_=ang[0:64, sl], func=AF.Sin)
                return e.activation(out=cdst[0:64, sl], in_=tmp[0:64, sl],
                                    func=AF.Sin)
            p.step("scalar", "se", f_sc, waits=[("ve", ve_rr)])
    A.tables_done = p.fence([("se", p.cur("se")), ("ve", ve_ms)])


def ln_block(A, ncols, loader, sq_off, mub, rbb, rows_out=None):
    p = A.p
    nblk = ncols // 512
    sq = A.carve(sq_off, ncols, BF16)
    srow = [A.ps0[0:1, :], A.ps1[0:1, :]]
    qrow = [A.ps2[0:1, :], A.psW[0:1, 0:512]]
    te_hist = {}
    for dc in range(DC):
        gates = [("te", te_hist[dc - 2])] if dc >= 2 else p.barrier_gates()
        ch, csem = loader(dc, gates)
        se_sq = p.step("scalar", "se",
                       lambda e, ch=ch: e.activation(out=sq[:, 0:ncols], in_=ch,
                                                     func=AF.Square),
                       waits=[csem, ("te", p.cur("te"))])

        def f_mm(e, ch=ch, dc=dc):
            last = None
            for b in range(nblk):
                sl = slice(b * 512, (b + 1) * 512)
                e.matmul(srow[b], A.ones_col[:], ch[:, sl],
                         start=(dc == 0), stop=True)
                last = e.matmul(qrow[b], A.ones_col[:], sq[:, sl],
                                start=(dc == 0), stop=True)
            return last
        wts = [csem, ("se", se_sq)]
        if dc == 0:
            wts.append(("ve", p.cur("ve")))
        te_hist[dc] = p.step("tensor", "te", f_mm, waits=wts)
    te_mm = p.cur("te")
    se_last = None
    te_b_prev = None
    for b in range(nblk):
        sl = slice(b * 512, (b + 1) * 512)

        def f_pp(e, b=b):
            e.tensor_scalar_mul(out=A.rowa[:], in0=srow[b], scalar1=1.0 / D)
            e.tensor_scalar_mul(out=A.rowb[:], in0=qrow[b], scalar1=1.0 / D)
            e.tensor_tensor(out=A.rowc[:], in0=A.rowa[:], in1=A.rowa[:],
                            op=ALU.mult)
            return e.tensor_tensor(out=A.rowb[:], in0=A.rowb[:], in1=A.rowc[:],
                                   op=ALU.subtract)
        ppw = [("te", te_mm)]
        if te_b_prev is not None:
            ppw.append(("te", te_b_prev))
        ve1 = p.step("vector", "ve", f_pp, waits=ppw)
        se1 = p.step("scalar", "se",
                     lambda e: e.activation(out=A.rowb[:], in_=A.rowb[:],
                                            func=AF.Sqrt, bias=A.eps_s[:]),
                     waits=[("ve", ve1)])
        ve2 = p.step("vector", "ve",
                     lambda e: e.reciprocal(out=A.rowc[:], in_=A.rowb[:]),
                     waits=[("se", se1)])
        if rows_out is not None:
            mu_row, r_row = rows_out

            def f_cp(e, sl=sl, mu_row=mu_row, r_row=r_row):
                e.tensor_copy(out=mu_row[0:1, sl], in_=A.rowa[:])
                return e.tensor_copy(out=r_row[0:1, sl], in_=A.rowc[:])
            ve2 = p.step("vector", "ve", f_cp, waits=[("ve", ve2)])

        def f_bc(e):
            e.matmul(A.psW2[:, 0:512], A.ones_row[:], A.rowa[:],
                     start=True, stop=True)
            return e.matmul(A.psW2[:, 512:1024], A.ones_row[:], A.rowc[:],
                            start=True, stop=True)
        te_b = p.step("tensor", "te", f_bc,
                      waits=[("ve", ve2), ("se", p.cur("se"))])
        te_b_prev = te_b

        def f_ev(e, sl=sl):
            e.copy(out=mub[:, sl], in_=A.psW2[:, 0:512])
            return e.copy(out=rbb[:, sl], in_=A.psW2[:, 512:1024])
        se_last = p.step("scalar", "se", f_ev, waits=[("te", te_b)])
    return se_last


def ln_normalize(A, ncols, loader, mub, rbb, xout, tmpf_off, ready_se):
    p = A.p
    tmpf = A.carve(tmpf_off, ncols, F32)
    ve_hist = {}
    ve_last = None
    for dc in range(DC):
        gates = [("ve", ve_hist[dc - 2])] if dc >= 2 else p.barrier_gates()
        ch, csem = loader(dc, gates)
        xo = xout(dc)

        def f_n(e, ch=ch, xo=xo):
            e.tensor_tensor(out=tmpf[:, 0:ncols], in0=ch, in1=mub[:, 0:ncols],
                            op=ALU.subtract)
            return e.tensor_tensor(out=xo[:, 0:ncols], in0=tmpf[:, 0:ncols],
                                   in1=rbb[:, 0:ncols], op=ALU.mult)
        wts = [csem]
        if dc == 0:
            wts += [("se", ready_se), ("sy", A.tables_done)]
        ve_last = p.step("vector", "ve", f_n, waits=wts)
        ve_hist[dc] = ve_last
    return ve_last


def router_pipeline(A, qcol, mean_scale, extra_gates=()):
    p = A.p

    def f_lg(e):
        last = None
        for dc in range(DC):
            last = e.matmul(A.ps2[0:8, 0:1], A.rw_s[:, dc * 8:(dc + 1) * 8],
                            qcol[:, dc:dc + 1], start=(dc == 0), stop=True)
        return last
    te_lg = p.step("tensor", "te", f_lg,
                   waits=[("ve", p.cur("ve")), ("se", p.cur("se"))] +
                   list(extra_gates))
    se_e = p.step("scalar", "se",
                  lambda e: e.activation(out=A.rowb8[:], in_=A.ps2[0:8, 0:1],
                                         func=AF.Exp, bias=A.rb8_s[:],
                                         scale=mean_scale),
                  waits=[("te", te_lg)])

    def f_sx(e):
        e.matmul(A.ps2[0:1, 8:9], A.rowb8[:], A.ones8f[:],
                 start=True, stop=True)
        return e.matmul(A.ps2[:, 16:17], A.e16_s[:], A.rowb8[:],
                        start=True, stop=True)
    te_sx = p.step("tensor", "te", f_sx, waits=[("se", se_e)])

    def f_rcp(e):
        e.reciprocal(out=A.sc2[:], in_=A.ps2[0:1, 8:9])
        return e.tensor_copy(out=A.sc1[:], in_=A.sc2[:])
    ve_r = p.step("vector", "ve", f_rcp, waits=[("te", te_sx)])
    te_bc = p.step("tensor", "te",
                   lambda e: e.matmul(A.ps2[:, 24:25], A.ones_row[0:1, :],
                                      A.sc2[:], start=True, stop=True),
                   waits=[("ve", ve_r)])
    bsem = p.dma("gpsimd", "gw", A.bt_bf[:], A.d["bmat"],
                 waits=p.barrier_gates())

    def f_bt(e):
        e.tensor_scalar_mul(out=A.bt_bf[:], in0=A.bt_bf[:],
                            scalar1=A.ps2[:, 16:17])
        e.tensor_scalar_mul(out=A.bt_bf[:], in0=A.bt_bf[:],
                            scalar1=A.ps2[:, 24:25])
        return e.tensor_copy(out=A.rt_col[:], in_=A.bt_bf[0:128, 0:1])
    return p.step("vector", "ve", f_bt, waits=[("te", te_bc), ("gw", bsem)])


def emit_t2(A, xout, ncols, t2_off, extra_gates=()):
    p = A.p
    t2 = A.carve(t2_off, ncols, BF16)
    se_t = None
    for blk in range(ncols // 512):
        sl = slice(blk * 512, (blk + 1) * 512)

        def f_mm(e, sl=sl):
            last = None
            for dc in range(DC):
                last = e.matmul(A.ps2[:], A.at_bf[:, dc * 128:(dc + 1) * 128],
                                xout(dc)[:, sl], start=(dc == 0), stop=True)
            return last
        wts = [("se", p.cur("se"))] + (list(extra_gates) if blk == 0 else [])
        te_t = p.step("tensor", "te", f_mm, waits=wts)
        se_t = p.step("scalar", "se",
                      lambda e, sl=sl: e.copy(out=t2[:, sl], in_=A.ps2[:]),
                      waits=[("te", te_t)])
    return se_t, t2


def ffn_base(A, slab_src, xout, ncols, t2, resid_loader, out_writer,
             first_gates=()):
    p = A.p
    nblk = ncols // 512
    slabs = [A.carve(A.SLAB_OFF + i * 8 * KB, DC * 128, BF16) for i in range(2)]
    esz = mybir.dt.size(A.OST_DT)
    osts = [A.carve(A.OST_OFF + i * 512 * esz, 512, A.OST_DT) for i in range(2)]
    te_oc = {}
    ve_hist = {}
    st_hist = {}
    for oc in range(DC):
        sg = [("te", te_oc[oc - 2])] if oc >= 2 else p.barrier_gates()
        gsem = ("g" + str(oc % 2),
                p.dma("gpsimd", "g" + str(oc % 2), slabs[oc % 2][:],
                      slab_src(oc), waits=sg))
        rg = [("ve", ve_hist[oc - 2])] if oc >= 2 else p.barrier_gates()
        rap, rsem = resid_loader(oc, rg)
        for blk in range(nblk):
            sl = slice(blk * 512, (blk + 1) * 512)
            psu = A.ps0 if blk % 2 == 0 else A.ps1

            def f_mm(e, sl=sl, psu=psu, slab=slabs[oc % 2], oc=oc):
                for dc in range(DC):
                    e.matmul(psu[:], slab[:, dc * 128:(dc + 1) * 128],
                             xout(dc)[:, sl], start=(dc == 0), stop=True)
                return e.matmul(psu[:], A.bt_bf[:, oc * 128:(oc + 1) * 128],
                                t2[:, sl], start=False, stop=True)
            wts = [gsem, ("se", p.cur("se"))]
            if oc == 0 and blk == 0:
                wts += list(first_gates)
            te_b = p.step("tensor", "te", f_mm, waits=wts)
            if blk == nblk - 1:
                te_oc[oc] = te_b
            se_b = p.step("scalar", "se",
                          lambda e, psu=psu, oc=oc:
                          e.activation(out=A.ps2[:], in_=psu[:],
                                       func=AF.Identity,
                                       bias=A.hbias_s[:, oc:oc + 1], scale=0.5),
                          waits=[("te", te_b), ("ve", p.cur("ve"))])
            it = oc * nblk + blk
            wts_v = [("se", se_b), rsem]
            if it >= 2:
                wts_v.append(st_hist[it - 2])
            ve_b = p.step("vector", "ve",
                          lambda e, rap=rap, sl=sl, ost=osts[blk % 2]:
                          e.tensor_tensor(out=ost[:], in0=A.ps2[:],
                                          in1=rap[:, sl], op=ALU.add),
                          waits=wts_v)
            ve_hist[oc] = ve_b
            st_hist[it] = out_writer(oc, blk, osts[blk % 2], ("ve", ve_b))


def spin_barrier(A, words, pbname):
    p = A.p

    p.sem(pbname)

    def f(e):
        with e.register() as tot, e.register() as cnd, e.register() as tmpr, \
             e.register() as non:
            e.reg_load(non, A.magic_s[0:1, 0:1])

            def cond():
                e.dma_start(out=A.fl_s[:], in_=A.sh_fl).then_inc(
                    p.sems[pbname], 16)
                e.nop(cycle_cnt=6000)
                e.nop(cycle_cnt=6000)
                e.reg_mov(tot, 0)
                for w in words:
                    e.reg_load(tmpr, A.fl_s[0:1, w:w + 1])
                    e.reg_alu(tmpr, tmpr, non, ALU.is_equal)
                    e.reg_alu(tot, tot, tmpr, ALU.add)
                e.reg_alu(cnd, tot, len(words), ALU.is_lt)
                return cnd
            with e.While(cond):
                e.nop()
    p.emit("sync", f)


def write_flag(A, words_by_pp, waits):
    p = A.p
    ws = [(p.sem(n), v) for n, v in waits]
    p.inc("so", 16)

    def f(e, ws=ws):
        for sem, v in ws:
            e.wait_ge(sem, v)
        with e.register() as r:
            e.reg_load(r, A.ids_s[0:1, 0:1])
            for pp, w in words_by_pp.items():
                with e.If_cmp(r, pp, "IS_EQ"):
                    e.dma_start(out=A.sh_fl[0:1, w:w + 1],
                                in_=A.magic_s[:]).then_inc(p.sems["so"], 16)
    p.emit("sync", f)
    return p.cur("so")


def phase1(A):
    p, d = A.p, A.d
    xbf = [A.carve(128 * KB + i * 4 * KB, OWN, BF16) for i in range(2)]
    mub = A.carve(136 * KB, OWN, F32)
    rbb = A.carve(144 * KB, OWN, F32)

    def xln1(dc):
        return A.carve(dc * 4 * KB, OWN, BF16)

    kx = [0]

    def mk_loader(h):
        def loader(dc, gates):
            slot = kx[0] % 2
            kx[0] += 1
            cols = OWN if h is None else 1024
            c0 = 0 if h is None else h * 1024
            sem = p.dma("gpsimd", "x" + str(slot), xbf[slot][:, 0:cols],
                        d["xT"][dc * 128:(dc + 1) * 128, c0:c0 + cols],
                        waits=gates)
            return xbf[slot][:, 0:cols], ("x" + str(slot), sem)
        return loader

    se_bc = None
    for h in range(2):
        se_bc = ln_block(A, 1024, mk_loader(h), 152 * KB,
                         mub[:, h * 1024:(h + 1) * 1024],
                         rbb[:, h * 1024:(h + 1) * 1024])
    ve_n = ln_normalize(A, OWN, mk_loader(None), mub, rbb, xln1, 152 * KB,
                        se_bc)

    def f_r1(e):
        with e.register() as r:
            e.reg_load(r, A.ids_s[0:1, 0:1])
            with e.If_cmp(r, 0, "IS_EQ"):
                for dc in range(DC):
                    e.tensor_reduce(out=A.r1part[:, dc:dc + 1],
                                    in_=xln1(dc)[:, Q0T:2048], axis=AX.X,
                                    op=ALU.add)
            with e.If_cmp(r, 1, "IS_EQ"):
                for dc in range(DC):
                    e.tensor_reduce(out=A.r1part[:, dc:dc + 1],
                                    in_=xln1(dc)[:, 0:Q1T - 2048], axis=AX.X,
                                    op=ALU.add)
        return e.tensor_copy(out=A.rowc[0:1, 0:1], in_=A.r1part[0:1, 0:1])
    ve_r1 = p.step("vector", "ve", f_r1, waits=[("ve", ve_n)])

    tgt_r1 = p.inc("so", 16)

    def f_wr1(e):
        e.wait_ge(p.sems["ve"], ve_r1)
        with e.register() as r:
            e.reg_load(r, A.ids_s[0:1, 0:1])
            for pp in (0, 1):
                with e.If_cmp(r, pp, "IS_EQ"):
                    e.dma_start(out=A.sh_r1[pp],
                                in_=A.r1part[:]).then_inc(p.sems["so"], 16)
    p.emit("sync", f_wr1)
    write_flag(A, {0: 0, 1: 1}, [("so", tgt_r1)])

    se_t2, t2 = emit_t2(A, xln1, OWN, 136 * KB, extra_gates=[("ve", ve_n)])

    spin_barrier(A, [0, 1], "pb0")
    tgt_m = p.inc("si", 16)

    def f_rdm(e):
        with e.register() as r:
            e.reg_load(r, A.ids_s[0:1, 0:1])
            for pp in (0, 1):
                with e.If_cmp(r, pp, "IS_EQ"):
                    e.dma_start(out=A.r1mate[:],
                                in_=A.sh_r1[1 - pp]).then_inc(p.sems["si"], 16)
    p.emit("sync", f_rdm)
    p.step("vector", "ve",
           lambda e: e.tensor_tensor(out=A.q2col[:], in0=A.r1part[:],
                                     in1=A.r1mate[:], op=ALU.add),
           waits=[("si", tgt_m)])
    ve_bt = router_pipeline(A, A.q2col, 1.0 / NROUT,
                            extra_gates=[("se", se_t2)])

    A.SLAB_OFF = 140 * KB
    A.OST_OFF = 156 * KB
    A.OST_DT = BF16

    def resid_loader(oc, gates):
        slot = kx[0] % 2
        kx[0] += 1
        sem = p.dma("gpsimd", "x" + str(slot), xbf[slot][:],
                    d["xT"][oc * 128:(oc + 1) * 128, :], waits=gates)
        return xbf[slot], ("x" + str(slot), sem)

    def out_writer(oc, blk, src, vsem):
        semn = "oA" + str((oc * 4 + blk) % 2)
        tgt = p.inc(semn, 16)

        def f(e, oc=oc, blk=blk, src=src, vsem=vsem, semn=semn):
            e.wait_ge(p.sems[vsem[0]], vsem[1])
            with e.register() as r:
                e.reg_load(r, A.ids_s[0:1, 0:1])
                for pp in (0, 1):
                    with e.If_cmp(r, pp, "IS_EQ"):
                        c0 = pp * 2048 + blk * 512
                        e.dma_start(
                            out=A.sh_x1[oc * 128:(oc + 1) * 128, c0:c0 + 512],
                            in_=src[:]).then_inc(p.sems[semn], 16)
        p.emit("sync", f)
        return (semn, tgt)

    ffn_base(A, lambda oc: d["wt_blk"][oc], xln1, OWN, t2, resid_loader,
             out_writer, first_gates=[("ve", ve_bt)])
    A.x1_done = [("oA0", p.cur("oA0")), ("oA1", p.cur("oA1"))]


def phase2(A):
    p, d = A.p, A.d
    x1bf = [A.carve(128 * KB + i * 4 * KB, OWN, BF16) for i in range(2)]
    mub = A.carve(136 * KB, OWN, F32)
    rbb = A.carve(144 * KB, OWN, F32)

    def xln2o(dc):
        return A.carve(dc * 4 * KB, OWN, BF16)

    kx = [0]

    def mk_loader(h):
        def loader(dc, gates):
            slot = kx[0] % 2
            kx[0] += 1
            cols = OWN if h is None else 1024
            tgt = p.inc("h" + str(slot), 16)
            s = p.sems["h" + str(slot)]
            wsl = [(p.sems[n], v) for n, v in gates] + \
                  [(p.sems[n], v) for n, v in A.x1_done]

            def f(e, dc=dc, slot=slot, h=h, s=s, wsl=wsl, cols=cols):
                for sem, v in wsl:
                    e.wait_ge(sem, v)
                with e.register() as r:
                    e.reg_load(r, A.ids_s[0:1, 0:1])
                    for pp in (0, 1):
                        with e.If_cmp(r, pp, "IS_EQ"):
                            c0 = pp * 2048 + (0 if h is None else h * 1024)
                            e.dma_start(out=x1bf[slot][:, 0:cols],
                                        in_=A.sh_x1[dc * 128:(dc + 1) * 128,
                                                    c0:c0 + cols]).then_inc(s, 16)
            p.emit("sync", f)
            return x1bf[slot][:, 0:cols], ("h" + str(slot), tgt)
        return loader

    se_bc = None
    for h in range(2):
        se_bc = ln_block(A, 1024, mk_loader(h), 152 * KB,
                         mub[:, h * 1024:(h + 1) * 1024],
                         rbb[:, h * 1024:(h + 1) * 1024])
    ve_n = ln_normalize(A, OWN, mk_loader(None), mub, rbb, xln2o, 152 * KB,
                        se_bc)

    kslab = [A.carve(136 * KB + i * 8 * KB, DC * 128, BF16) for i in range(2)]
    rs1 = A.carve(152 * KB, 512, F32)
    rs2 = A.carve(154 * KB, 512, F32)
    kvst = [A.carve(156 * KB + i * KB, 512, BF16) for i in range(2)]
    vtst = [A.carve(158 * KB + i * 256, 128, BF16) for i in range(2)]
    te_oc = {}
    st_hist = {}

    for oc in range(NKV):
        sg = [("te", te_oc[oc - 2])] if oc >= 2 else \
            [("ve", ve_n), ("te", p.cur("te"))]
        gsem = p.dma("gpsimd", "g" + str(oc % 2), kslab[oc % 2][:],
                     d["wk_blk"][oc], waits=sg)
        for blk in range(4):
            sl = slice(blk * 512, (blk + 1) * 512)
            psu = A.ps0 if blk % 2 == 0 else A.ps1

            def f_mm(e, sl=sl, psu=psu, slab=kslab[oc % 2]):
                last = None
                for dc in range(DC):
                    last = e.matmul(psu[:], slab[:, dc * 128:(dc + 1) * 128],
                                    xln2o(dc)[:, sl], start=(dc == 0), stop=True)
                return last
            te_b = p.step("tensor", "te", f_mm,
                          waits=[("g" + str(oc % 2), gsem),
                                 ("ve", p.cur("ve"))])
            if blk == 3:
                te_oc[oc] = te_b
            it = oc * 4 + blk
            wts_r = [("te", te_b)]
            if it >= 2:
                wts_r.append(st_hist[it - 2])

            def f_rope(e, sl=sl, psu=psu, st=kvst[blk % 2]):
                e.tensor_tensor(out=rs1[0:64, :], in0=psu[0:64, :],
                                in1=A.cs_cos[:, sl], op=ALU.mult)
                e.tensor_tensor(out=rs2[0:64, :], in0=psu[64:128, :],
                                in1=A.cs_sin[:, sl], op=ALU.mult)
                e.tensor_tensor(out=st[0:64, :], in0=rs1[0:64, :], in1=rs2[0:64, :],
                                op=ALU.subtract)
                e.tensor_tensor(out=rs1[0:64, :], in0=psu[0:64, :],
                                in1=A.cs_sin[:, sl], op=ALU.mult)
                e.tensor_tensor(out=rs2[0:64, :], in0=psu[64:128, :],
                                in1=A.cs_cos[:, sl], op=ALU.mult)
                return e.tensor_tensor(out=st[64:128, :], in0=rs1[0:64, :],
                                       in1=rs2[0:64, :], op=ALU.add)
            ve_r = p.step("vector", "ve", f_rope, waits=wts_r)
            semn = "oK" + str(blk % 2)
            tgt = p.inc(semn, 16)

            def wf(e, oc=oc, blk=blk, st=kvst[blk % 2], ve_r=ve_r, semn=semn):
                e.wait_ge(p.sems["ve"], ve_r)
                with e.register() as r:
                    e.reg_load(r, A.ids_s[0:1, 0:1])
                    for pp in (0, 1):
                        with e.If_cmp(r, pp, "IS_EQ"):
                            c0 = pp * 2048 + blk * 512
                            e.dma_start(
                                out=A.sh_k[oc * 128:(oc + 1) * 128, c0:c0 + 512],
                                in_=st[:]).then_inc(p.sems[semn], 16)
            p.emit("sync", wf)
            st_hist[it] = (semn, tgt)

    stv_hist = {}
    for oc in range(NKV):
        koc = NKV + oc
        sg = [("te", te_oc[koc - 2])]
        gsem = p.dma("gpsimd", "g" + str(koc % 2), kslab[koc % 2][:],
                     d["wv_blk"][oc], waits=sg)
        for blk in range(4):
            sl = slice(blk * 512, (blk + 1) * 512)
            psu = A.ps0 if blk % 2 == 0 else A.ps1

            def f_mm(e, sl=sl, psu=psu, slab=kslab[koc % 2]):
                last = None
                for dc in range(DC):
                    last = e.matmul(psu[:], slab[:, dc * 128:(dc + 1) * 128],
                                    xln2o(dc)[:, sl], start=(dc == 0), stop=True)
                return last
            te_b = p.step("tensor", "te", f_mm,
                          waits=[("g" + str(koc % 2), gsem),
                                 ("se", p.cur("se"))])
            if blk == 3:
                te_oc[koc] = te_b
            it = oc * 4 + blk
            wts_s = [("te", te_b), ("ve", p.cur("ve"))]
            if it >= 2:
                wts_s.append(st_hist[it - 2])
            else:
                wts_s += [("oK0", p.cur("oK0")), ("oK1", p.cur("oK1"))]
            se_v = p.step("scalar", "se",
                          lambda e, psu=psu, st=kvst[blk % 2]:
                          e.copy(out=st[:], in_=psu[:]),
                          waits=wts_s)
            st_hist[it] = ("se", se_v)
            for tt in range(4):
                itv = it * 4 + tt
                wts_t = [("se", se_v), ("ve", p.cur("ve"))]
                te_t = p.step("tensor", "te",
                              lambda e, st=kvst[blk % 2], tt=tt:
                              e.transpose(A.ps_t[:, 0:128],
                                          st[:, tt * 128:(tt + 1) * 128],
                                          A.id_bf[:]),
                              waits=wts_t)
                wts_c = [("te", te_t)]
                if itv >= 2:
                    wts_c.append(stv_hist[itv - 2])
                ve_t = p.step("vector", "ve",
                              lambda e, tt=tt:
                              e.tensor_copy(out=vtst[tt % 2][:],
                                            in_=A.ps_t[:, 0:128]),
                              waits=wts_c)
                semn = "oV" + str(tt % 2)
                tgt = p.inc(semn, 16)

                def wf(e, oc=oc, blk=blk, tt=tt, ve_t=ve_t,
                       st=vtst[tt % 2], semn=semn):
                    e.wait_ge(p.sems["ve"], ve_t)
                    with e.register() as r:
                        e.reg_load(r, A.ids_s[0:1, 0:1])
                        for pp in (0, 1):
                            with e.If_cmp(r, pp, "IS_EQ"):
                                r0 = pp * 2048 + blk * 512 + tt * 128
                                e.dma_start(
                                    out=A.sh_v[r0:r0 + 128,
                                               oc * 128:(oc + 1) * 128],
                                    in_=st[:]).then_inc(p.sems[semn], 16)
                p.emit("sync", wf)
                stv_hist[itv] = (semn, tgt)

    fl_waits = [("oK0", p.cur("oK0")), ("oK1", p.cur("oK1")),
                ("oV0", p.cur("oV0")), ("oV1", p.cur("oV1"))] + A.x1_done
    write_flag(A, {0: 2, 1: 3}, fl_waits)


def phase3(A):
    p, d = A.p, A.d
    spin_barrier(A, [2, 3], "pb1")
    b1 = p.fence([], name="sy")

    x1bf = [A.carve(64 * KB + i * 2 * KB, 1024, BF16) for i in range(2)]
    mub = A.carve(72 * KB, 1024, F32)
    rbb = A.carve(76 * KB, 1024, F32)
    wqs = [A.carve(80 * KB + i * 8 * KB, DC * 128, BF16) for i in range(2)]
    khs = [A.carve(96 * KB + i * 8 * KB, 4096, BF16) for i in range(2)]
    vhs = [A.carve(112 * KB + i * 8 * KB, 4096, BF16) for i in range(2)]
    qss = [A.carve(128 * KB + i * 2 * KB, 1024, BF16) for i in range(2)]
    pbf = [A.carve(132 * KB + i * 2 * KB, 1024, BF16) for i in range(2)]
    rs1 = A.carve(136 * KB, 512, F32)
    rs2 = A.carve(138 * KB, 512, F32)
    ctxf = A.carve(140 * KB, 512, F32)
    ctxb = [A.carve(142 * KB + i * KB, 512, BF16) for i in range(2)]

    def xln2q(dc):
        return A.carve(dc * 2 * KB, 1024, BF16)

    kx = [0]
    last_exp_psX = {0: None, 1: None}
    last_av_pbf = {0: None, 1: None}
    last_block_end = {"ve_rc": None, "se_cx": None, "ve_cx": None}
    ctxst = {}
    nst = [0]

    for qh_half in ([0] if _ATT1 else range(2)):
        kmax = 16 if qh_half == 0 else 32

        def loader(dc, gates):
            slot = kx[0] % 2
            kx[0] += 1
            tgt = p.inc("h" + str(slot), 16)
            s = p.sems["h" + str(slot)]
            wsl = [(p.sems[n], v) for n, v in gates]

            def f(e, dc=dc, slot=slot, s=s, wsl=wsl, qh_half=qh_half):
                for sem, v in wsl:
                    e.wait_ge(sem, v)
                e.wait_ge(p.sems["sy"], b1)
                src = A.sh_x1[dc * 128:(dc + 1) * 128,
                              2048 * qh_half:2048 * (qh_half + 1)]
                src = src.rearrange("p (s two i) -> p s two i", two=2, i=128)
                with e.register() as r:
                    e.reg_load(r, A.ids_s[0:1, 0:1])
                    for pp in (0, 1):
                        with e.If_cmp(r, pp, "IS_EQ"):
                            e.dma_start(
                                out=x1bf[slot][:].rearrange(
                                    "p (s i) -> p s i", i=128),
                                in_=src[:, :, pp, :]).then_inc(s, 16)
            p.emit("sync", f)
            return x1bf[slot], ("h" + str(slot), tgt)

        se_bc = ln_block(A, 1024, loader, 68 * KB, mub, rbb)
        ve_n = ln_normalize(A, 1024, loader, mub, rbb, xln2q, 68 * KB, se_bc)

        te_h_done = {}
        for h in ([0] if _ATT1 else range(NKV)):
            kg = [("te", te_h_done[h - 2])] if h >= 2 else \
                [("te", p.cur("te")), ("se", p.cur("se"))]
            ksem = ("K" + str(h % 2),
                    p.dma("sync", "K" + str(h % 2), khs[h % 2][:, 0:kmax * 128],
                          A.sh_k[h * 128:(h + 1) * 128, 0:kmax * 128],
                          waits=kg + [("sy", b1)]))
            vsem = ("V" + str(h % 2),
                    p.dma("sync", "V" + str(h % 2),
                          vhs[h % 2][:, 0:kmax * 128].rearrange(
                              "p (tc v) -> p tc v", v=128),
                          A.sh_v[0:kmax * 128,
                                 h * 128:(h + 1) * 128].rearrange(
                              "(tc p) v -> p tc v", p=128),
                          waits=kg + [("sy", b1)]))
            kh = khs[h % 2]
            vh = vhs[h % 2]
            for s4 in ([0] if _ATT1 else range(4)):
                qh = 4 * h + s4
                qsl = qss[qh % 2]
                wg = [("ve", p.cur("ve"))]
                wqsem = ("g" + str(qh % 2),
                         p.dma("gpsimd", "g" + str(qh % 2), wqs[qh % 2][:],
                               d["wq_blk"][qh],
                               waits=[("te", p.cur("te"))] + (
                                   [("ve", ve_n)] if qh < 2 and qh_half == 0
                                   else [])))
                ve_rope = None
                for qb in range(2):
                    sl = slice(qb * 512, (qb + 1) * 512)

                    def f_q(e, sl=sl, slab=wqs[qh % 2]):
                        last = None
                        for dc in range(DC):
                            last = e.matmul(A.ps2[:],
                                            slab[:, dc * 128:(dc + 1) * 128],
                                            xln2q(dc)[:, sl],
                                            start=(dc == 0), stop=True)
                        return last
                    te_q = p.step("tensor", "te", f_q,
                                  waits=[wqsem, ("ve", p.cur("ve"))])
                    csl = slice(qh_half * 1024 + qb * 512,
                                qh_half * 1024 + qb * 512 + 512)

                    def f_rope(e, sl=sl, csl=csl, qsl=qsl):
                        e.tensor_tensor(out=rs1[0:64, :], in0=A.ps2[0:64, :],
                                        in1=A.csq_cos[:, csl], op=ALU.mult)
                        e.tensor_tensor(out=rs2[0:64, :], in0=A.ps2[64:128, :],
                                        in1=A.csq_sin[:, csl], op=ALU.mult)
                        e.tensor_tensor(out=qsl[0:64, sl], in0=rs1[0:64, :],
                                        in1=rs2[0:64, :], op=ALU.subtract)
                        e.tensor_tensor(out=rs1[0:64, :], in0=A.ps2[0:64, :],
                                        in1=A.csq_sin[:, csl], op=ALU.mult)
                        e.tensor_tensor(out=rs2[0:64, :], in0=A.ps2[64:128, :],
                                        in1=A.csq_cos[:, csl], op=ALU.mult)
                        e.tensor_tensor(out=qsl[64:128, sl], in0=rs1[0:64, :],
                                        in1=rs2[0:64, :], op=ALU.add)
                        return e.tensor_copy(out=A.rowa[0:1, 0:1],
                                             in_=qsl[64:65, 0:1])
                    ve_rope = p.step("vector", "ve", f_rope,
                                     waits=[("te", te_q)])

                for mb in ([0] if _ATT1 else range(2)):
                    m0g = 8 * qh_half + 4 * mb
                    qcols = slice(mb * 512, (mb + 1) * 512)
                    first = [True]

                    def block_first_gates():
                        g = []
                        if last_block_end["ve_rc"] is not None:
                            g.append(("ve", last_block_end["ve_rc"]))
                        if last_block_end["se_cx"] is not None:
                            g.append(("se", last_block_end["se_cx"]))
                        return g

                    def do_tile_s(kt, par, N, qoff, masked, z):
                        psX = A.psW if par == 0 else A.psW2
                        pb_ = pbf[par]
                        wts = [("ve", ve_rope)]
                        if first[0]:
                            wts.append(ksem)
                        if last_exp_psX[par] is not None:
                            wts.append(("se", last_exp_psX[par]))
                        if N == 512 and qoff == 0 and kt + 1 <= kmax - 1 and \
                                not masked and kt % 2 == 0 and z is None:
                            def f_sa(e, kt=kt, psX=psX, kh=kh, qsl=qsl,
                                     qcols=qcols):
                                return e.matmul(psX[:, 0:512],
                                                kh[:, kt * 128:(kt + 1) * 128],
                                                qsl[:, qcols], start=True,
                                                stop=True)
                            te_sa = p.step("tensor", "te", f_sa, waits=wts)

                            def f_sb(e, kt=kt, psX=psX, kh=kh, qsl=qsl,
                                     qcols=qcols):
                                return e.matmul(
                                    psX[:, 512:1024],
                                    kh[:, (kt + 1) * 128:(kt + 2) * 128],
                                    qsl[:, qcols], start=True, stop=True)
                            te_sb = p.step("tensor", "te", f_sb, waits=[])
                            w2a = [("te", te_sa)]
                            if last_av_pbf[par] is not None:
                                w2a.append(("te", last_av_pbf[par]))
                            se_ea = p.step(
                                "scalar", "se",
                                lambda e, psX=psX, pb_=pb_:
                                e.activation(out=pb_[:, 0:512],
                                             in_=psX[:, 0:512],
                                             func=AF.Exp, scale=ISQ),
                                waits=w2a)
                            se_eb = p.step(
                                "scalar", "se",
                                lambda e, psX=psX, pb_=pb_:
                                e.activation(out=pb_[:, 512:1024],
                                             in_=psX[:, 512:1024],
                                             func=AF.Exp, scale=ISQ),
                                waits=[("te", te_sb)])
                            last_exp_psX[par] = se_eb
                            first[0] = False
                            return dict(kt=kt, par=par, wid=1024, qoff=0,
                                        dep=("se", se_ea),
                                        dep_b=("se", se_eb))
                        else:
                            def f_s(e, kt=kt, psX=psX, N=N, qoff=qoff,
                                    kh=kh, qsl=qsl, mb=mb):
                                return e.matmul(
                                    psX[:, 0:N],
                                    kh[:, kt * 128:(kt + 1) * 128],
                                    qsl[:, mb * 512 + qoff:(mb + 1) * 512],
                                    start=True, stop=True)
                            te_s = p.step("tensor", "te", f_s, waits=wts)
                            wid = N
                        wts2 = [("te", te_s)]
                        if last_av_pbf[par] is not None:
                            wts2.append(("te", last_av_pbf[par]))
                        se_e = p.step("scalar", "se",
                                      lambda e, psX=psX, pb_=pb_, wid=wid:
                                      e.activation(out=pb_[:, 0:wid],
                                                   in_=psX[:, 0:wid],
                                                   func=AF.Exp, scale=ISQ),
                                      waits=wts2)
                        last_exp_psX[par] = se_e
                        dep = ("se", se_e)
                        if masked:
                            mask = A.mA_s if z % 2 == 0 else A.mB_s

                            def f_m(e, pb_=pb_, mask=mask):
                                e.tensor_tensor(out=pb_[:, 0:128],
                                                in0=pb_[:, 0:128],
                                                in1=mask[:], op=ALU.mult)
                                return e.tensor_copy(out=A.rowa[0:1, 1:2],
                                                     in_=pb_[0:1, 0:1])
                            ve_m = p.step("vector", "ve", f_m,
                                          waits=[("se", se_e)])
                            dep = ("ve", ve_m)
                        first[0] = False
                        return dict(kt=kt, par=par, wid=wid, qoff=qoff,
                                    dep=dep)

                    def do_tile_av(u):
                        wts3 = [u["dep"]]
                        if firstav[0]:
                            wts3.append(vsem)
                            wts3 += block_first_gates()
                        if "dep_b" in u:
                            def f_ava(e, kt=u["kt"], pb_=pbf[u["par"]],
                                      vh=vh, st=firstav[0]):
                                e.matmul(A.ps1[0:1, :], A.ones_col[:],
                                         pb_[:, 0:512], start=st, stop=True)
                                return e.matmul(
                                    A.ps0[:], vh[:, kt * 128:(kt + 1) * 128],
                                    pb_[:, 0:512], start=st, stop=True)
                            p.step("tensor", "te", f_ava, waits=wts3)

                            def f_avb(e, kt=u["kt"], pb_=pbf[u["par"]],
                                      vh=vh):
                                e.matmul(A.ps1[0:1, :], A.ones_col[:],
                                         pb_[:, 512:1024], start=False,
                                         stop=True)
                                return e.matmul(
                                    A.ps0[:],
                                    vh[:, (u["kt"] + 1) * 128:(u["kt"] + 2) * 128],
                                    pb_[:, 512:1024], start=False, stop=True)
                            te_av = p.step("tensor", "te", f_avb,
                                           waits=[u["dep_b"]])
                        else:
                            def f_av(e, kt=u["kt"], pb_=pbf[u["par"]],
                                     wid=u["wid"], qoff=u["qoff"],
                                     vh=vh, st=firstav[0]):
                                e.matmul(A.ps1[0:1, qoff:512], A.ones_col[:],
                                         pb_[:, 0:wid], start=st, stop=True)
                                return e.matmul(A.ps0[:, qoff:512],
                                                vh[:, kt * 128:(kt + 1) * 128],
                                                pb_[:, 0:wid], start=st,
                                                stop=True)
                            te_av = p.step("tensor", "te", f_av, waits=wts3)
                        last_av_pbf[u["par"]] = te_av
                        firstav[0] = False
                        return te_av

                    par = 0
                    te_av = None
                    pend = None
                    firstav = [True]
                    for g in range(m0g):
                        u = do_tile_s(2 * g, par, 512, 0, False, None)
                        if pend is not None:
                            te_av = do_tile_av(pend)
                        pend = u
                        par ^= 1
                    for z in range(8):
                        kt = 2 * m0g + z
                        qoff = (z // 2) * 128
                        u = do_tile_s(kt, par, 512 - qoff, qoff, True, z)
                        if pend is not None:
                            te_av = do_tile_av(pend)
                        pend = u
                        par ^= 1
                    te_av = do_tile_av(pend)
                    te_h_done[h] = te_av

                    def f_rc(e):
                        e.reciprocal(out=A.rrow[:], in_=A.ps1[0:1, :])
                        return e.tensor_copy(out=A.rowa[0:1, 2:3],
                                             in_=A.rrow[0:1, 511:512])
                    ve_rc = p.step("vector", "ve", f_rc,
                                   waits=[("te", te_av)])
                    te_bc = p.step("tensor", "te",
                                   lambda e: e.matmul(A.ps2[:], A.ones_row[:],
                                                      A.rrow[:], start=True,
                                                      stop=True),
                                   waits=[("ve", ve_rc)])
                    se_cx = p.step("scalar", "se",
                                   lambda e: e.copy(out=ctxf[:], in_=A.ps0[:]),
                                   waits=[("te", te_bc)])
                    cslot = nst[0] % 2
                    wts_cx = [("se", se_cx)]
                    if nst[0] >= 2:
                        wts_cx.append(ctxst[nst[0] - 2])
                    def f_cx(e, cb=ctxb[cslot]):
                        e.tensor_tensor(out=cb[:], in0=ctxf[:],
                                        in1=A.ps2[:], op=ALU.mult)
                        return e.tensor_copy(out=A.rowa[0:1, 3:4],
                                             in_=cb[0:1, 511:512])
                    ve_cx = p.step("vector", "ve", f_cx, waits=wts_cx)
                    semn = "oC" + str(cslot)
                    tgt = p.dma("sync", semn,
                                A.pr_ctx[qh * 128:(qh + 1) * 128,
                                         qh_half * 1024 + mb * 512:
                                         qh_half * 1024 + mb * 512 + 512],
                                ctxb[cslot][:], waits=[("ve", ve_cx)])
                    ctxst[nst[0]] = (semn, tgt)
                    nst[0] += 1
                    last_block_end["ve_rc"] = ve_rc
                    last_block_end["se_cx"] = se_cx

    if _ATT1:
        A.qss0 = qss[0]
        A.khs0 = khs[0]
        A.vhs0 = vhs[0]
        A.ctxf0 = ctxf
        A.rrow0 = A.rrow
        A.ctxb0 = ctxb[0]
        A.pbf0 = pbf[0]
        A.pbf1 = pbf[1]
        A.xq0 = xln2q(0)
        return
    phase3_wo(A)


def phase3_wo(A):
    p, d = A.p, A.d
    slabs = [[A.carve((g * 32 + i * 8) * KB, DC * 128, BF16) for i in range(4)]
             for g in range(2)]
    ctxblk = [A.carve(64 * KB + i * 32 * KB, 32 * 512, BF16) for i in range(2)]
    x1res = [A.carve(128 * KB + i * 4 * KB, 2048, BF16) for i in range(2)]
    x2st = [A.carve(136 * KB + i * 4 * KB, 2048, BF16) for i in range(2)]
    banks = [A.ps0[:, 0:512], A.ps1[:, 0:512], A.psW[:, 0:512],
             A.psW[:, 512:1024]]

    te_g = {}
    ve_it = {}
    st_it = {}
    for ocg in range(8):
        par = ocg % 2
        sg = [("te", te_g[ocg - 2])] if ocg >= 2 else p.barrier_gates()
        gsems = []
        for i in range(4):
            gsems.append(("g" + str(par),
                          p.dma("gpsimd", "g" + str(par), slabs[par][i][:],
                                d["wo_blk"][ocg * 4 + i],
                                waits=(sg if i == 0 else []))))
        for blk in range(4):
            it = ocg * 4 + blk
            cslot = it % 2
            cg = [("te", te_g[("c", it - 2)])] if it >= 2 else p.barrier_gates()
            csem = ("c" + str(cslot),
                    p.dma("sync", "c" + str(cslot),
                          ctxblk[cslot][:].rearrange("p (dcc i) -> p dcc i",
                                                     i=512),
                          A.pr_ctx[:, blk * 512:(blk + 1) * 512].rearrange(
                              "(dcc p) i -> p dcc i", p=128),
                          waits=cg))
            rg = [("ve", ve_it[it - 2])] if it >= 2 else p.barrier_gates()
            rtgt = p.inc("r" + str(cslot), 64)
            wsl = [(p.sems[n], v) for n, v in rg]

            def f_rr(e, ocg=ocg, blk=blk, cslot=cslot, wsl=wsl):
                for sem, v in wsl:
                    e.wait_ge(sem, v)
                with e.register() as r:
                    e.reg_load(r, A.ids_s[0:1, 0:1])
                    for pp in (0, 1):
                        with e.If_cmp(r, pp, "IS_EQ"):
                            for oci in range(4):
                                oc = ocg * 4 + oci
                                s3 = A.sh_x1[oc * 128:(oc + 1) * 128,
                                             blk * 1024:(blk + 1) * 1024]
                                s3 = s3.rearrange("p (s two i) -> p s two i",
                                                  two=2, i=128)
                                e.dma_start(
                                    out=x1res[cslot][:, oci * 512:(oci + 1) * 512]
                                    .rearrange("p (s i) -> p s i", i=128),
                                    in_=s3[:, :, pp, :]).then_inc(
                                    p.sems["r" + str(cslot)], 16)
            p.emit("sync", f_rr)
            rsem = ("r" + str(cslot), rtgt)

            te_last = None
            for i in range(4):
                def f_mm(e, i=i, cslot=cslot, par=par):
                    last = None
                    for dcc in range(DC):
                        last = e.matmul(
                            banks[i], slabs[par][i][:, dcc * 128:(dcc + 1) * 128],
                            ctxblk[cslot][:, dcc * 512:(dcc + 1) * 512],
                            start=(dcc == 0), stop=True)
                    return last
                wts = [gsems[i], csem, ("ve", p.cur("ve"))]
                te_last = p.step("tensor", "te", f_mm, waits=wts)
            te_g[("c", it)] = te_last
            if blk == 3:
                te_g[ocg] = te_last

            ve_last = None
            for i in range(4):
                wts_v = [("te", te_last), rsem]
                if it >= 2:
                    wts_v.append(st_it[it - 2])
                ve_last = p.step("vector", "ve",
                                 lambda e, i=i, cslot=cslot:
                                 e.tensor_tensor(
                                     out=x2st[cslot][:, i * 512:(i + 1) * 512],
                                     in0=banks[i],
                                     in1=x1res[cslot][:, i * 512:(i + 1) * 512],
                                     op=ALU.add),
                                 waits=wts_v)
            ve_it[it] = ve_last

            semn = "oX" + str(cslot)
            tgt0 = p.cur(semn)
            for i in range(4):
                oc = ocg * 4 + i
                t = p.inc(semn, 16)

                def wf(e, oc=oc, blk=blk, i=i, cslot=cslot, ve_last=ve_last,
                       semn=semn):
                    e.wait_ge(p.sems["ve"], ve_last)
                    with e.register() as r:
                        e.reg_load(r, A.ids_s[0:1, 0:1])
                        for pp in (0, 1):
                            with e.If_cmp(r, pp, "IS_EQ"):
                                c0 = pp * 2048 + blk * 512
                                e.dma_start(
                                    out=A.sh_x2[oc * 128:(oc + 1) * 128,
                                                c0:c0 + 512],
                                    in_=x2st[cslot][:, i * 512:(i + 1) * 512]
                                ).then_inc(p.sems[semn], 16)
                p.emit("sync", wf)
            st_it[it] = (semn, p.cur(semn))

    write_flag(A, {0: 4, 1: 5},
               [("oX0", p.cur("oX0")), ("oX1", p.cur("oX1"))])


def phase4(A):
    p, d = A.p, A.d
    spin_barrier(A, [4, 5], "pb2")
    b2 = p.fence([], name="sy")

    x2bf = [A.carve(i * 8 * KB, T, BF16) for i in range(2)]
    sq = A.carve(16 * KB, T, BF16)
    mu3 = A.carve(24 * KB, T, F32, rows=1)
    r3 = A.carve(40 * KB, T, F32, rows=1)
    wrow = A.carve(56 * KB, T, F32, rows=1)
    mubq = A.carve(72 * KB, 1024, F32)
    rbbq = A.carve(76 * KB, 1024, F32)
    wrowb = A.carve(96 * KB, T, F32)
    tmpf2 = A.carve(112 * KB, T, F32)
    trow = A.carve(88 * KB, T, F32, rows=1)

    srow = [A.ps0[0:1, :], A.ps0[32:33, :], A.ps0[64:65, :],
            A.ps1[0:1, :], A.ps1[32:33, :], A.ps1[64:65, :],
            A.ps2[0:1, :], A.ps2[32:33, :]]
    qrow = [A.psW[0:1, 0:512], A.psW[32:33, 0:512], A.psW[64:65, 0:512],
            A.psW[0:1, 512:1024], A.psW[32:33, 512:1024],
            A.psW[64:65, 512:1024],
            A.psW2[0:1, 0:512], A.psW2[32:33, 0:512]]
    kx = [0]
    te_hist = {}
    for dc in range(DC):
        slot = kx[0] % 2
        kx[0] += 1
        gates = [("te", te_hist[dc - 2])] if dc >= 2 else \
            p.barrier_gates() + [("sy", b2)]
        csem = p.dma("sync", "h" + str(slot), x2bf[slot][:],
                     A.sh_x2[dc * 128:(dc + 1) * 128, :], waits=gates)
        se_sq = p.step("scalar", "se",
                       lambda e, slot=slot:
                       e.activation(out=sq[:], in_=x2bf[slot][:],
                                    func=AF.Square),
                       waits=[("h" + str(slot), csem), ("te", p.cur("te"))])

        def f_mm(e, slot=slot, dc=dc):
            last = None
            for b in range(8):
                sl = slice(b * 512, (b + 1) * 512)
                e.matmul(srow[b], A.ones_col[:], x2bf[slot][:, sl],
                         start=(dc == 0), stop=True)
                last = e.matmul(qrow[b], A.ones_col[:], sq[:, sl],
                                start=(dc == 0), stop=True)
            return last
        wts = [("h" + str(slot), csem), ("se", se_sq)]
        if dc == 0:
            wts.append(("ve", p.cur("ve")))
        te_hist[dc] = p.step("tensor", "te", f_mm, waits=wts)
    te_mm = p.cur("te")
    ve_rows = None
    se1_prev = None
    for b in range(8):
        sl = slice(b * 512, (b + 1) * 512)

        def f_pp(e, b=b, sl=sl):
            e.tensor_scalar_mul(out=mu3[0:1, sl], in0=srow[b], scalar1=1.0 / D)
            e.tensor_scalar_mul(out=A.rowb[:], in0=qrow[b], scalar1=1.0 / D)
            e.tensor_tensor(out=A.rowc[:], in0=mu3[0:1, sl],
                            in1=mu3[0:1, sl], op=ALU.mult)
            return e.tensor_tensor(out=A.rowb[:], in0=A.rowb[:],
                                   in1=A.rowc[:], op=ALU.subtract)
        ppw = [("te", te_mm)]
        if se1_prev is not None:
            ppw.append(("se", se1_prev))
        ve1 = p.step("vector", "ve", f_pp, waits=ppw)
        se1 = p.step("scalar", "se",
                     lambda e: e.activation(out=A.rowb[:], in_=A.rowb[:],
                                            func=AF.Sqrt, bias=A.eps_s[:]),
                     waits=[("ve", ve1)])
        se1_prev = se1
        ve_rows = p.step("vector", "ve",
                         lambda e, sl=sl: e.reciprocal(out=r3[0:1, sl],
                                                       in_=A.rowb[:]),
                         waits=[("se", se1)])

    A.mu3_d = mu3
    A.r3_d = r3
    A.wrow_d = wrow
    if _P4N <= 1:
        return

    def f_w(e):
        e.tensor_copy(out=wrow[:], in_=r3[:])
        e.memset(wrow[0:1, 0:355], 0.0)
        e.memset(wrow[0:1, 2048:2304], 0.0)
        return e.memset(wrow[0:1, 4095:4096], 0.0)
    ve_w = p.step("vector", "ve", f_w, waits=[("ve", ve_rows)])

    def f_s1(e):
        e.tensor_tensor(out=trow[:], in0=mu3[:], in1=wrow[:], op=ALU.mult)
        return e.tensor_reduce(out=A.sc1[:], in_=trow[:], axis=AX.X,
                               op=ALU.add)
    ve_s1 = p.step("vector", "ve", f_s1, waits=[("ve", ve_w)])

    if _P4N <= 2:
        return
    se_wb = None
    for b in range(8):
        sl = slice(b * 512, (b + 1) * 512)
        te_b = p.step("tensor", "te",
                      lambda e, sl=sl:
                      e.matmul(A.psW2[:, 0:512], A.ones_row[:], wrow[0:1, sl],
                               start=True, stop=True),
                      waits=[("ve", ve_w), ("se", p.cur("se"))])
        se_wb = p.step("scalar", "se",
                       lambda e, sl=sl: e.copy(out=wrowb[:, sl],
                                               in_=A.psW2[:, 0:512]),
                       waits=[("te", te_b)])
    ve_q2 = None
    ve_q2h = {}
    for dc in range(DC):
        slot = kx[0] % 2
        kx[0] += 1
        gates = [("ve", ve_q2h[dc - 2])] if dc >= 2 else [("se", se_wb)]
        csem = p.dma("sync", "h" + str(slot), x2bf[slot][:],
                     A.sh_x2[dc * 128:(dc + 1) * 128, :], waits=gates)

        def f_q2(e, slot=slot, dc=dc):
            e.tensor_tensor(out=tmpf2[:], in0=x2bf[slot][:],
                            in1=wrowb[:], op=ALU.mult)
            return e.tensor_reduce(out=A.q2col[:, dc:dc + 1],
                                   in_=tmpf2[:], axis=AX.X, op=ALU.add)
        ve_q2 = p.step("vector", "ve", f_q2,
                       waits=[("h" + str(slot), csem), ("se", se_wb)])
        ve_q2h[dc] = ve_q2

    if _P4N <= 3:
        return
    te_s1 = p.step("tensor", "te",
                   lambda e: e.matmul(A.ps2[:, 0:1], A.ones_row[:], A.sc1[:],
                                      start=True, stop=True),
                   waits=[("ve", ve_s1), ("ve", ve_q2), ("se", p.cur("se"))])

    def f_sub(e):
        e.tensor_copy(out=A.s1_col[:], in_=A.ps2[:, 0:1])
        return e.tensor_scalar(out=A.q2col[:], in0=A.q2col[:],
                               scalar1=A.s1_col[:], scalar2=None,
                               op0=ALU.subtract)
    p.step("vector", "ve", f_sub, waits=[("te", te_s1)])
    ve_bt = router_pipeline(A, A.q2col, 1.0 / NROUT)

    if _P4N <= 4:
        return
    QOFF = [0, 1024, 2048, 3072]

    def xln3(dc):
        return A.carve(96 * KB + dc * 2 * KB, 1024, BF16)

    murq = A.carve(96 * KB, 2048, F32, rows=1)
    tgt_mq = p.inc("si", 32)
    wsl_mq = [(p.sems["ve"], p.cur("ve")), (p.sems["se"], p.cur("se"))]

    def f_mq(e, wsl=wsl_mq):
        for sem, v in wsl:
            e.wait_ge(sem, v)
        with e.register() as r:
            e.reg_load(r, A.ids_s[0:1, 4:5])
            for quad in range(4):
                with e.If_cmp(r, quad, "IS_EQ"):
                    q0 = QOFF[quad]
                    e.dma_start(out=murq[0:1, 0:1024],
                                in_=mu3[0:1, q0:q0 + 1024]).then_inc(
                        p.sems["si"], 16)
                    e.dma_start(out=murq[0:1, 1024:2048],
                                in_=r3[0:1, q0:q0 + 1024]).then_inc(
                        p.sems["si"], 16)
    p.emit("sync", f_mq)

    se_bcq = None
    for b in range(2):
        sl = slice(b * 512, (b + 1) * 512)

        def f_tb(e, b=b):
            e.matmul(A.psW2[:, 0:512], A.ones_row[:],
                     murq[0:1, b * 512:(b + 1) * 512], start=True, stop=True)
            return e.matmul(A.psW2[:, 512:1024], A.ones_row[:],
                            murq[0:1, 1024 + b * 512:1024 + (b + 1) * 512],
                            start=True, stop=True)
        tgt_t = p.step("tensor", "te", f_tb,
                       waits=[("si", tgt_mq), ("ve", ve_bt),
                              ("se", p.cur("se"))])

        def f_ev(e, sl=sl):
            e.copy(out=mubq[:, sl], in_=A.psW2[:, 0:512])
            return e.copy(out=rbbq[:, sl], in_=A.psW2[:, 512:1024])
        se_bcq = p.step("scalar", "se", f_ev, waits=[("te", tgt_t)])

    def loader_q(dc, gates):
        slot = kx[0] % 2
        kx[0] += 1
        tgt = p.inc("h" + str(slot), 16)
        s = p.sems["h" + str(slot)]
        wsl = [(p.sems[n], v) for n, v in gates]

        def f(e, dc=dc, slot=slot, s=s, wsl=wsl):
            for sem, v in wsl:
                e.wait_ge(sem, v)
            with e.register() as r:
                e.reg_load(r, A.ids_s[0:1, 4:5])
                for quad in range(4):
                    with e.If_cmp(r, quad, "IS_EQ"):
                        e.dma_start(out=x2bf[slot][:, 0:1024],
                                    in_=A.sh_x2[dc * 128:(dc + 1) * 128,
                                                QOFF[quad]:QOFF[quad] + 1024]
                                    ).then_inc(s, 16)
        p.emit("sync", f)
        return x2bf[slot][:, 0:1024], ("h" + str(slot), tgt)

    A.mubq_d = mubq
    A.rbbq_d = rbbq
    if _P4N <= 5:
        return
    ve_n3 = ln_normalize(A, 1024, loader_q, mubq, rbbq, xln3, 80 * KB, se_bcq)
    A.xln3_0 = xln3(0)
    if _P4N <= 6:
        return

    se_t2, t2q = emit_t2(A, xln3, 1024, 88 * KB, extra_gates=[("ve", ve_n3)])
    A.t2q_d = t2q
    if _P4N <= 7:
        return

    A.SLAB_OFF = 24 * KB
    A.OST_OFF = 40 * KB
    A.OST_DT = F32

    def resid_loader(oc, gates):
        slot = kx[0] % 2
        kx[0] += 1
        tgt = p.inc("h" + str(slot), 16)
        s = p.sems["h" + str(slot)]
        wsl = [(p.sems[n], v) for n, v in gates]

        def f(e, oc=oc, slot=slot, s=s, wsl=wsl):
            for sem, v in wsl:
                e.wait_ge(sem, v)
            with e.register() as r:
                e.reg_load(r, A.ids_s[0:1, 4:5])
                for quad in range(4):
                    with e.If_cmp(r, quad, "IS_EQ"):
                        e.dma_start(out=x2bf[slot][:, 0:1024],
                                    in_=A.sh_x2[oc * 128:(oc + 1) * 128,
                                                QOFF[quad]:QOFF[quad] + 1024]
                                    ).then_inc(s, 16)
        p.emit("sync", f)
        return x2bf[slot][:, 0:1024], ("h" + str(slot), tgt)

    def out_writer(oc, blk, src, vsem):
        semn = "oF" + str((oc * 2 + blk) % 2)
        tgt = p.dma("sync", semn,
                    A.out[oc * 128:(oc + 1) * 128, blk * 512:(blk + 1) * 512],
                    src[:], waits=[vsem])
        return (semn, tgt)

    ffn_base(A, lambda oc: d["wt_blk"][oc], xln3, 1024, t2q, resid_loader,
             out_writer, first_gates=[("se", se_t2)])
    A.out_done = [("oF0", p.cur("oF0")), ("oF1", p.cur("oF1"))]


def finish(A):
    p = A.p
    nc = A.nc
    for name in A._dbg_pending:
        s = getattr(A, name)
        sap = s if isinstance(s, bass.AP) else s[:]
        A.dbg[name] = (nc.dram_tensor("dbg_" + name, list(sap.shape),
                                      sap.dtype, kind="ExternalOutput").ap(),
                       sap)
    waitall = [(n, p.cur(n)) for n in list(p.ctr) if not n.startswith("pb")]
    ws0 = [(p.sem(n), v) for n, v in waitall]

    def f0(e, ws=ws0):
        for sem, v in ws:
            e.wait_ge(sem, v)
    p.emit("sync", f0)
    for name, (dst, src) in A.dbg.items():
        p.dma("sync", "so", dst, src)
    tot = p.cur("so")

    def f1(e, tot=tot):
        e.wait_ge(p.sems["so"], tot)
    p.emit("sync", f1)


def _prep_host(inputs):
    x = np.asarray(inputs["L0_attn_x"], np.float32)
    Wq = np.asarray(inputs["L0_attn_Wq"], np.float32)
    Wk = np.asarray(inputs["L0_attn_Wk"], np.float32)
    Wv = np.asarray(inputs["L0_attn_Wv"], np.float32)
    Wo = np.asarray(inputs["L0_attn_Wo"], np.float32)
    W = np.asarray(inputs["L0_ffn_W"], np.float32)
    bb = np.asarray(inputs["L0_ffn_b"], np.float32)
    Am = np.asarray(inputs["L0_ffn_A"], np.float32)
    Bm = np.asarray(inputs["L0_ffn_B"], np.float32)
    rW = np.asarray(inputs["L0_ffn_router_W"], np.float32)
    rb = np.asarray(inputs["L0_ffn_router_b"], np.float32)

    tri = np.ascontiguousarray(np.tril(np.ones((128, 128), np.float32)).T)
    shared = dict(
        wq_blk=_bk3(Wq), wo_blk=_bk3(Wo),
        wt_blk=_bk3(np.ascontiguousarray(W.T)),
        wk_blk=_bk3(Wk), wv_blk=_bk3(Wv),
        at_blk=np.ascontiguousarray(
            Am.reshape(128, D).T.reshape(DC, 128, 128).transpose(1, 0, 2)
        ).reshape(128, DC * 128),
        bmat=np.ascontiguousarray(
            np.concatenate([Bm[e].T for e in range(8)], axis=0)),
        rwT_blk=np.ascontiguousarray(
            rW.T.reshape(DC, 128, 8).transpose(1, 0, 2)).reshape(128, DC * 8),
        biasw=np.ascontiguousarray(bb.reshape(DC, 128).T),
        rbias=rb.reshape(8, 1).astype(np.float32),
        id128=np.eye(128, dtype=np.float32),
        e16=np.repeat(np.eye(8, dtype=np.float32) * 2.0, 16, axis=1),
        iot=np.arange(64, dtype=np.float32).reshape(1, 64),
    )
    ones = np.ones((128, 128), np.float32)
    zeros = np.zeros((128, 128), np.float32)
    nonce = np.array([[np.random.randint(1 << 20, 1 << 30)]], np.int32)

    in_maps = []
    for c in range(8):
        pp, ph, b = c % 2, (c // 2) % 2, c // 4
        own = slice(2048 * pp, 2048 * (pp + 1))
        posq = np.concatenate(
            [np.arange(128 * (2 * m + pp), 128 * (2 * m + pp) + 128)
             for m in range(NSLOT)])
        m = dict(shared)
        m["xT"] = np.ascontiguousarray(x[b, own, :].T)
        m["posr"] = np.arange(2048 * pp, 2048 * (pp + 1),
                              dtype=np.float32).reshape(1, OWN)
        m["posq"] = posq.astype(np.float32).reshape(1, QSH)
        m["maskA"] = tri if pp == 0 else ones
        m["maskB"] = zeros if pp == 0 else tri
        m["ids"] = np.array([[pp, ph, b, c, 2 * pp + ph, 0, 0, 0]], np.int32)
        m["nonce"] = nonce
        in_maps.append(m)
    return in_maps


def kernel(**inputs):
    global _LAST_DEBUG
    in_maps = _prep_host(inputs)
    nc = build()
    res = run_bass_kernel_spmd(nc, in_maps, core_ids=list(range(8)))
    _LAST_DEBUG = res.results

    outf = np.zeros((B, T, D), np.float32)
    for c in range(8):
        pp, ph, b = c % 2, (c // 2) % 2, c // 4
        o = np.asarray(res.results[c]["out"])
        for i, mslot in enumerate(range(8 * ph, 8 * ph + 8)):
            j = 2 * mslot + pp
            outf[b, 128 * j:128 * (j + 1), :] = o[:, 128 * i:128 * (i + 1)].T
    return outf
